# version 02
# baseline (speedup 1.0000x reference)
import numpy as np
import ml_dtypes
from contextlib import ExitStack

import concourse.bass as bass
import concourse.mybir as mybir
from concourse.bass_utils import run_bass_kernel_spmd

F32 = mybir.dt.float32
BF16 = mybir.dt.bfloat16
I32 = mybir.dt.int32
AF = mybir.ActivationFunctionType
ALU = mybir.AluOpType
AX = mybir.AxisListType
BF = ml_dtypes.bfloat16

ENGS = ("sync", "tensor", "vector", "scalar", "gpsimd")
SELF_SER = ("vector", "scalar", "gpsimd")

D = 1024
NCH = 8



class Op:
    __slots__ = ("fn", "waits", "inc")

    def __init__(self, fn, waits, inc):
        self.fn = fn
        self.waits = waits
        self.inc = inc


class Sched:

    N_DMA_POOL = 8

    def __init__(self, nc, es):
        self.nc = nc
        self.ops = {e: [] for e in ENGS}
        self.val = {e: 0 for e in ENGS}
        self.seq = {e: es.enter_context(nc.semaphore("seq_" + e)) for e in ENGS}
        self.dpool = [es.enter_context(nc.semaphore(f"dpool{j}"))
                      for j in range(self.N_DMA_POOL)]
        self.dpool_tot = [0] * self.N_DMA_POOL
        self.dpool_idx = 0
        self.last_w = {}
        self.readers = {}
        self.waited = {e: {} for e in ENGS}

    def _sem_of(self, name):
        if name.startswith("@d"):
            return self.dpool[int(name[2:])]
        return self.seq[name]

    def _add_wait(self, eng, dep, waits):
        deng, dval = dep
        if deng == eng:
            return
        if self.waited[eng].get(deng, 0) >= dval:
            return
        self.waited[eng][deng] = dval
        waits.append((self._sem_of(deng), dval))

    def _book(self, eng, fn, reads, writes, inc, completion=None, pre_waits=()):
        waits = list(pre_waits)
        if eng in SELF_SER and self.val[eng] > 0:
            waits.append((self.seq[eng], self.val[eng]))
        for r in reads:
            lw = self.last_w.get(r)
            if lw is not None:
                self._add_wait(eng, lw, waits)
        for w in writes:
            lw = self.last_w.get(w)
            if lw is not None:
                self._add_wait(eng, lw, waits)
            for reng, rval in self.readers.get(w, {}).items():
                self._add_wait(eng, (reng, rval), waits)
        self.val[eng] += inc
        comp = completion if completion is not None else (eng, self.val[eng])
        for r in reads:
            self.readers.setdefault(r, {})[comp[0]] = comp[1]
        for w in writes:
            self.last_w[w] = comp
            self.readers[w] = {}
        self.ops[eng].append(Op(fn, waits, inc))

    def dma(self, transfers, reads=(), writes=(), queue="sync"):
        j = self.dpool_idx
        self.dpool_idx = (j + 1) % self.N_DMA_POOL
        sem = self.dpool[j]
        pre = []
        if self.dpool_tot[j] > 0:
            self._add_wait(queue, (f"@d{j}", self.dpool_tot[j]), pre)
        self.dpool_tot[j] += 16 * len(transfers)
        tot = self.dpool_tot[j]

        def fn(eng):
            for dst, src in transfers:
                eng.dma_start(out=dst, in_=src).then_inc(sem, 16)

        self._book(queue, fn, reads, writes, 0,
                   completion=(f"@d{j}", tot), pre_waits=pre)

    def comp(self, eng, fn, reads=(), writes=()):
        sem = self.seq[eng]

        if eng == "tensor":
            def wrapped(e):
                fn(e)
                e.drain().then_inc(sem, 1)
        else:
            def wrapped(e):
                last = fn(e)
                last.then_inc(sem, 1)

        self._book(eng, wrapped, reads, writes, 1)

    def barrier_wait(self, eng, keys):
        waits = []
        for r in keys:
            lw = self.last_w.get(r)
            if lw is not None:
                self._add_wait(eng, lw, waits)
        if eng in SELF_SER and self.val[eng] > 0:
            waits.append((self.seq[eng], self.val[eng]))
        self.ops[eng].append(Op(None, waits, 0))

    def finalize(self):
        nc = self.nc
        with nc.Block() as block:
            for e in ENGS:
                ops = self.ops[e]
                if not ops:
                    continue

                def runner(eng, ops=ops):
                    for op in ops:
                        for sem, v in op.waits:
                            eng.wait_ge(sem, v)
                        if op.fn is not None:
                            op.fn(eng)

                getattr(block, e)(runner)


class Buffers:
    def __init__(self, nc, es):
        self.nc = nc
        self.es = es
        self.t = {}

    def sb(self, name, shape, dtype):
        h = self.es.enter_context(self.nc.sbuf_tensor("sb_" + name, list(shape), dtype))
        self.t[name] = h
        return h

    def ps(self, name, shape, dtype=F32):
        h = self.es.enter_context(self.nc.psum_tensor("ps_" + name, list(shape), dtype))
        self.t[name] = h
        return h

    def __getitem__(self, name):
        return self.t[name]

    def __contains__(self, name):
        return name in self.t



def emit_weight_load(S, B, dram_ap, sb_bf, n_chunks, width, tag):
    piece = 512
    idx = 0
    for k in range(n_chunks):
        for p0 in range(0, width, piece):
            pw = min(piece, width - p0)
            stg = "wstage0" if idx % 2 == 0 else "wstage1"
            idx += 1
            S.dma([(B[stg][:, 0:pw], dram_ap[128 * k:128 * (k + 1), p0:p0 + pw])],
                  writes=(stg,))
            S.comp("vector",
                   lambda e, k=k, p0=p0, pw=pw, stg=stg: e.tensor_copy(
                       out=sb_bf[:, width * k + p0:width * k + p0 + pw],
                       in_=B[stg][:, 0:pw]),
                   reads=(stg,), writes=(tag,))


def emit_flat_bias_load(S, B, dram_ap, sb_bf, width, tag):
    piece = 512
    idx = 0
    for p0 in range(0, width, piece):
        pw = min(piece, width - p0)
        stg = "wstage0" if idx % 2 == 0 else "wstage1"
        idx += 1
        S.dma([(B[stg][0:1, 0:pw], dram_ap[0:1, p0:p0 + pw])], writes=(stg,))
        S.comp("vector", lambda e, p0=p0, pw=pw, stg=stg: e.tensor_copy(
            out=sb_bf[0:1, p0:p0 + pw], in_=B[stg][0:1, 0:pw]),
            reads=(stg,), writes=(tag,))


def emit_ln_stats(S, B, chunk_ap_fn, src_keys, width):
    psum = B["PSBIG"]

    def mm_sum(e):
        last = None
        for k in range(NCH):
            last = e.matmul(psum[:, 0:width], B["ones128"][:, 0:128],
                            chunk_ap_fn(k), start=(k == 0), stop=(k == NCH - 1))
        return last

    S.comp("tensor", mm_sum, reads=tuple(src_keys) + ("ones128",), writes=("PSBIG_a",))

    def sq(e):
        last = None
        for k in range(NCH):
            last = e.tensor_mul(out=B["sqbuf"][:, k * width:(k + 1) * width],
                                in0=chunk_ap_fn(k), in1=chunk_ap_fn(k))
        return last

    S.comp("gpsimd", sq, reads=tuple(src_keys), writes=("sqbuf",))

    def mm_sq(e):
        last = None
        for k in range(NCH):
            last = e.matmul(psum[:, 512:512 + width], B["ones128"][:, 0:128],
                            B["sqbuf"][:, k * width:(k + 1) * width],
                            start=(k == 0), stop=(k == NCH - 1))
        return last

    S.comp("tensor", mm_sq, reads=("sqbuf", "ones128"), writes=("PSBIG_b",))
    S.comp("vector", lambda e: e.tensor_scalar(out=B["st_mu"][:, 0:width],
                                               in0=psum[:, 0:width],
                                               scalar1=1.0 / D, scalar2=None, op0=ALU.mult),
           reads=("PSBIG_a",), writes=("st_mu",))
    S.comp("vector", lambda e: e.tensor_scalar(out=B["st_t1"][:, 0:width],
                                               in0=psum[:, 512:512 + width],
                                               scalar1=1.0 / D, scalar2=None, op0=ALU.mult),
           reads=("PSBIG_b",), writes=("st_t1",))
    S.comp("vector", lambda e: e.tensor_tensor(out=B["st_t2"][:, 0:width],
                                               in0=B["st_mu"][:, 0:width],
                                               in1=B["st_mu"][:, 0:width], op=ALU.mult),
           reads=("st_mu",), writes=("st_t2",))
    S.comp("vector", lambda e: e.tensor_tensor(out=B["st_t1"][:, 0:width],
                                               in0=B["st_t1"][:, 0:width],
                                               in1=B["st_t2"][:, 0:width], op=ALU.subtract),
           reads=("st_t1", "st_t2"), writes=("st_t1",))
    S.comp("scalar", lambda e: e.activation(out=B["st_t2"][:, 0:width],
                                            in_=B["st_t1"][:, 0:width],
                                            func=AF.Ln, bias=B["c_eps"][:], scale=1.0),
           reads=("st_t1", "c_eps"), writes=("st_t2",))
    S.comp("scalar", lambda e: e.activation(out=B["st_r"][:, 0:width],
                                            in_=B["st_t2"][:, 0:width],
                                            func=AF.Exp, bias=0.0, scale=B["c_nh"][:]),
           reads=("st_t2", "c_nh"), writes=("st_r",))
    S.comp("vector", lambda e: e.tensor_copy(out=B["st_mu_bf"][:, 0:width],
                                             in_=B["st_mu"][:, 0:width]),
           reads=("st_mu",), writes=("st_mu_bf",))
    S.comp("vector", lambda e: e.tensor_copy(out=B["st_r_bf"][:, 0:width],
                                             in_=B["st_r"][:, 0:width]),
           reads=("st_r",), writes=("st_r_bf",))


def emit_stats_tm(S, B, width, col0):
    ntt = width // 128
    for tt in range(ntt):
        S.comp("tensor", lambda e, tt=tt: e.transpose(
            B["PSTR"][:], B["st_mu"][:, tt * 128:(tt + 1) * 128], B["ident_f32"][:]),
            reads=("st_mu", "ident_f32"), writes=("PSTR",))
        S.comp("vector", lambda e, tt=tt: e.tensor_copy(
            out=B["st_mu_tm"][:, col0 + tt:col0 + tt + 1], in_=B["PSTR"][:, 0:1]),
            reads=("PSTR",), writes=("st_mu_tm",))
        S.comp("tensor", lambda e, tt=tt: e.transpose(
            B["PSTR"][:], B["st_r"][:, tt * 128:(tt + 1) * 128], B["ident_f32"][:]),
            reads=("st_r", "ident_f32", "st_mu_tm"), writes=("PSTR",))
        S.comp("vector", lambda e, tt=tt: e.tensor_copy(
            out=B["st_r_tm"][:, col0 + tt:col0 + tt + 1], in_=B["PSTR"][:, 0:1]),
            reads=("PSTR",), writes=("st_r_tm",))


def emit_ln_apply_bf(S, B, chunk_ap_fn, src_keys, width, dst_ap_fn, dst_key):
    for k in range(NCH):
        S.comp("vector", lambda e, k=k: e.tensor_tensor(
            out=B["ln_t"][:, 0:width], in0=chunk_ap_fn(k),
            in1=B["st_mu_bf"][:, 0:width], op=ALU.subtract),
            reads=tuple(src_keys) + ("st_mu_bf",), writes=("ln_t",))
        S.comp("vector", lambda e, k=k: e.tensor_tensor(
            out=dst_ap_fn(k), in0=B["ln_t"][:, 0:width],
            in1=B["st_r_bf"][:, 0:width], op=ALU.mult),
            reads=("ln_t", "st_r_bf"), writes=(dst_key,))


def emit_mha(S, B, src_bf, src_key, src_col0, T, dst_write_fn,
             w_key="w_in", wo_key="w_out", scale=1.0 / 32.0, cscale_key="c_sc",
             n_heads=8):
    src_stride = src_bf.shape[1] // NCH
    n_nt = (T + 511) // 512
    ntt = T // 128
    psum = B["PSBIG"]
    w_in = B["w_in"]
    w_out = B["w_out"]
    ctx_stride = B["ctxfm"].shape[1] // n_heads

    for h in range(n_heads):
        for which, off, bias_col in (("qh", 128 * h, h), ("kh", D + 128 * h, 8 + h)):
            for nt in range(n_nt):
                tw = min(512, T - nt * 512)
                half = (nt % 2) * 512
                pk = "PSBIG_a" if nt % 2 == 0 else "PSBIG_b"

                def mmq(e, off=off, nt=nt, tw=tw, half=half):
                    last = None
                    for k in range(NCH):
                        last = e.matmul(
                            psum[:, half:half + tw],
                            w_in[:, k * 3 * D + off:k * 3 * D + off + 128],
                            src_bf[:, k * src_stride + src_col0 + nt * 512:
                                   k * src_stride + src_col0 + nt * 512 + tw],
                            start=(k == 0), stop=(k == NCH - 1))
                    return last

                S.comp("tensor", mmq, reads=(w_key, src_key), writes=(pk,))
                S.comp("scalar", lambda e, which=which, nt=nt, tw=tw, half=half,
                       bias_col=bias_col: e.add(
                           out=B[which][:, nt * 512:nt * 512 + tw],
                           in_=psum[:, half:half + tw],
                           add=B["b_in_p"][:, bias_col:bias_col + 1]),
                       reads=(pk, "b_in_p"), writes=(which,))
        for tt in range(ntt):
            half = (tt % 2) * 512
            pk = "PSBIG_a" if tt % 2 == 0 else "PSBIG_b"

            def mmv(e, tt=tt, half=half, h=h):
                for k in range(NCH):
                    e.matmul(
                        psum[:, half:half + 128],
                        src_bf[:, k * src_stride + src_col0 + tt * 128:
                               k * src_stride + src_col0 + tt * 128 + 128],
                        w_in[:, k * 3 * D + 2 * D + 128 * h:
                             k * 3 * D + 2 * D + 128 * h + 128],
                        start=(k == 0), stop=False)
                return e.matmul(psum[:, half:half + 128],
                                B["ones_row"][0:1, 0:128],
                                B["b_in_f_bf"][0:1, 2 * D + 128 * h:2 * D + 128 * h + 128],
                                start=False, stop=True)

            S.comp("tensor", mmv, reads=(w_key, src_key, "ones_row", "b_in_f_bf"),
                   writes=(pk,))
            S.comp("scalar", lambda e, tt=tt, half=half: e.copy(
                out=B["vtm"][:, tt * 128:(tt + 1) * 128], in_=psum[:, half:half + 128]),
                reads=(pk,), writes=("vtm",))
        for which, bcol in (("qh", "b_q"), ("kh", "b_k")):
            S.comp("gpsimd", lambda e, which=which: e.tensor_mul(
                out=B["sqbuf"][:, 0:T], in0=B[which][:, 0:T], in1=B[which][:, 0:T]),
                reads=(which,), writes=("sqbuf",))
            for nt in range(n_nt):
                tw = min(512, T - nt * 512)
                half = (nt % 2) * 512
                pk = "PSBIG_a" if nt % 2 == 0 else "PSBIG_b"
                S.comp("tensor", lambda e, nt=nt, tw=tw, half=half: e.matmul(
                    psum[:, half:half + tw], B["ones128"][:, 0:128],
                    B["sqbuf"][:, nt * 512:nt * 512 + tw], start=True, stop=True),
                    reads=("sqbuf", "ones128"), writes=(pk,))
                S.comp("vector", lambda e, nt=nt, tw=tw, half=half: e.reduce_max(
                    out=B["b_nt"][:, nt:nt + 1], in_=psum[:, half:half + tw], axis=AX.X),
                    reads=(pk,), writes=("b_nt",))
            S.comp("vector", lambda e, bcol=bcol: e.reduce_max(
                out=B[bcol][:], in_=B["b_nt"][:, 0:n_nt], axis=AX.X),
                reads=("b_nt",), writes=(bcol,))
        S.comp("vector", lambda e: e.tensor_tensor(out=B["b_q"][:], in0=B["b_q"][:],
                                                   in1=B["b_k"][:], op=ALU.mult),
               reads=("b_q", "b_k"), writes=("b_q",))
        S.comp("scalar", lambda e: e.activation(out=B["b_k"][:], in_=B["b_q"][:],
                                                func=AF.Ln, bias=0.0, scale=1.0),
               reads=("b_q",), writes=("b_k",))
        S.comp("scalar", lambda e: e.activation(out=B["b_q"][:], in_=B["b_k"][:],
                                                func=AF.Exp, bias=0.0, scale=B["c_ph"][:]),
               reads=("b_k", "c_ph"), writes=("b_q",))
        S.comp("vector", lambda e: e.tensor_scalar(out=B["negB"][:], in0=B["b_q"][:],
                                                   scalar1=-scale, scalar2=None,
                                                   op0=ALU.mult),
               reads=("b_q",), writes=("negB",))
        nkt = T // 128
        for qb in range(n_nt):
            qw = min(512, T - qb * 512)
            for kt in range(nkt):
                half = (kt % 2) * 512
                pk = "PSBIG_a" if kt % 2 == 0 else "PSBIG_b"
                ak = "attnT0" if kt % 2 == 0 else "attnT1"
                at = B[ak]
                S.comp("tensor", lambda e, kt=kt, qb=qb, qw=qw, half=half: e.matmul(
                    psum[:, half:half + qw], B["kh"][:, kt * 128:(kt + 1) * 128],
                    B["qh"][:, qb * 512:qb * 512 + qw], start=True, stop=True),
                    reads=("kh", "qh"), writes=(pk,))
                S.comp("scalar", lambda e, at=at, qw=qw, half=half: e.activation(
                    out=at[:, 0:qw], in_=psum[:, half:half + qw], func=AF.Exp,
                    bias=B["negB"][:], scale=B[cscale_key][:]),
                    reads=(pk, "negB", cscale_key), writes=(ak,))

                def mmctx(e, kt=kt, qw=qw, at=at, first=(kt == 0), last=(kt == nkt - 1)):
                    e.matmul(B["PCTX"][:, 0:qw], B["vtm"][:, kt * 128:(kt + 1) * 128],
                             at[:, 0:qw], start=first, stop=last)
                    return e.matmul(B["PDEN"][:, 0:qw], B["ones128"][:, 0:128],
                                    at[:, 0:qw], start=first, stop=last)

                S.comp("tensor", mmctx, reads=(ak, "vtm", "ones128"),
                       writes=("PCTX", "PDEN"))
            S.comp("scalar", lambda e, qw=qw: e.activation(
                out=B["lnd"][:, 0:qw], in_=B["PDEN"][:, 0:qw], func=AF.Ln,
                bias=0.0, scale=1.0),
                reads=("PDEN",), writes=("lnd",))
            S.comp("scalar", lambda e, qw=qw: e.activation(
                out=B["rden"][:, 0:qw], in_=B["lnd"][:, 0:qw], func=AF.Exp,
                bias=0.0, scale=B["c_neg1"][:]),
                reads=("lnd", "c_neg1"), writes=("rden",))
            S.comp("vector", lambda e, h=h, qb=qb, qw=qw: e.tensor_tensor(
                out=B["ctxfm"][:, h * ctx_stride + qb * 512:h * ctx_stride + qb * 512 + qw],
                in0=B["PCTX"][:, 0:qw], in1=B["rden"][:, 0:qw], op=ALU.mult),
                reads=("PCTX", "rden"), writes=("ctxfm",))
    for nt in range(n_nt):
        tw = min(512, T - nt * 512)
        for m in range(NCH):
            half = ((nt * NCH + m) % 2) * 512
            pk = "PSBIG_a" if (nt * NCH + m) % 2 == 0 else "PSBIG_b"

            def mmo(e, nt=nt, tw=tw, m=m, half=half):
                last = None
                for k in range(NCH):
                    last = e.matmul(
                        psum[:, half:half + tw],
                        w_out[:, k * D + 128 * m:k * D + 128 * m + 128],
                        B["ctxfm"][:, k * ctx_stride + nt * 512:
                                   k * ctx_stride + nt * 512 + tw],
                        start=(k == 0), stop=(k == NCH - 1))
                return last

            S.comp("tensor", mmo, reads=(wo_key, "ctxfm"), writes=(pk,))
            dst_write_fn(S, B, nt, tw, m, psum[:, half:half + tw], pk)


def emit_init_consts(S, B, extra=()):
    S.comp("vector", lambda e: e.memset(B["ones128"][:], 1.0), writes=("ones128",))
    S.comp("vector", lambda e: e.memset(B["ones_row"][:], 1.0), writes=("ones_row",))
    S.comp("vector", lambda e: e.memset(B["c_eps"][:], 1e-5), writes=("c_eps",))
    S.comp("vector", lambda e: e.memset(B["c_nh"][:], -0.5), writes=("c_nh",))
    S.comp("vector", lambda e: e.memset(B["c_ph"][:], 0.5), writes=("c_ph",))
    S.comp("vector", lambda e: e.memset(B["c_neg1"][:], -1.0), writes=("c_neg1",))
    for name, v in extra:
        S.comp("vector", lambda e, name=name, v=v: e.memset(B[name][:], v),
               writes=(name,))


def alloc_common(B):
    B.sb("ones128", (128, 128), BF16)
    B.sb("ones_row", (1, 512), BF16)
    B.sb("c_eps", (128, 1), F32)
    B.sb("c_nh", (128, 1), F32)
    B.sb("c_ph", (128, 1), F32)
    B.sb("c_neg1", (128, 1), F32)
    B.sb("wstage0", (128, 512), F32)
    B.sb("wstage1", (128, 512), F32)
    B.ps("PSBIG", (128, 1024), F32)


def alloc_ln(B, width):
    if "sqbuf" not in B:
        B.sb("sqbuf", (128, NCH * width), BF16)
    B.sb("st_mu", (128, width), F32)
    B.sb("st_t1", (128, width), F32)
    B.sb("st_t2", (128, width), F32)
    B.sb("st_r", (128, width), F32)
    B.sb("st_mu_bf", (128, width), BF16)
    B.sb("st_r_bf", (128, width), BF16)
    B.sb("ln_t", (128, width), BF16)


def alloc_attn(B, T, lnw=0):
    B.sb("qh", (128, T), BF16)
    B.sb("kh", (128, T), BF16)
    B.sb("vtm", (128, T), BF16)
    if "sqbuf" not in B:
        B.sb("sqbuf", (128, max(T, NCH * lnw)), BF16)
    B.sb("b_nt", (128, 4), F32)
    B.sb("b_q", (128, 1), F32)
    B.sb("b_k", (128, 1), F32)
    B.sb("negB", (128, 1), F32)
    B.sb("attnT0", (128, 512), BF16)
    B.sb("attnT1", (128, 512), BF16)
    B.sb("lnd", (128, 512), F32)
    B.sb("rden", (128, 512), F32)
    B.sb("ctxfm", (128, NCH * T), BF16)
    B.ps("PCTX", (128, 512), F32)
    B.ps("PDEN", (128, 512), F32)
    B.sb("c_sc", (128, 1), F32)



A_T = [512, 1024, 2048]
A_NCHUNK = [4, 2, 1]
A_INW = [1856, 1984, 2048]
A_COFF = [448, 960, 1984]


def build_A(level):
    nc = bass.Bass("TRN2", target_bir_lowering=False)
    es = ExitStack()
    T = A_T[level]
    n_chunks = A_NCHUNK[level]
    in_w = A_INW[level]
    scale = 1.0 / np.sqrt(128.0)

    x_in = nc.dram_tensor("x_in", [D, in_w], F32 if level == 0 else BF16,
                          kind="ExternalInput").ap()
    w_in_t = nc.dram_tensor("w_in_t", [D, 3 * D], F32, kind="ExternalInput").ap()
    b_in_p = nc.dram_tensor("b_in_p", [128, 24], F32, kind="ExternalInput").ap()
    b_in_f = nc.dram_tensor("b_in_f", [1, 3 * D], F32, kind="ExternalInput").ap()
    w_out_t = nc.dram_tensor("w_out_t", [D, D], F32, kind="ExternalInput").ap()
    b_out_p = nc.dram_tensor("b_out_p", [128, 8], F32, kind="ExternalInput").ap()
    if level == 2:
        x_own = nc.dram_tensor("x_own", [D, 2048], F32, kind="ExternalInput").ap()
        out_t = nc.dram_tensor("out", [D, 2048], F32, kind="ExternalOutput").ap()
    else:
        out_t = nc.dram_tensor("out", [D, 2048], BF16, kind="ExternalOutput").ap()

    with es:
        B = Buffers(nc, es)
        S = Sched(nc, es)
        alloc_common(B)
        lnw = 512 if level == 0 else 128
        alloc_attn(B, T, lnw if level in (0, 2) else 0)
        B.sb("w_in", (128, NCH * 3 * D), BF16)
        B.sb("w_out", (128, NCH * D), BF16)
        B.sb("b_in_p", (128, 24), F32)
        B.sb("b_in_f_bf", (1, 3 * D), BF16)
        B.sb("b_out_p", (128, 8), F32)
        B.sb("src", (128, NCH * in_w), BF16)
        B.sb("dstbuf", (128, NCH * 2048), BF16)
        if level in (0, 2):
            alloc_ln(B, lnw)
            B.sb("xg", (128, NCH * lnw), F32)
        if level == 2:
            B.sb("res_t", (128, lnw), F32)

        emit_init_consts(S, B, [("c_sc", scale)])
        emit_weight_load(S, B, w_in_t, B["w_in"], NCH, 3 * D, "w_in")
        emit_weight_load(S, B, w_out_t, B["w_out"], NCH, D, "w_out")
        S.dma([(B["b_in_p"][:], b_in_p), (B["b_out_p"][:], b_out_p)],
              writes=("b_in_p", "b_out_p"))
        emit_flat_bias_load(S, B, b_in_f, B["b_in_f_bf"], 3 * D, "b_in_f_bf")

        if level == 0:
            B.sb("xbf", (128, NCH * 512), BF16)
            c0 = 0
            while c0 < in_w:
                gw = min(512, in_w - c0)
                S.dma([(B["xg"][:, k * gw:(k + 1) * gw],
                        x_in[128 * k:128 * (k + 1), c0:c0 + gw]) for k in range(NCH)],
                      writes=("xg",))
                S.comp("vector", lambda e, gw=gw: e.tensor_copy(
                    out=B["xbf"][:, 0:NCH * gw], in_=B["xg"][:, 0:NCH * gw]),
                    reads=("xg",), writes=("xbf",))
                emit_ln_stats(S, B, lambda k, gw=gw: B["xbf"][:, k * gw:(k + 1) * gw],
                              ("xbf",), gw)
                emit_ln_apply_bf(S, B, lambda k, gw=gw: B["xbf"][:, k * gw:(k + 1) * gw],
                                 ("xbf",), gw,
                                 lambda k, c0=c0, gw=gw: B["src"][:, k * in_w + c0:
                                                                  k * in_w + c0 + gw],
                                 "src")
                c0 += gw
        else:
            S.dma([(B["src"][:, k * in_w:(k + 1) * in_w],
                    x_in[128 * k:128 * (k + 1), :]) for k in range(NCH)],
                  writes=("src",))

        for j in range(n_chunks):
            def dst_write(S_, B_, nt, tw, m, psum_ap, pk, j=j):
                S_.comp("scalar", lambda e, nt=nt, tw=tw, m=m, psum_ap=psum_ap, j=j:
                        e.add(out=B_["dstbuf"][:, m * 2048 + T * j + nt * 512:
                                               m * 2048 + T * j + nt * 512 + tw],
                              in_=psum_ap, add=B_["b_out_p"][:, m:m + 1]),
                        reads=(pk, "b_out_p"), writes=("dstbuf",))

            emit_mha(S, B, B["src"], "src", A_COFF[level] * j, T, dst_write,
                     scale=scale)

        if level < 2:
            S.dma([(out_t[128 * k:128 * (k + 1), :],
                    B["dstbuf"][:, k * 2048:(k + 1) * 2048]) for k in range(NCH)],
                  reads=("dstbuf",), writes=("out",))
        else:
            for g0 in range(0, 2048, lnw):
                S.dma([(B["xg"][:, k * lnw:(k + 1) * lnw],
                        x_own[128 * k:128 * (k + 1), g0:g0 + lnw]) for k in range(NCH)],
                      writes=("xg",))
                emit_ln_stats(S, B, lambda k, g0=g0: B["dstbuf"][:, k * 2048 + g0:
                                                                 k * 2048 + g0 + lnw],
                              ("dstbuf",), lnw)
                for k in range(NCH):
                    S.comp("vector", lambda e, k=k, g0=g0: e.tensor_tensor(
                        out=B["ln_t"][:, 0:lnw],
                        in0=B["dstbuf"][:, k * 2048 + g0:k * 2048 + g0 + lnw],
                        in1=B["st_mu_bf"][:, 0:lnw], op=ALU.subtract),
                        reads=("dstbuf", "st_mu_bf"), writes=("ln_t",))
                    S.comp("vector", lambda e, k=k: e.tensor_tensor(
                        out=B["res_t"][:, 0:lnw], in0=B["ln_t"][:, 0:lnw],
                        in1=B["st_r_bf"][:, 0:lnw], op=ALU.mult),
                        reads=("ln_t", "st_r_bf"), writes=("res_t",))
                    S.comp("vector", lambda e, k=k: e.tensor_tensor(
                        out=B["res_t"][:, 0:lnw], in0=B["res_t"][:, 0:lnw],
                        in1=B["xg"][:, k * lnw:(k + 1) * lnw], op=ALU.add),
                        reads=("res_t", "xg"), writes=("res_t",))
                    S.dma([(out_t[128 * k:128 * (k + 1), g0:g0 + lnw],
                            B["res_t"][:, 0:lnw])],
                          reads=("res_t",), writes=("out",))

        S.barrier_wait("sync", ("out",))
        S.finalize()
    return nc



def _fmT(a):
    return np.ascontiguousarray(np.asarray(a, dtype=np.float32).T)


def _fmT_bf(a):
    return np.ascontiguousarray(np.asarray(a).T)


def _bias_p(v):
    v = np.asarray(v, dtype=np.float32)
    return np.ascontiguousarray(v.reshape(-1, 128).T)


def _bias_f(v):
    return np.ascontiguousarray(np.asarray(v, dtype=np.float32).reshape(1, -1))


def _bias_p24(bin_):
    return _bias_p(bin_)


def run_A(level, x_fm_f32, x_fm_prev_bf, Win, bin_, Wout, bout):
    nc = build_A(level)
    w_in_t = _fmT(Win)
    w_out_t = _fmT(Wout)
    b_in_p = _bias_p24(bin_)
    b_in_f = _bias_f(bin_)
    b_out_p = _bias_p(bout)
    in_maps = []
    in_w = A_INW[level]
    for c in range(8):
        b, q = c // 4, c % 4
        m = {"w_in_t": w_in_t, "b_in_p": b_in_p, "b_in_f": b_in_f,
             "w_out_t": w_out_t, "b_out_p": b_out_p}
        if level == 0:
            s0 = 1792 * q
            m["x_in"] = np.ascontiguousarray(x_fm_f32[b][:, s0:s0 + in_w])
        elif level == 1:
            s0 = 1920 * q
            m["x_in"] = np.ascontiguousarray(x_fm_prev_bf[b][:, s0:s0 + in_w])
        else:
            s0 = 1984 * q
            m["x_in"] = np.ascontiguousarray(x_fm_prev_bf[b][:, s0:s0 + in_w])
            m["x_own"] = np.ascontiguousarray(x_fm_f32[b][:, 2048 * q:2048 * (q + 1)])
        in_maps.append(m)
    res = run_bass_kernel_spmd(nc, in_maps, core_ids=list(range(8)))
    odt = np.float32 if level == 2 else BF
    out = np.empty((2, D, 8192), dtype=odt)
    for c in range(8):
        b, q = c // 4, c % 4
        out[b][:, 2048 * q:2048 * (q + 1)] = np.asarray(res.results[c]["out"])
    return out





def emit_tail_group(S, B, chunk_ap_fn, src_keys, lnw, res_dram_ap, out_dram_ap, g0):
    S.dma([(B["xg"][:, k * lnw:(k + 1) * lnw],
            res_dram_ap[128 * k:128 * (k + 1), g0:g0 + lnw]) for k in range(NCH)],
          writes=("xg",))
    emit_ln_stats(S, B, chunk_ap_fn, src_keys, lnw)
    for k in range(NCH):
        S.comp("vector", lambda e, k=k: e.tensor_tensor(
            out=B["ln_t"][:, 0:lnw], in0=chunk_ap_fn(k),
            in1=B["st_mu_bf"][:, 0:lnw], op=ALU.subtract),
            reads=tuple(src_keys) + ("st_mu_bf",), writes=("ln_t",))
        S.comp("vector", lambda e, k=k: e.tensor_tensor(
            out=B["res_t"][:, 0:lnw], in0=B["ln_t"][:, 0:lnw],
            in1=B["st_r_bf"][:, 0:lnw], op=ALU.mult),
            reads=("ln_t", "st_r_bf"), writes=("res_t",))
        S.comp("vector", lambda e, k=k: e.tensor_tensor(
            out=B["res_t"][:, 0:lnw], in0=B["res_t"][:, 0:lnw],
            in1=B["xg"][:, k * lnw:(k + 1) * lnw], op=ALU.add),
            reads=("res_t", "xg"), writes=("res_t",))
        S.dma([(out_dram_ap[128 * k:128 * (k + 1), g0:g0 + lnw],
                B["res_t"][:, 0:lnw])],
              reads=("res_t",), writes=("out",))



B_TH = 1024
B_HH = [(0, 6), (6, 5)]


def build_B():
    nc = bass.Bass("TRN2", target_bir_lowering=False)
    es = ExitStack()
    H = 1408
    x1 = nc.dram_tensor("x1", [D, 2048], F32, kind="ExternalInput").ap()
    gw_t = nc.dram_tensor("gw_t", [D, 8], F32, kind="ExternalInput").ap()
    gb_b = nc.dram_tensor("gb_b", [128, 8], F32, kind="ExternalInput").ap()
    w1a = nc.dram_tensor("w1a", [9 * D, H], F32, kind="ExternalInput").ap()
    w3a = nc.dram_tensor("w3a", [9 * D, H], F32, kind="ExternalInput").ap()
    w2a = nc.dram_tensor("w2a", [9 * H, D], F32, kind="ExternalInput").ap()
    b1a = nc.dram_tensor("b1a", [9 * 128, 11], F32, kind="ExternalInput").ap()
    b3a = nc.dram_tensor("b3a", [9 * 128, 11], F32, kind="ExternalInput").ap()
    b2d = nc.dram_tensor("b2d", [8, D], F32, kind="ExternalInput").ap()
    sb2f = nc.dram_tensor("sb2f", [1, D], F32, kind="ExternalInput").ap()
    ident = nc.dram_tensor("identf", [128, 128], F32, kind="ExternalInput").ap()
    out_t = nc.dram_tensor("out", [D, 2048], F32, kind="ExternalOutput").ap()

    with es:
        B = Buffers(nc, es)
        S = Sched(nc, es)
        alloc_common(B)
        alloc_ln(B, 512)
        B.sb("y", (128, NCH * B_TH), BF16)
        B.sb("acc", (128, NCH * B_TH), F32)
        B.sb("accbf", (128, NCH * 512), BF16)
        B.sb("xg", (128, NCH * 512), F32)
        B.sb("res_t", (128, 512), F32)
        B.sb("ident_f32", (128, 128), F32)
        B.ps("PSTR", (128, 128), F32)
        for p in range(2):
            B.sb(f"w1h{p}", (128, NCH * 768), BF16)
            B.sb(f"w3h{p}", (128, NCH * 768), BF16)
            B.sb(f"w2h{p}", (128, 6 * D), BF16)
            B.sb(f"b1h{p}", (128, 11), F32)
            B.sb(f"b3h{p}", (128, 11), F32)
        B.sb("hbuf", (128, 6 * B_TH), BF16)
        B.sb("g1sb", (128, 512), BF16)
        B.sb("g3sb", (128, 512), BF16)
        B.sb("gw_sb", (128, NCH * 8), F32)
        B.sb("gwcs", (128, 8), F32)
        B.sb("gb_sb", (128, 8), F32)
        B.sb("b2bf", (8, D), BF16)
        B.sb("b2st", (8, D), F32)
        B.sb("sb2bf", (1, D), BF16)
        B.sb("st_mu_tm", (128, 8), F32)
        B.sb("st_negmu_tm", (128, 8), F32)
        B.sb("st_r_tm", (128, 8), F32)
        B.sb("sc", (128, 8), F32)
        B.sb("sc2", (128, 8), F32)
        B.sb("m1", (128, 1), F32)
        B.sb("rw", (128, 8), F32)
        B.sb("rwT", (8, B_TH), BF16)
        B.sb("rwrow", (1, B_TH), BF16)
        B.sb("rw_b", (128, B_TH), BF16)

        emit_init_consts(S, B)
        S.dma([(B["gb_sb"][:], gb_b), (B["ident_f32"][:], ident),
               (B["b2st"][:], b2d)],
              writes=("gb_sb", "ident_f32", "b2st"))
        S.comp("vector", lambda e: e.tensor_copy(out=B["b2bf"][:], in_=B["b2st"][:]),
               reads=("b2st",), writes=("b2bf",))
        emit_flat_bias_load(S, B, sb2f, B["sb2bf"], D, "sb2bf")
        for k in range(NCH):
            S.dma([(B["gw_sb"][:, k * 8:(k + 1) * 8], gw_t[128 * k:128 * (k + 1), :])],
                  writes=("gw_sb",))
        B.sb("ones_f32", (128, 128), F32)
        S.comp("vector", lambda e: e.memset(B["ones_f32"][:], 1.0), writes=("ones_f32",))

        def mm_gwcs(e):
            last = None
            for k in range(NCH):
                last = e.matmul(B["PSBIG"][:, 0:8], B["ones_f32"][:],
                                B["gw_sb"][:, k * 8:(k + 1) * 8],
                                start=(k == 0), stop=(k == NCH - 1))
            return last

        S.comp("tensor", mm_gwcs, reads=("gw_sb", "ones_f32"), writes=("PSBIG_a",))
        S.comp("vector", lambda e: e.tensor_copy(out=B["gwcs"][:], in_=B["PSBIG"][:, 0:8]),
               reads=("PSBIG_a",), writes=("gwcs",))

        psum = B["PSBIG"]
        for th in range(2):
            t0 = th * B_TH
            for g in range(2):
                g0 = t0 + g * 512
                S.dma([(B["xg"][:, k * 512:(k + 1) * 512],
                        x1[128 * k:128 * (k + 1), g0:g0 + 512]) for k in range(NCH)],
                      writes=("xg",))
                S.comp("vector", lambda e: e.tensor_copy(
                    out=B["accbf"][:, 0:NCH * 512], in_=B["xg"][:, 0:NCH * 512]),
                    reads=("xg",), writes=("accbf",))
                emit_ln_stats(S, B, lambda k: B["accbf"][:, k * 512:(k + 1) * 512],
                              ("accbf",), 512)
                emit_stats_tm(S, B, 512, g * 4)
                emit_ln_apply_bf(S, B, lambda k: B["accbf"][:, k * 512:(k + 1) * 512],
                                 ("accbf",), 512,
                                 lambda k, g=g: B["y"][:, k * B_TH + g * 512:
                                                       k * B_TH + g * 512 + 512],
                                 "y")
                for tt in range(4):
                    gt = g * 4 + tt

                    def mm_gate(e, g=g, tt=tt):
                        last = None
                        for k in range(NCH):
                            last = e.matmul(
                                psum[:, 0:8],
                                B["xg"][:, k * 512 + tt * 128:k * 512 + tt * 128 + 128],
                                B["gw_sb"][:, k * 8:(k + 1) * 8],
                                start=(k == 0), stop=(k == NCH - 1))
                        return last

                    S.comp("tensor", mm_gate, reads=("xg", "gw_sb"), writes=("PSBIG_a",))
                    S.comp("vector", lambda e, gt=gt: e.tensor_scalar(
                        out=B["st_negmu_tm"][:, gt:gt + 1], in0=B["st_mu_tm"][:, gt:gt + 1],
                        scalar1=-1.0, scalar2=None, op0=ALU.mult),
                        reads=("st_mu_tm",), writes=("st_negmu_tm",))
                    S.comp("vector", lambda e, gt=gt: e.scalar_tensor_tensor(
                        out=B["sc"][:], in0=B["gwcs"][:],
                        scalar=B["st_negmu_tm"][:, gt:gt + 1], in1=psum[:, 0:8],
                        op0=ALU.mult, op1=ALU.add),
                        reads=("gwcs", "st_negmu_tm", "PSBIG_a"), writes=("sc",))
                    S.comp("vector", lambda e, gt=gt: e.tensor_scalar(
                        out=B["sc"][:], in0=B["sc"][:],
                        scalar1=B["st_r_tm"][:, gt:gt + 1], scalar2=None, op0=ALU.mult),
                        reads=("sc", "st_r_tm"), writes=("sc",))
                    S.comp("vector", lambda e: e.reduce_max(
                        out=B["m1"][:], in_=B["sc"][:], axis=AX.X, negate=True),
                        reads=("sc",), writes=("m1",))
                    S.comp("scalar", lambda e: e.activation(
                        out=B["sc2"][:], in_=B["sc"][:], func=AF.Exp,
                        bias=B["m1"][:], scale=1.0),
                        reads=("sc", "m1"), writes=("sc2",))
                    S.comp("vector", lambda e: e.reduce_sum(
                        out=B["m1"][:], in_=B["sc2"][:], axis=AX.X),
                        reads=("sc2",), writes=("m1",))
                    S.comp("vector", lambda e: e.reciprocal(
                        out=B["m1"][:], in_=B["m1"][:]),
                        reads=("m1",), writes=("m1",))
                    S.comp("vector", lambda e: e.tensor_scalar(
                        out=B["sc"][:], in0=B["sc2"][:], scalar1=B["m1"][:],
                        scalar2=None, op0=ALU.mult),
                        reads=("sc2", "m1"), writes=("sc",))
                    S.comp("vector", lambda e: e.tensor_tensor(
                        out=B["sc2"][:], in0=B["sc"][:], in1=B["gb_sb"][:], op=ALU.add),
                        reads=("sc", "gb_sb"), writes=("sc2",))
                    S.comp("vector", lambda e: e.reduce_max(
                        out=B["m1"][:], in_=B["sc2"][:], axis=AX.X),
                        reads=("sc2",), writes=("m1",))
                    S.comp("vector", lambda e: e.tensor_scalar(
                        out=B["rw"][:], in0=B["sc2"][:], scalar1=B["m1"][:],
                        scalar2=None, op0=ALU.is_ge),
                        reads=("sc2", "m1"), writes=("rw",))
                    S.comp("vector", lambda e: e.scalar_tensor_tensor(
                        out=B["rw"][:], in0=B["rw"][:], scalar=-1e30, in1=B["sc2"][:],
                        op0=ALU.mult, op1=ALU.add),
                        reads=("rw", "sc2"), writes=("rw",))
                    S.comp("vector", lambda e: e.reduce_max(
                        out=B["m1"][:], in_=B["rw"][:], axis=AX.X),
                        reads=("rw",), writes=("m1",))
                    S.comp("vector", lambda e: e.tensor_scalar(
                        out=B["rw"][:], in0=B["sc2"][:], scalar1=B["m1"][:],
                        scalar2=None, op0=ALU.is_ge),
                        reads=("sc2", "m1"), writes=("rw",))
                    S.comp("vector", lambda e: e.tensor_tensor(
                        out=B["rw"][:], in0=B["rw"][:], in1=B["sc"][:], op=ALU.mult),
                        reads=("rw", "sc"), writes=("rw",))
                    S.comp("tensor", lambda e: e.transpose(
                        B["PSTR"][0:8, 0:128], B["rw"][:], B["ident_f32"][:]),
                        reads=("rw", "ident_f32"), writes=("PSTR",))
                    S.comp("vector", lambda e, g=g, tt=tt: e.tensor_copy(
                        out=B["rwT"][0:8, (g * 4 + tt) * 128:(g * 4 + tt) * 128 + 128],
                        in_=B["PSTR"][0:8, 0:128]),
                        reads=("PSTR",), writes=("rwT",))
            for e_i in range(9):
                if e_i < 8:
                    S.dma([(B["rwrow"][0:1, :], B["rwT"][e_i:e_i + 1, :])],
                          reads=("rwT",), writes=("rwrow",))
                    for nt in range(2):
                        S.comp("tensor", lambda e, nt=nt: e.matmul(
                            psum[:, 0:512], B["ones_row"][0:1, 0:128],
                            B["rwrow"][0:1, nt * 512:nt * 512 + 512],
                            start=True, stop=True),
                            reads=("rwrow", "ones_row"), writes=("PSBIG_a",))
                        S.comp("vector", lambda e, nt=nt: e.tensor_copy(
                            out=B["rw_b"][:, nt * 512:nt * 512 + 512],
                            in_=psum[:, 0:512]),
                            reads=("PSBIG_a",), writes=("rw_b",))
                for hh_i, (hc0, hnc) in enumerate(B_HH):
                    p = (e_i * 2 + hh_i) % 2
                    hw = hnc * 128
                    w1k, w3k, w2k = f"w1h{p}", f"w3h{p}", f"w2h{p}"
                    b1k, b3k = f"b1h{p}", f"b3h{p}"
                    for k in range(NCH):
                        for p0 in range(0, hw, 512):
                            pw = min(512, hw - p0)
                            stg = "wstage0" if (k + p0) % 2 == 0 else "wstage1"
                            S.dma([(B[stg][:, 0:pw],
                                    w1a[e_i * D + 128 * k:e_i * D + 128 * (k + 1),
                                        hc0 * 128 + p0:hc0 * 128 + p0 + pw])],
                                  writes=(stg,))
                            S.comp("vector", lambda e, k=k, p0=p0, pw=pw, stg=stg, w1k=w1k: e.tensor_copy(
                                out=B[w1k][:, k * 768 + p0:k * 768 + p0 + pw],
                                in_=B[stg][:, 0:pw]),
                                reads=(stg,), writes=(w1k,))
                            stg2 = "wstage0" if (k + p0) % 2 == 1 else "wstage1"
                            S.dma([(B[stg2][:, 0:pw],
                                    w3a[e_i * D + 128 * k:e_i * D + 128 * (k + 1),
                                        hc0 * 128 + p0:hc0 * 128 + p0 + pw])],
                                  writes=(stg2,))
                            S.comp("vector", lambda e, k=k, p0=p0, pw=pw, stg2=stg2, w3k=w3k: e.tensor_copy(
                                out=B[w3k][:, k * 768 + p0:k * 768 + p0 + pw],
                                in_=B[stg2][:, 0:pw]),
                                reads=(stg2,), writes=(w3k,))
                    for j in range(hnc):
                        for p0 in range(0, D, 512):
                            stg = "wstage0" if (j + p0 // 512) % 2 == 0 else "wstage1"
                            S.dma([(B[stg][:, 0:512],
                                    w2a[e_i * H + (hc0 + j) * 128:e_i * H + (hc0 + j) * 128 + 128,
                                        p0:p0 + 512])],
                                  writes=(stg,))
                            S.comp("vector", lambda e, j=j, p0=p0, stg=stg, w2k=w2k: e.tensor_copy(
                                out=B[w2k][:, j * D + p0:j * D + p0 + 512],
                                in_=B[stg][:, 0:512]),
                                reads=(stg,), writes=(w2k,))
                    S.dma([(B[b1k][:, 0:11], b1a[e_i * 128:(e_i + 1) * 128, :]),
                           (B[b3k][:, 0:11], b3a[e_i * 128:(e_i + 1) * 128, :])],
                          writes=(b1k, b3k))
                    for mc in range(hnc):
                        for nt in range(2):
                            pka, halfa = ("PSBIG_a", 0)
                            pkb, halfb = ("PSBIG_b", 512)

                            def mm1(e, mc=mc, nt=nt, w1k=w1k):
                                last = None
                                for k in range(NCH):
                                    last = e.matmul(
                                        psum[:, halfa:halfa + 512],
                                        B[w1k][:, k * 768 + 128 * mc:k * 768 + 128 * mc + 128],
                                        B["y"][:, k * B_TH + nt * 512:k * B_TH + nt * 512 + 512],
                                        start=(k == 0), stop=(k == NCH - 1))
                                return last

                            S.comp("tensor", mm1, reads=(w1k, "y"), writes=(pka,))
                            S.comp("scalar", lambda e, mc=mc, b1k=b1k, hc0=hc0: e.add(
                                out=B["g1sb"][:], in_=psum[:, halfa:halfa + 512],
                                add=B[b1k][:, hc0 + mc:hc0 + mc + 1]),
                                reads=(pka, b1k), writes=("g1sb",))

                            def mm3(e, mc=mc, nt=nt, w3k=w3k):
                                last = None
                                for k in range(NCH):
                                    last = e.matmul(
                                        psum[:, halfb:halfb + 512],
                                        B[w3k][:, k * 768 + 128 * mc:k * 768 + 128 * mc + 128],
                                        B["y"][:, k * B_TH + nt * 512:k * B_TH + nt * 512 + 512],
                                        start=(k == 0), stop=(k == NCH - 1))
                                return last

                            S.comp("tensor", mm3, reads=(w3k, "y"), writes=(pkb,))
                            S.comp("scalar", lambda e, mc=mc, b3k=b3k, hc0=hc0: e.add(
                                out=B["g3sb"][:], in_=psum[:, halfb:halfb + 512],
                                add=B[b3k][:, hc0 + mc:hc0 + mc + 1]),
                                reads=(pkb, b3k), writes=("g3sb",))
                            S.comp("vector", lambda e: e.tensor_tensor(
                                out=B["g1sb"][:], in0=B["g1sb"][:], in1=B["g3sb"][:],
                                op=ALU.mult),
                                reads=("g1sb", "g3sb"), writes=("g1sb",))
                            if e_i < 8:
                                S.comp("scalar", lambda e: e.activation(
                                    out=B["g3sb"][:], in_=B["g1sb"][:], func=AF.Silu,
                                    bias=0.0, scale=1.0),
                                    reads=("g1sb",), writes=("g3sb",))
                                S.comp("vector", lambda e, mc=mc, nt=nt: e.tensor_tensor(
                                    out=B["hbuf"][:, mc * B_TH + nt * 512:mc * B_TH + nt * 512 + 512],
                                    in0=B["g3sb"][:], in1=B["rw_b"][:, nt * 512:nt * 512 + 512],
                                    op=ALU.mult),
                                    reads=("g3sb", "rw_b"), writes=("hbuf",))
                            else:
                                S.comp("scalar", lambda e, mc=mc, nt=nt: e.activation(
                                    out=B["hbuf"][:, mc * B_TH + nt * 512:mc * B_TH + nt * 512 + 512],
                                    in_=B["g1sb"][:], func=AF.Silu, bias=0.0, scale=1.0),
                                    reads=("g1sb",), writes=("hbuf",))
                    for nt in range(2):
                        for m in range(NCH):
                            half = ((nt * NCH + m) % 2) * 512
                            pk = "PSBIG_a" if (nt * NCH + m) % 2 == 0 else "PSBIG_b"

                            def mm2b(e, nt=nt, m=m, half=half, hnc=hnc, w2k=w2k,
                                     add_bias=(e_i == 8 and hh_i == 0)):
                                for j in range(hnc):
                                    is_last = (j == hnc - 1) and not add_bias
                                    e.matmul(
                                        psum[:, half:half + 512],
                                        B[w2k][:, j * D + 128 * m:j * D + 128 * m + 128],
                                        B["hbuf"][:, j * B_TH + nt * 512:j * B_TH + nt * 512 + 512],
                                        start=(j == 0), stop=is_last)
                                last = None
                                if add_bias:
                                    e.matmul(psum[:, half:half + 512],
                                             B["b2bf"][0:8, 128 * m:128 * m + 128],
                                             B["rwT"][0:8, nt * 512:nt * 512 + 512],
                                             start=False, stop=False)
                                    last = e.matmul(
                                        psum[:, half:half + 512],
                                        B["sb2bf"][0:1, 128 * m:128 * m + 128],
                                        B["ones_row"][0:1, 0:512],
                                        start=False, stop=True)
                                return last

                            S.comp("tensor", mm2b,
                                   reads=(w2k, "hbuf", "b2bf", "rwT", "sb2bf", "ones_row"),
                                   writes=(pk,))
                            if e_i == 0 and hh_i == 0:
                                S.comp("vector", lambda e, nt=nt, m=m, half=half: e.tensor_copy(
                                    out=B["acc"][:, m * B_TH + nt * 512:m * B_TH + nt * 512 + 512],
                                    in_=psum[:, half:half + 512]),
                                    reads=(pk,), writes=("acc",))
                            else:
                                S.comp("vector", lambda e, nt=nt, m=m, half=half: e.tensor_tensor(
                                    out=B["acc"][:, m * B_TH + nt * 512:m * B_TH + nt * 512 + 512],
                                    in0=B["acc"][:, m * B_TH + nt * 512:m * B_TH + nt * 512 + 512],
                                    in1=psum[:, half:half + 512], op=ALU.add),
                                    reads=("acc", pk), writes=("acc",))
            for g in range(2):
                g0 = t0 + g * 512
                for k in range(NCH):
                    S.comp("vector", lambda e, k=k, g=g: e.tensor_copy(
                        out=B["accbf"][:, k * 512:(k + 1) * 512],
                        in_=B["acc"][:, k * B_TH + g * 512:k * B_TH + g * 512 + 512]),
                        reads=("acc",), writes=("accbf",))
                emit_tail_group(S, B, lambda k: B["accbf"][:, k * 512:(k + 1) * 512],
                                ("accbf",), 512, x1, out_t, g0)

        S.barrier_wait("sync", ("out",))
        S.finalize()
    return nc


def run_B(x1_fm, gate_w, gate_b, w1, b1, w3, b3, w2, b2, sw1, sb1, sw3, sb3,
          sw2, sb2):
    nc = build_B()
    H = 1408
    w1a = np.ascontiguousarray(np.concatenate(
        [np.asarray(w1, np.float32).reshape(8 * 1024, H),
         np.asarray(sw1, np.float32)], axis=0))
    w3a = np.ascontiguousarray(np.concatenate(
        [np.asarray(w3, np.float32).reshape(8 * 1024, H),
         np.asarray(sw3, np.float32)], axis=0))
    w2a = np.ascontiguousarray(np.concatenate(
        [np.asarray(w2, np.float32).reshape(8 * H, 1024),
         np.asarray(sw2, np.float32)], axis=0))
    b1a = np.ascontiguousarray(np.concatenate(
        [np.stack([_bias_p(np.asarray(b1, np.float32)[e]) for e in range(8)]),
         _bias_p(np.asarray(sb1, np.float32))[None]], axis=0).reshape(9 * 128, 11))
    b3a = np.ascontiguousarray(np.concatenate(
        [np.stack([_bias_p(np.asarray(b3, np.float32)[e]) for e in range(8)]),
         _bias_p(np.asarray(sb3, np.float32))[None]], axis=0).reshape(9 * 128, 11))
    m_common = {
        "gw_t": _fmT(gate_w), "gb_b": np.ascontiguousarray(
            np.broadcast_to(np.asarray(gate_b, np.float32)[None, :], (128, 8))),
        "w1a": w1a, "w3a": w3a, "w2a": w2a, "b1a": b1a, "b3a": b3a,
        "b2d": np.ascontiguousarray(np.asarray(b2, np.float32)),
        "sb2f": _bias_f(sb2), "identf": np.eye(128, dtype=np.float32),
    }
    in_maps = []
    for c in range(8):
        b, q = c // 4, c % 4
        m = dict(m_common)
        m["x1"] = np.ascontiguousarray(x1_fm[b][:, 2048 * q:2048 * (q + 1)])
        in_maps.append(m)
    res = run_bass_kernel_spmd(nc, in_maps, core_ids=list(range(8)))
    out = np.empty((2, D, 8192), dtype=np.float32)
    for c in range(8):
        b, q = c // 4, c % 4
        out[b][:, 2048 * q:2048 * (q + 1)] = np.asarray(res.results[c]["out"])
    return out



def build_C1():
    nc = bass.Bass("TRN2", target_bir_lowering=False)
    es = ExitStack()
    S_LEN = 8192
    x2 = nc.dram_tensor("x2", [D, S_LEN], F32, kind="ExternalInput").ap()
    wq = nc.dram_tensor("wq", [D, 256], F32, kind="ExternalInput").ap()
    wk = nc.dram_tensor("wk", [D, 256], F32, kind="ExternalInput").ap()
    wv = nc.dram_tensor("wv", [D, 256], F32, kind="ExternalInput").ap()
    thr = nc.dram_tensor("thr", [1, S_LEN], F32, kind="ExternalInput").ap()
    out_t = nc.dram_tensor("out", [256, S_LEN], BF16, kind="ExternalOutput").ap()

    with es:
        B = Buffers(nc, es)
        S = Sched(nc, es)
        alloc_common(B)
        alloc_ln(B, 512)
        B.sb("qfm", (128, 2 * S_LEN), BF16)
        B.sb("kfm", (128, 2 * S_LEN), BF16)
        B.sb("vtm", (128, 64 * 256), BF16)
        B.sb("xg", (128, NCH * 512), F32)
        B.sb("y2g", (128, NCH * 512), BF16)
        B.sb("wq_sb", (128, NCH * 256), BF16)
        B.sb("wk_sb", (128, NCH * 256), BF16)
        B.sb("wv_sb", (128, NCH * 256), BF16)
        B.sb("attnT0", (128, 512), BF16)
        B.sb("attnT1", (128, 512), BF16)
        B.sb("thr_pc", (1, 512), F32)
        B.sb("thrb", (128, 512), F32)
        B.sb("kcol", (128, 64), F32)
        B.sb("mask_sb", (128, 512), BF16)
        B.sb("ctxbuf0", (128, 512), BF16)
        B.sb("ctxbuf1", (128, 512), BF16)
        B.sb("lnd", (128, 512), F32)
        B.sb("rden", (128, 512), F32)
        B.sb("ones_f32", (128, 128), F32)
        B.sb("b_nt", (128, 16), F32)
        B.sb("bq4", (128, 4), F32)
        B.sb("bk4", (128, 4), F32)
        B.sb("negB4", (128, 4), F32)
        B.sb("c_sc", (128, 1), F32)
        B.ps("PCTX", (128, 512), F32)
        B.ps("PDEN", (128, 512), F32)
        psum = B["PSBIG"]

        emit_init_consts(S, B, [("c_sc", 0.125)])
        S.comp("vector", lambda e: e.memset(B["ones_f32"][:], 1.0), writes=("ones_f32",))
        S.comp("gpsimd", lambda e: e.iota(B["kcol"][:], pattern=[[128, 64]],
                                          base=0, channel_multiplier=1,
                                          allow_small_or_imprecise_dtypes=True),
               writes=("kcol",))
        emit_weight_load(S, B, wq, B["wq_sb"], NCH, 256, "wq_sb")
        emit_weight_load(S, B, wk, B["wk_sb"], NCH, 256, "wk_sb")
        emit_weight_load(S, B, wv, B["wv_sb"], NCH, 256, "wv_sb")

        for g in range(16):
            g0 = g * 512
            S.dma([(B["xg"][:, k * 512:(k + 1) * 512],
                    x2[128 * k:128 * (k + 1), g0:g0 + 512]) for k in range(NCH)],
                  writes=("xg",))
            S.comp("vector", lambda e: e.tensor_copy(
                out=B["y2g"][:, 0:NCH * 512], in_=B["xg"][:, 0:NCH * 512]),
                reads=("xg",), writes=("y2g",))
            emit_ln_stats(S, B, lambda k: B["y2g"][:, k * 512:(k + 1) * 512],
                          ("y2g",), 512)
            emit_ln_apply_bf(S, B, lambda k: B["y2g"][:, k * 512:(k + 1) * 512],
                             ("y2g",), 512,
                             lambda k: B["y2g"][:, k * 512:(k + 1) * 512],
                             "y2g")
            for which, wkey in (("qfm", "wq_sb"), ("kfm", "wk_sb")):
                for mh in range(2):
                    half = (mh % 2) * 512
                    pk = "PSBIG_a" if mh % 2 == 0 else "PSBIG_b"

                    def mmqk(e, mh=mh, half=half, wkey=wkey):
                        last = None
                        for k in range(NCH):
                            last = e.matmul(
                                psum[:, half:half + 512],
                                B[wkey][:, k * 256 + 128 * mh:k * 256 + 128 * mh + 128],
                                B["y2g"][:, k * 512:(k + 1) * 512],
                                start=(k == 0), stop=(k == NCH - 1))
                        return last

                    S.comp("tensor", mmqk, reads=(wkey, "y2g"), writes=(pk,))
                    S.comp("scalar", lambda e, which=which, mh=mh, half=half, g0=g0: e.copy(
                        out=B[which][:, mh * 8192 + g0:mh * 8192 + g0 + 512],
                        in_=psum[:, half:half + 512]),
                        reads=(pk,), writes=(which,))
            for tt in range(4):
                half = (tt % 2) * 512
                pk = "PSBIG_a" if tt % 2 == 0 else "PSBIG_b"

                def mmv(e, tt=tt, half=half):
                    last = None
                    for k in range(NCH):
                        last = e.matmul(
                            psum[:, half:half + 256],
                            B["y2g"][:, k * 512 + tt * 128:k * 512 + tt * 128 + 128],
                            B["wv_sb"][:, k * 256:(k + 1) * 256],
                            start=(k == 0), stop=(k == NCH - 1))
                    return last

                S.comp("tensor", mmv, reads=("wv_sb", "y2g"), writes=(pk,))
                S.comp("scalar", lambda e, g=g, tt=tt, half=half: e.copy(
                    out=B["vtm"][:, (g * 4 + tt) * 256:(g * 4 + tt) * 256 + 256],
                    in_=psum[:, half:half + 256]),
                    reads=(pk,), writes=("vtm",))

        for h in range(4):
            hp, hl = h // 2, h % 2
            p0 = 64 * hl
            for which, bcol in (("qfm", "bq4"), ("kfm", "bk4")):
                for half_tok in range(2):
                    c0 = hp * 8192 + half_tok * 4096
                    S.comp("gpsimd", lambda e, which=which, p0=p0, c0=c0: e.tensor_mul(
                        out=B["sqbuf"][0:64, 0:4096],
                        in0=B[which][p0:p0 + 64, c0:c0 + 4096],
                        in1=B[which][p0:p0 + 64, c0:c0 + 4096]),
                        reads=(which,), writes=("sqbuf",))
                    for nt in range(8):
                        halfp = (nt % 2) * 512
                        pk = "PSBIG_a" if nt % 2 == 0 else "PSBIG_b"
                        S.comp("tensor", lambda e, nt=nt, halfp=halfp: e.matmul(
                            psum[:, halfp:halfp + 512], B["ones128"][0:64, 0:128],
                            B["sqbuf"][0:64, nt * 512:nt * 512 + 512],
                            start=True, stop=True),
                            reads=("sqbuf", "ones128"), writes=(pk,))
                        S.comp("vector", lambda e, nt=nt, halfp=halfp, half_tok=half_tok: e.reduce_max(
                            out=B["b_nt"][:, half_tok * 8 + nt:half_tok * 8 + nt + 1],
                            in_=psum[:, halfp:halfp + 512], axis=AX.X),
                            reads=(pk,), writes=("b_nt",))
                S.comp("vector", lambda e, bcol=bcol, h=h: e.reduce_max(
                    out=B[bcol][:, h:h + 1], in_=B["b_nt"][:, 0:16], axis=AX.X),
                    reads=("b_nt",), writes=(bcol,))
            S.comp("vector", lambda e, h=h: e.tensor_tensor(
                out=B["bq4"][:, h:h + 1], in0=B["bq4"][:, h:h + 1],
                in1=B["bk4"][:, h:h + 1], op=ALU.mult),
                reads=("bq4", "bk4"), writes=("bq4",))
            S.comp("scalar", lambda e, h=h: e.activation(
                out=B["bk4"][:, h:h + 1], in_=B["bq4"][:, h:h + 1], func=AF.Ln,
                bias=0.0, scale=1.0),
                reads=("bq4",), writes=("bk4",))
            S.comp("scalar", lambda e, h=h: e.activation(
                out=B["bq4"][:, h:h + 1], in_=B["bk4"][:, h:h + 1], func=AF.Exp,
                bias=0.0, scale=B["c_ph"][:]),
                reads=("bk4", "c_ph"), writes=("bq4",))
            S.comp("vector", lambda e, h=h: e.tensor_scalar(
                out=B["negB4"][:, h:h + 1], in0=B["bq4"][:, h:h + 1],
                scalar1=-0.125, scalar2=None, op0=ALU.mult),
                reads=("bq4",), writes=("negB4",))

        for hp in range(2):
            for qb in range(16):
                nkt = 4 * qb + 4
                S.dma([(B["thr_pc"][0:1, :], thr[0:1, qb * 512:qb * 512 + 512])],
                      writes=("thr_pc",))
                S.comp("tensor", lambda e: e.matmul(
                    psum[:, 0:512], B["ones_f32"][0:1, 0:128], B["thr_pc"][0:1, :],
                    start=True, stop=True),
                    reads=("thr_pc", "ones_f32"), writes=("PSBIG_a",))
                S.comp("vector", lambda e: e.tensor_copy(
                    out=B["thrb"][:], in_=psum[:, 0:512]),
                    reads=("PSBIG_a",), writes=("thrb",))
                for hl in range(2):
                    h = 2 * hp + hl
                    p0 = 64 * hl
                    prev = None
                    for kt in range(nkt):
                        half = (kt % 2) * 512
                        pk = "PSBIG_a" if kt % 2 == 0 else "PSBIG_b"
                        ak = "attnT0" if kt % 2 == 0 else "attnT1"
                        at = B[ak]
                        S.comp("tensor", lambda e, kt=kt, qb=qb, half=half, hp=hp, p0=p0: e.matmul(
                            psum[:, half:half + 512],
                            B["kfm"][p0:p0 + 64, hp * 8192 + kt * 128:hp * 8192 + kt * 128 + 128],
                            B["qfm"][p0:p0 + 64, hp * 8192 + qb * 512:hp * 8192 + qb * 512 + 512],
                            start=True, stop=True),
                            reads=("kfm", "qfm"), writes=(pk,))
                        S.comp("scalar", lambda e, at=at, half=half, h=h: e.activation(
                            out=at[:, 0:512], in_=psum[:, half:half + 512], func=AF.Exp,
                            bias=B["negB4"][:, h:h + 1], scale=B["c_sc"][:]),
                            reads=(pk, "negB4", "c_sc"), writes=(ak,))
                        if kt >= max(4 * qb, 1):
                            S.comp("vector", lambda e, kt=kt: e.tensor_scalar(
                                out=B["mask_sb"][:], in0=B["thrb"][:],
                                scalar1=B["kcol"][:, kt:kt + 1], scalar2=None,
                                op0=ALU.is_ge),
                                reads=("thrb", "kcol"), writes=("mask_sb",))
                            S.comp("vector", lambda e, at=at: e.tensor_tensor(
                                out=at[:, 0:512], in0=at[:, 0:512], in1=B["mask_sb"][:],
                                op=ALU.mult),
                                reads=(ak, "mask_sb"), writes=(ak,))
                        if prev is not None:
                            emit_c1_ctx(S, B, psum, prev, p0, h, nkt)
                        prev = kt
                    emit_c1_ctx(S, B, psum, prev, p0, h, nkt)
                cb = "ctxbuf0" if qb % 2 == 0 else "ctxbuf1"
                S.comp("scalar", lambda e: e.activation(
                    out=B["lnd"][:], in_=B["PDEN"][:, 0:512], func=AF.Ln,
                    bias=0.0, scale=1.0),
                    reads=("PDEN",), writes=("lnd",))
                S.comp("scalar", lambda e: e.activation(
                    out=B["rden"][:], in_=B["lnd"][:], func=AF.Exp,
                    bias=0.0, scale=B["c_neg1"][:]),
                    reads=("lnd", "c_neg1"), writes=("rden",))
                S.comp("vector", lambda e, cb=cb: e.tensor_tensor(
                    out=B[cb][:], in0=B["PCTX"][:, 0:512], in1=B["rden"][:],
                    op=ALU.mult),
                    reads=("PCTX", "rden"), writes=(cb,))
                S.dma([(out_t[hp * 128:(hp + 1) * 128, qb * 512:qb * 512 + 512],
                        B[cb][:])],
                      reads=(cb,), writes=("out",))

        S.barrier_wait("sync", ("out",))
        S.finalize()
    return nc


def emit_c1_ctx(S, B, psum, kt, p0, h, nkt):
    ak = "attnT0" if kt % 2 == 0 else "attnT1"
    at = B[ak]
    first = (kt == 0)
    last = (kt == nkt - 1)

    def mm(e, kt=kt, p0=p0, h=h, at=at, first=first, last=last):
        e.matmul(B["PCTX"][p0:p0 + 64, 0:512],
                 B["vtm"][:, kt * 256 + 64 * h:kt * 256 + 64 * h + 64],
                 at[:, 0:512], start=first, stop=last)
        return e.matmul(B["PDEN"][p0:p0 + 64, 0:512],
                        B["ones128"][:, 0:64],
                        at[:, 0:512], start=first, stop=last)

    S.comp("tensor", mm, reads=(ak, "vtm", "ones128"), writes=("PCTX", "PDEN"))


def run_C1(x2_fm, w_q, w_k, w_v):
    nc = build_C1()
    wqT = _fmT(w_q)
    wkT = _fmT(w_k)
    wvT = _fmT(w_v)
    thr = np.maximum(np.arange(8192, dtype=np.float32), 195.0).reshape(1, 8192)
    in_maps = []
    for c in range(8):
        b, hq = c // 4, c % 4
        in_maps.append({
            "x2": np.ascontiguousarray(x2_fm[b]),
            "wq": np.ascontiguousarray(wqT[:, 256 * hq:256 * (hq + 1)]),
            "wk": np.ascontiguousarray(wkT[:, 256 * hq:256 * (hq + 1)]),
            "wv": np.ascontiguousarray(wvT[:, 256 * hq:256 * (hq + 1)]),
            "thr": thr,
        })
    res = run_bass_kernel_spmd(nc, in_maps, core_ids=list(range(8)))
    ctx = np.empty((2, D, 8192), dtype=BF)
    for c in range(8):
        b, hq = c // 4, c % 4
        ctx[b][256 * hq:256 * (hq + 1), :] = np.asarray(res.results[c]["out"])
    return ctx



def build_C2():
    nc = bass.Bass("TRN2", target_bir_lowering=False)
    es = ExitStack()
    ctx = nc.dram_tensor("ctx", [D, 2048], BF16, kind="ExternalInput").ap()
    x2 = nc.dram_tensor("x2", [D, 2048], F32, kind="ExternalInput").ap()
    wo = nc.dram_tensor("wo", [D, D], F32, kind="ExternalInput").ap()
    bo = nc.dram_tensor("bo", [128, 8], F32, kind="ExternalInput").ap()
    out_t = nc.dram_tensor("out", [D, 2048], F32, kind="ExternalOutput").ap()

    with es:
        B = Buffers(nc, es)
        S = Sched(nc, es)
        alloc_common(B)
        alloc_ln(B, 512)
        B.sb("ctx_sb", (128, NCH * 2048), BF16)
        B.sb("wo_sb", (128, NCH * D), BF16)
        B.sb("bo_sb", (128, 8), F32)
        B.sb("dstbuf", (128, NCH * 2048), BF16)
        B.sb("xg", (128, NCH * 512), F32)
        B.sb("res_t", (128, 512), F32)
        psum = B["PSBIG"]

        emit_init_consts(S, B)
        emit_weight_load(S, B, wo, B["wo_sb"], NCH, D, "wo_sb")
        S.dma([(B["bo_sb"][:], bo)], writes=("bo_sb",))
        S.dma([(B["ctx_sb"][:, k * 2048:(k + 1) * 2048],
                ctx[128 * k:128 * (k + 1), :]) for k in range(NCH)],
              writes=("ctx_sb",))
        for nt in range(4):
            for m in range(NCH):
                half = ((nt * NCH + m) % 2) * 512
                pk = "PSBIG_a" if (nt * NCH + m) % 2 == 0 else "PSBIG_b"

                def mmo(e, nt=nt, m=m, half=half):
                    last = None
                    for k in range(NCH):
                        last = e.matmul(
                            psum[:, half:half + 512],
                            B["wo_sb"][:, k * D + 128 * m:k * D + 128 * m + 128],
                            B["ctx_sb"][:, k * 2048 + nt * 512:k * 2048 + nt * 512 + 512],
                            start=(k == 0), stop=(k == NCH - 1))
                    return last

                S.comp("tensor", mmo, reads=("wo_sb", "ctx_sb"), writes=(pk,))
                S.comp("scalar", lambda e, nt=nt, m=m, half=half: e.add(
                    out=B["dstbuf"][:, m * 2048 + nt * 512:m * 2048 + nt * 512 + 512],
                    in_=psum[:, half:half + 512], add=B["bo_sb"][:, m:m + 1]),
                    reads=(pk, "bo_sb"), writes=("dstbuf",))
        for g in range(4):
            g0 = g * 512
            emit_tail_group(S, B, lambda k, g0=g0: B["dstbuf"][:, k * 2048 + g0:
                                                              k * 2048 + g0 + 512],
                            ("dstbuf",), 512, x2, out_t, g0)
        S.barrier_wait("sync", ("out",))
        S.finalize()
    return nc


def run_C2(ctx_fm_bf, x2_fm, w_o, b_o):
    nc = build_C2()
    woT = _fmT(w_o)
    bop = _bias_p(b_o)
    in_maps = []
    for c in range(8):
        b, q = c // 4, c % 4
        in_maps.append({
            "ctx": np.ascontiguousarray(ctx_fm_bf[b][:, 2048 * q:2048 * (q + 1)]),
            "x2": np.ascontiguousarray(x2_fm[b][:, 2048 * q:2048 * (q + 1)]),
            "wo": woT, "bo": bop,
        })
    res = run_bass_kernel_spmd(nc, in_maps, core_ids=list(range(8)))
    out = np.empty((2, D, 8192), dtype=np.float32)
    for c in range(8):
        b, q = c // 4, c % 4
        out[b][:, 2048 * q:2048 * (q + 1)] = np.asarray(res.results[c]["out"])
    return out



D_BLK = 6 * 128


def build_D():
    nc = bass.Bass("TRN2", target_bir_lowering=False)
    es = ExitStack()
    HID = 3072
    x3 = nc.dram_tensor("x3", [D, 2048], F32, kind="ExternalInput").ap()
    wg_t = nc.dram_tensor("wg_t", [D, HID], I32, kind="ExternalInput").ap()
    wu_t = nc.dram_tensor("wu_t", [D, HID], I32, kind="ExternalInput").ap()
    wd_t = nc.dram_tensor("wd_t", [HID, D], I32, kind="ExternalInput").ap()
    out_t = nc.dram_tensor("out", [D, 2048], F32, kind="ExternalOutput").ap()

    with es:
        B = Buffers(nc, es)
        S = Sched(nc, es)
        alloc_common(B)
        alloc_ln(B, 512)
        B.sb("y3", (128, NCH * 1024), BF16)
        B.sb("acc", (128, NCH * 1024), F32)
        B.sb("accbf", (128, NCH * 512), BF16)
        B.sb("xg", (128, NCH * 512), F32)
        B.sb("res_t", (128, 512), F32)
        B.sb("istage0", (128, 512), I32)
        B.sb("istage1", (128, 512), I32)
        for p in range(2):
            B.sb(f"wgb{p}", (128, NCH * 768), BF16)
            B.sb(f"wub{p}", (128, NCH * 768), BF16)
            B.sb(f"wdb{p}", (128, 6 * D), BF16)
        B.sb("hbuf", (128, 6 * 1024), BF16)
        B.sb("g1sb", (128, 512), BF16)
        B.sb("g3sb", (128, 512), BF16)
        B.sb("c127", (128, 1), F32)
        psum = B["PSBIG"]

        emit_init_consts(S, B, [("c127", 1.0 / 127.0)])

        for th in range(2):
            t0 = th * 1024
            for g in range(2):
                g0 = t0 + g * 512
                S.dma([(B["xg"][:, k * 512:(k + 1) * 512],
                        x3[128 * k:128 * (k + 1), g0:g0 + 512]) for k in range(NCH)],
                      writes=("xg",))
                S.comp("vector", lambda e: e.tensor_copy(
                    out=B["accbf"][:, 0:NCH * 512], in_=B["xg"][:, 0:NCH * 512]),
                    reads=("xg",), writes=("accbf",))
                emit_ln_stats(S, B, lambda k: B["accbf"][:, k * 512:(k + 1) * 512],
                              ("accbf",), 512)
                emit_ln_apply_bf(S, B, lambda k: B["accbf"][:, k * 512:(k + 1) * 512],
                                 ("accbf",), 512,
                                 lambda k, g=g: B["y3"][:, k * 1024 + g * 512:
                                                        k * 1024 + g * 512 + 512],
                                 "y3")
            for blk in range(4):
                p = blk % 2
                wgk, wuk, wdk = f"wgb{p}", f"wub{p}", f"wdb{p}"
                c0 = blk * 768
                for k in range(NCH):
                    for p0 in (0, 512):
                        pw = 512 if p0 == 0 else 256
                        stg = "istage0" if (k + p0 // 512) % 2 == 0 else "istage1"
                        S.dma([(B[stg][:, 0:pw],
                                wg_t[128 * k:128 * (k + 1), c0 + p0:c0 + p0 + pw])],
                              writes=(stg,))
                        S.comp("vector", lambda e, k=k, p0=p0, pw=pw, stg=stg, wgk=wgk: e.tensor_copy(
                            out=B[wgk][:, k * 768 + p0:k * 768 + p0 + pw],
                            in_=B[stg][:, 0:pw]),
                            reads=(stg,), writes=(wgk,))
                        stg2 = "istage0" if (k + p0 // 512) % 2 == 1 else "istage1"
                        S.dma([(B[stg2][:, 0:pw],
                                wu_t[128 * k:128 * (k + 1), c0 + p0:c0 + p0 + pw])],
                              writes=(stg2,))
                        S.comp("vector", lambda e, k=k, p0=p0, pw=pw, stg2=stg2, wuk=wuk: e.tensor_copy(
                            out=B[wuk][:, k * 768 + p0:k * 768 + p0 + pw],
                            in_=B[stg2][:, 0:pw]),
                            reads=(stg2,), writes=(wuk,))
                for j in range(6):
                    for p0 in (0, 512):
                        stg = "istage0" if (j + p0 // 512) % 2 == 0 else "istage1"
                        S.dma([(B[stg][:, 0:512],
                                wd_t[(blk * 6 + j) * 128:(blk * 6 + j) * 128 + 128,
                                     p0:p0 + 512])],
                              writes=(stg,))
                        S.comp("vector", lambda e, j=j, p0=p0, stg=stg, wdk=wdk: e.tensor_copy(
                            out=B[wdk][:, j * D + p0:j * D + p0 + 512],
                            in_=B[stg][:, 0:512]),
                            reads=(stg,), writes=(wdk,))
                for mc in range(6):
                    for nt in range(2):
                        def mmg(e, mc=mc, nt=nt, wgk=wgk):
                            last = None
                            for k in range(NCH):
                                last = e.matmul(
                                    psum[:, 0:512],
                                    B[wgk][:, k * 768 + 128 * mc:k * 768 + 128 * mc + 128],
                                    B["y3"][:, k * 1024 + nt * 512:k * 1024 + nt * 512 + 512],
                                    start=(k == 0), stop=(k == NCH - 1))
                            return last

                        S.comp("tensor", mmg, reads=(wgk, "y3"), writes=("PSBIG_a",))
                        S.comp("scalar", lambda e: e.activation(
                            out=B["g1sb"][:], in_=psum[:, 0:512], func=AF.Silu,
                            bias=0.0, scale=B["c127"][:]),
                            reads=("PSBIG_a", "c127"), writes=("g1sb",))

                        def mmu(e, mc=mc, nt=nt, wuk=wuk):
                            last = None
                            for k in range(NCH):
                                last = e.matmul(
                                    psum[:, 512:1024],
                                    B[wuk][:, k * 768 + 128 * mc:k * 768 + 128 * mc + 128],
                                    B["y3"][:, k * 1024 + nt * 512:k * 1024 + nt * 512 + 512],
                                    start=(k == 0), stop=(k == NCH - 1))
                            return last

                        S.comp("tensor", mmu, reads=(wuk, "y3"), writes=("PSBIG_b",))
                        S.comp("scalar", lambda e: e.copy(
                            out=B["g3sb"][:], in_=psum[:, 512:1024]),
                            reads=("PSBIG_b",), writes=("g3sb",))
                        S.comp("vector", lambda e, mc=mc, nt=nt: e.tensor_tensor(
                            out=B["hbuf"][:, mc * 1024 + nt * 512:mc * 1024 + nt * 512 + 512],
                            in0=B["g1sb"][:], in1=B["g3sb"][:], op=ALU.mult),
                            reads=("g1sb", "g3sb"), writes=("hbuf",))
                for nt in range(2):
                    for m in range(NCH):
                        half = ((nt * NCH + m) % 2) * 512
                        pk = "PSBIG_a" if (nt * NCH + m) % 2 == 0 else "PSBIG_b"

                        def mmd(e, nt=nt, m=m, half=half, wdk=wdk):
                            last = None
                            for j in range(6):
                                last = e.matmul(
                                    psum[:, half:half + 512],
                                    B[wdk][:, j * D + 128 * m:j * D + 128 * m + 128],
                                    B["hbuf"][:, j * 1024 + nt * 512:j * 1024 + nt * 512 + 512],
                                    start=(j == 0), stop=(j == 5))
                            return last

                        S.comp("tensor", mmd, reads=(wdk, "hbuf"), writes=(pk,))
                        if blk == 0:
                            S.comp("vector", lambda e, nt=nt, m=m, half=half: e.tensor_scalar(
                                out=B["acc"][:, m * 1024 + nt * 512:m * 1024 + nt * 512 + 512],
                                in0=psum[:, half:half + 512],
                                scalar1=1.0 / (127.0 * 127.0), scalar2=None, op0=ALU.mult),
                                reads=(pk,), writes=("acc",))
                        else:
                            S.comp("vector", lambda e, nt=nt, m=m, half=half: e.scalar_tensor_tensor(
                                out=B["acc"][:, m * 1024 + nt * 512:m * 1024 + nt * 512 + 512],
                                in0=psum[:, half:half + 512], scalar=1.0 / (127.0 * 127.0),
                                in1=B["acc"][:, m * 1024 + nt * 512:m * 1024 + nt * 512 + 512],
                                op0=ALU.mult, op1=ALU.add),
                                reads=(pk, "acc"), writes=("acc",))
            for g in range(2):
                g0 = t0 + g * 512
                for k in range(NCH):
                    S.comp("vector", lambda e, k=k, g=g: e.tensor_copy(
                        out=B["accbf"][:, k * 512:(k + 1) * 512],
                        in_=B["acc"][:, k * 1024 + g * 512:k * 1024 + g * 512 + 512]),
                        reads=("acc",), writes=("accbf",))
                emit_tail_group(S, B, lambda k: B["accbf"][:, k * 512:(k + 1) * 512],
                                ("accbf",), 512, x3, out_t, g0)

        S.barrier_wait("sync", ("out",))
        S.finalize()
    return nc


def run_D(x3_fm, w_gate, w_up, w_down):
    nc = build_D()
    wg_t = np.ascontiguousarray(np.asarray(w_gate).T)
    wu_t = np.ascontiguousarray(np.asarray(w_up).T)
    wd_t = np.ascontiguousarray(np.asarray(w_down).T)
    in_maps = []
    for c in range(8):
        b, q = c // 4, c % 4
        in_maps.append({
            "x3": np.ascontiguousarray(x3_fm[b][:, 2048 * q:2048 * (q + 1)]),
            "wg_t": wg_t, "wu_t": wu_t, "wd_t": wd_t,
        })
    res = run_bass_kernel_spmd(nc, in_maps, core_ids=list(range(8)))
    out = np.empty((2, D, 8192), dtype=np.float32)
    for c in range(8):
        b, q = c // 4, c % 4
        out[b][:, 2048 * q:2048 * (q + 1)] = np.asarray(res.results[c]["out"])
    return out



def kernel(**inputs):
    g = lambda n: inputs[n]
    x = np.asarray(g("L0_attn_x"), np.float32)
    x_fm = np.ascontiguousarray(x.transpose(0, 2, 1))

    l0 = run_A(0, x_fm, None, g("L0_attn_Win0"), g("L0_attn_bin0"),
               g("L0_attn_Wout0"), g("L0_attn_bout0"))
    l1 = run_A(1, x_fm, l0, g("L0_attn_Win1"), g("L0_attn_bin1"),
               g("L0_attn_Wout1"), g("L0_attn_bout1"))
    x1 = run_A(2, x_fm, l1, g("L0_attn_Win2"), g("L0_attn_bin2"),
               g("L0_attn_Wout2"), g("L0_attn_bout2"))
    x2 = run_B(x1, g("L0_ffn_gate_w"), g("L0_ffn_gate_b"), g("L0_ffn_w1"),
               g("L0_ffn_b1"), g("L0_ffn_w3"), g("L0_ffn_b3"), g("L0_ffn_w2"),
               g("L0_ffn_b2"), g("L0_ffn_sw1"), g("L0_ffn_sb1"), g("L0_ffn_sw3"),
               g("L0_ffn_sb3"), g("L0_ffn_sw2"), g("L0_ffn_sb2"))
    ctx = run_C1(x2, g("L1_attn_w_q"), g("L1_attn_w_k"), g("L1_attn_w_v"))
    x3 = run_C2(ctx, x2, g("L1_attn_w_o"), g("L1_attn_b_o"))
    y = run_D(x3, g("L1_ffn_w_gate"), g("L1_ffn_w_up"), g("L1_ffn_w_down"))
    return np.ascontiguousarray(y.transpose(0, 2, 1))


# version 04
# speedup vs baseline: 1.0767x; 1.0767x over previous
import numpy as np
import ml_dtypes
from contextlib import ExitStack

import concourse.bass as bass
import concourse.mybir as mybir
from concourse.bass_utils import run_bass_kernel_spmd

F32 = mybir.dt.float32
BF16 = mybir.dt.bfloat16
I32 = mybir.dt.int32
AF = mybir.ActivationFunctionType
ALU = mybir.AluOpType
AX = mybir.AxisListType
BF = ml_dtypes.bfloat16

ENGS = ("sync", "tensor", "vector", "scalar", "gpsimd")
SELF_SER = ("vector", "scalar", "gpsimd")

D = 1024
NCH = 8



class Op:
    __slots__ = ("fn", "waits", "inc")

    def __init__(self, fn, waits, inc):
        self.fn = fn
        self.waits = waits
        self.inc = inc


class Sched:

    N_DMA_POOL = 8

    def __init__(self, nc, es):
        self.nc = nc
        self.ops = {e: [] for e in ENGS}
        self.val = {e: 0 for e in ENGS}
        self.seq = {e: es.enter_context(nc.semaphore("seq_" + e)) for e in ENGS}
        self.dpool = [es.enter_context(nc.semaphore(f"dpool{j}"))
                      for j in range(self.N_DMA_POOL)]
        self.dpool_tot = [0] * self.N_DMA_POOL
        self.dpool_idx = 0
        self.last_w = {}
        self.readers = {}
        self.waited = {e: {} for e in ENGS}

    def _sem_of(self, name):
        if name.startswith("@d"):
            return self.dpool[int(name[2:])]
        return self.seq[name]

    def _add_wait(self, eng, dep, waits):
        deng, dval = dep
        if deng == eng:
            return
        if self.waited[eng].get(deng, 0) >= dval:
            return
        self.waited[eng][deng] = dval
        waits.append((self._sem_of(deng), dval))

    def _book(self, eng, fn, reads, writes, inc, completion=None, pre_waits=()):
        waits = list(pre_waits)
        if eng in SELF_SER and self.val[eng] > 0:
            waits.append((self.seq[eng], self.val[eng]))
        for r in reads:
            lw = self.last_w.get(r)
            if lw is not None:
                self._add_wait(eng, lw, waits)
        for w in writes:
            lw = self.last_w.get(w)
            if lw is not None:
                self._add_wait(eng, lw, waits)
            for reng, rval in self.readers.get(w, {}).items():
                self._add_wait(eng, (reng, rval), waits)
        self.val[eng] += inc
        comp = completion if completion is not None else (eng, self.val[eng])
        for r in reads:
            self.readers.setdefault(r, {})[comp[0]] = comp[1]
        for w in writes:
            self.last_w[w] = comp
            self.readers[w] = {}
        self.ops[eng].append(Op(fn, waits, inc))

    def dma(self, transfers, reads=(), writes=(), queue="sync"):
        j = self.dpool_idx
        self.dpool_idx = (j + 1) % self.N_DMA_POOL
        sem = self.dpool[j]
        pre = []
        if self.dpool_tot[j] > 0:
            self._add_wait(queue, (f"@d{j}", self.dpool_tot[j]), pre)
        self.dpool_tot[j] += 16 * len(transfers)
        tot = self.dpool_tot[j]

        def fn(eng):
            for dst, src in transfers:
                eng.dma_start(out=dst, in_=src).then_inc(sem, 16)

        self._book(queue, fn, reads, writes, 0,
                   completion=(f"@d{j}", tot), pre_waits=pre)

    def comp(self, eng, fn, reads=(), writes=()):
        sem = self.seq[eng]

        if eng == "tensor":
            def wrapped(e):
                fn(e)
                e.drain().then_inc(sem, 1)
        else:
            def wrapped(e):
                last = fn(e)
                last.then_inc(sem, 1)

        self._book(eng, wrapped, reads, writes, 1)

    def barrier_wait(self, eng, keys):
        waits = []
        for r in keys:
            lw = self.last_w.get(r)
            if lw is not None:
                self._add_wait(eng, lw, waits)
        if eng in SELF_SER and self.val[eng] > 0:
            waits.append((self.seq[eng], self.val[eng]))
        self.ops[eng].append(Op(None, waits, 0))

    def finalize(self):
        nc = self.nc
        with nc.Block() as block:
            for e in ENGS:
                ops = self.ops[e]
                if not ops:
                    continue

                def runner(eng, ops=ops):
                    for op in ops:
                        for sem, v in op.waits:
                            eng.wait_ge(sem, v)
                        if op.fn is not None:
                            op.fn(eng)

                getattr(block, e)(runner)


class Buffers:
    def __init__(self, nc, es):
        self.nc = nc
        self.es = es
        self.t = {}

    def sb(self, name, shape, dtype):
        h = self.es.enter_context(self.nc.sbuf_tensor("sb_" + name, list(shape), dtype))
        self.t[name] = h
        return h

    def ps(self, name, shape, dtype=F32):
        h = self.es.enter_context(self.nc.psum_tensor("ps_" + name, list(shape), dtype))
        self.t[name] = h
        return h

    def __getitem__(self, name):
        return self.t[name]

    def __contains__(self, name):
        return name in self.t



def emit_weight_load(S, B, dram_ap, sb_bf, n_chunks, width, tag):
    piece = 512
    idx = 0
    for k in range(n_chunks):
        for p0 in range(0, width, piece):
            pw = min(piece, width - p0)
            stg = "wstage0" if idx % 2 == 0 else "wstage1"
            idx += 1
            S.dma([(B[stg][:, 0:pw], dram_ap[128 * k:128 * (k + 1), p0:p0 + pw])],
                  writes=(stg,))
            S.comp("gpsimd",
                   lambda e, k=k, p0=p0, pw=pw, stg=stg: e.tensor_copy(
                       out=sb_bf[:, width * k + p0:width * k + p0 + pw],
                       in_=B[stg][:, 0:pw]),
                   reads=(stg,), writes=(tag,))


def emit_flat_bias_load(S, B, dram_ap, sb_bf, width, tag):
    piece = 512
    idx = 0
    for p0 in range(0, width, piece):
        pw = min(piece, width - p0)
        stg = "wstage0" if idx % 2 == 0 else "wstage1"
        idx += 1
        S.dma([(B[stg][0:1, 0:pw], dram_ap[0:1, p0:p0 + pw])], writes=(stg,))
        S.comp("gpsimd", lambda e, p0=p0, pw=pw, stg=stg: e.tensor_copy(
            out=sb_bf[0:1, p0:p0 + pw], in_=B[stg][0:1, 0:pw]),
            reads=(stg,), writes=(tag,))


def emit_ln_stats(S, B, chunk_ap_fn, src_keys, width):
    psum = B["PSBIG"]

    def mm_sum(e):
        last = None
        for k in range(NCH):
            last = e.matmul(psum[:, 0:width], B["ones128"][:, 0:128],
                            chunk_ap_fn(k), start=(k == 0), stop=(k == NCH - 1))
        return last

    S.comp("tensor", mm_sum, reads=tuple(src_keys) + ("ones128",), writes=("PSBIG_a",))

    def sq(e):
        last = None
        for k in range(NCH):
            last = e.tensor_mul(out=B["sqbuf"][:, k * width:(k + 1) * width],
                                in0=chunk_ap_fn(k), in1=chunk_ap_fn(k))
        return last

    S.comp("gpsimd", sq, reads=tuple(src_keys), writes=("sqbuf",))

    def mm_sq(e):
        last = None
        for k in range(NCH):
            last = e.matmul(psum[:, 512:512 + width], B["ones128"][:, 0:128],
                            B["sqbuf"][:, k * width:(k + 1) * width],
                            start=(k == 0), stop=(k == NCH - 1))
        return last

    S.comp("tensor", mm_sq, reads=("sqbuf", "ones128"), writes=("PSBIG_b",))
    S.comp("vector", lambda e: e.tensor_scalar(out=B["st_mu"][:, 0:width],
                                               in0=psum[:, 0:width],
                                               scalar1=1.0 / D, scalar2=None, op0=ALU.mult),
           reads=("PSBIG_a",), writes=("st_mu",))
    S.comp("vector", lambda e: e.tensor_scalar(out=B["st_t1"][:, 0:width],
                                               in0=psum[:, 512:512 + width],
                                               scalar1=1.0 / D, scalar2=None, op0=ALU.mult),
           reads=("PSBIG_b",), writes=("st_t1",))
    S.comp("vector", lambda e: e.tensor_tensor(out=B["st_t2"][:, 0:width],
                                               in0=B["st_mu"][:, 0:width],
                                               in1=B["st_mu"][:, 0:width], op=ALU.mult),
           reads=("st_mu",), writes=("st_t2",))
    S.comp("vector", lambda e: e.tensor_tensor(out=B["st_t1"][:, 0:width],
                                               in0=B["st_t1"][:, 0:width],
                                               in1=B["st_t2"][:, 0:width], op=ALU.subtract),
           reads=("st_t1", "st_t2"), writes=("st_t1",))
    S.comp("scalar", lambda e: e.activation(out=B["st_t2"][:, 0:width],
                                            in_=B["st_t1"][:, 0:width],
                                            func=AF.Ln, bias=B["c_eps"][:], scale=1.0),
           reads=("st_t1", "c_eps"), writes=("st_t2",))
    S.comp("scalar", lambda e: e.activation(out=B["st_r"][:, 0:width],
                                            in_=B["st_t2"][:, 0:width],
                                            func=AF.Exp, bias=0.0, scale=B["c_nh"][:]),
           reads=("st_t2", "c_nh"), writes=("st_r",))
    S.comp("vector", lambda e: e.tensor_copy(out=B["st_mu_bf"][:, 0:width],
                                             in_=B["st_mu"][:, 0:width]),
           reads=("st_mu",), writes=("st_mu_bf",))
    S.comp("vector", lambda e: e.tensor_copy(out=B["st_r_bf"][:, 0:width],
                                             in_=B["st_r"][:, 0:width]),
           reads=("st_r",), writes=("st_r_bf",))


def emit_stats_tm(S, B, width, col0):
    ntt = width // 128
    for tt in range(ntt):
        S.comp("tensor", lambda e, tt=tt: e.transpose(
            B["PSTR"][:], B["st_mu"][:, tt * 128:(tt + 1) * 128], B["ident_f32"][:]),
            reads=("st_mu", "ident_f32"), writes=("PSTR",))
        S.comp("vector", lambda e, tt=tt: e.tensor_copy(
            out=B["st_mu_tm"][:, col0 + tt:col0 + tt + 1], in_=B["PSTR"][:, 0:1]),
            reads=("PSTR",), writes=("st_mu_tm",))
        S.comp("tensor", lambda e, tt=tt: e.transpose(
            B["PSTR"][:], B["st_r"][:, tt * 128:(tt + 1) * 128], B["ident_f32"][:]),
            reads=("st_r", "ident_f32", "st_mu_tm"), writes=("PSTR",))
        S.comp("vector", lambda e, tt=tt: e.tensor_copy(
            out=B["st_r_tm"][:, col0 + tt:col0 + tt + 1], in_=B["PSTR"][:, 0:1]),
            reads=("PSTR",), writes=("st_r_tm",))


def emit_ln_apply_bf(S, B, chunk_ap_fn, src_keys, width, dst_ap_fn, dst_key):
    for k in range(NCH):
        S.comp("vector", lambda e, k=k: e.tensor_tensor(
            out=B["ln_t"][:, 0:width], in0=chunk_ap_fn(k),
            in1=B["st_mu_bf"][:, 0:width], op=ALU.subtract),
            reads=tuple(src_keys) + ("st_mu_bf",), writes=("ln_t",))
        S.comp("vector", lambda e, k=k: e.tensor_tensor(
            out=dst_ap_fn(k), in0=B["ln_t"][:, 0:width],
            in1=B["st_r_bf"][:, 0:width], op=ALU.mult),
            reads=("ln_t", "st_r_bf"), writes=(dst_key,))


def emit_mha(S, B, src_bf, src_key, src_col0, T, dst_write_fn,
             w_key="w_in", wo_key="w_out", scale=1.0 / 32.0, cscale_key="c_sc",
             n_heads=8):
    src_stride = src_bf.shape[1] // NCH
    n_nt = (T + 511) // 512
    ntt = T // 128
    psum = B["PSBIG"]
    w_in = B["w_in"]
    w_out = B["w_out"]
    ctx_stride = B["ctxfm"].shape[1] // n_heads

    for h in range(n_heads):
        for which, off, bias_col in (("qh", 128 * h, h), ("kh", D + 128 * h, 8 + h)):
            for nt in range(n_nt):
                tw = min(512, T - nt * 512)
                half = (nt % 2) * 512
                pk = "PSBIG_a" if nt % 2 == 0 else "PSBIG_b"

                def mmq(e, off=off, nt=nt, tw=tw, half=half):
                    last = None
                    for k in range(NCH):
                        last = e.matmul(
                            psum[:, half:half + tw],
                            w_in[:, k * 3 * D + off:k * 3 * D + off + 128],
                            src_bf[:, k * src_stride + src_col0 + nt * 512:
                                   k * src_stride + src_col0 + nt * 512 + tw],
                            start=(k == 0), stop=(k == NCH - 1))
                    return last

                S.comp("tensor", mmq, reads=(w_key, src_key), writes=(pk,))
                S.comp("scalar", lambda e, which=which, nt=nt, tw=tw, half=half,
                       bias_col=bias_col: e.add(
                           out=B[which][:, nt * 512:nt * 512 + tw],
                           in_=psum[:, half:half + tw],
                           add=B["b_in_p"][:, bias_col:bias_col + 1]),
                       reads=(pk, "b_in_p"), writes=(which,))
        for tg in range(0, ntt, 4):
            tts = [t for t in range(tg, min(tg + 4, ntt))]
            par = (tg // 4) % 2
            pb = psum if par == 0 else B["PS2"]
            pk = ("PSBIG_a", "PSBIG_b") if par == 0 else ("PS2_a", "PS2_b")

            def mmv(e, tts=tts, h=h, pb=pb):
                last = None
                for i, tt in enumerate(tts):
                    for k in range(NCH):
                        e.matmul(
                            pb[:, i * 128:i * 128 + 128],
                            src_bf[:, k * src_stride + src_col0 + tt * 128:
                                   k * src_stride + src_col0 + tt * 128 + 128],
                            w_in[:, k * 3 * D + 2 * D + 128 * h:
                                 k * 3 * D + 2 * D + 128 * h + 128],
                            start=(k == 0), stop=False)
                    last = e.matmul(pb[:, i * 128:i * 128 + 128],
                                    B["ones_row"][0:1, 0:128],
                                    B["b_in_f_bf"][0:1, 2 * D + 128 * h:2 * D + 128 * h + 128],
                                    start=False, stop=True)
                return last

            S.comp("tensor", mmv, reads=(w_key, src_key, "ones_row", "b_in_f_bf"),
                   writes=pk)
            S.comp("scalar", lambda e, tg=tg, tts=tts, pb=pb: e.copy(
                out=B["vtm"][:, tg * 128:tg * 128 + len(tts) * 128],
                in_=pb[:, 0:len(tts) * 128]),
                reads=pk, writes=("vtm",))
        for which, bcol in (("qh", "b_q"), ("kh", "b_k")):
            S.comp("gpsimd", lambda e, which=which: e.tensor_mul(
                out=B["sqbuf"][:, 0:T], in0=B[which][:, 0:T], in1=B[which][:, 0:T]),
                reads=(which,), writes=("sqbuf",))
            for nt in range(n_nt):
                tw = min(512, T - nt * 512)
                half = (nt % 2) * 512
                pk = "PSBIG_a" if nt % 2 == 0 else "PSBIG_b"
                S.comp("tensor", lambda e, nt=nt, tw=tw, half=half: e.matmul(
                    psum[:, half:half + tw], B["ones128"][:, 0:128],
                    B["sqbuf"][:, nt * 512:nt * 512 + tw], start=True, stop=True),
                    reads=("sqbuf", "ones128"), writes=(pk,))
                S.comp("vector", lambda e, nt=nt, tw=tw, half=half: e.reduce_max(
                    out=B["b_nt"][:, nt:nt + 1], in_=psum[:, half:half + tw], axis=AX.X),
                    reads=(pk,), writes=("b_nt",))
            S.comp("vector", lambda e, bcol=bcol: e.reduce_max(
                out=B[bcol][:], in_=B["b_nt"][:, 0:n_nt], axis=AX.X),
                reads=("b_nt",), writes=(bcol,))
        S.comp("vector", lambda e: e.tensor_tensor(out=B["b_q"][:], in0=B["b_q"][:],
                                                   in1=B["b_k"][:], op=ALU.mult),
               reads=("b_q", "b_k"), writes=("b_q",))
        S.comp("scalar", lambda e: e.activation(out=B["b_k"][:], in_=B["b_q"][:],
                                                func=AF.Ln, bias=0.0, scale=1.0),
               reads=("b_q",), writes=("b_k",))
        S.comp("scalar", lambda e: e.activation(out=B["b_q"][:], in_=B["b_k"][:],
                                                func=AF.Exp, bias=0.0, scale=B["c_ph"][:]),
               reads=("b_k", "c_ph"), writes=("b_q",))
        S.comp("vector", lambda e: e.tensor_scalar(out=B["negB"][:], in0=B["b_q"][:],
                                                   scalar1=-scale, scalar2=None,
                                                   op0=ALU.mult),
               reads=("b_q",), writes=("negB",))
        nkt = T // 128
        for qb in range(n_nt):
            qw = min(512, T - qb * 512)
            npair = (nkt + 1) // 2
            prev = None
            for pr in range(npair):
                kts = [2 * pr + j for j in range(2) if 2 * pr + j < nkt]
                pb, pk = (psum, ("PSBIG_a", "PSBIG_b")) if pr % 2 == 0 else                          (B["PS2"], ("PS2_a", "PS2_b"))
                ak = "attnT0" if pr % 2 == 0 else "attnT1"
                at = B[ak]

                def mmsc(e, kts=kts, qb=qb, qw=qw, pb=pb):
                    return mm_scores_body(e, B, kts, qb, qw, pb)

                if prev is not None:
                    pkts, pak = prev

                    def mmsc2(e, kts=kts, qb=qb, qw=qw, pb=pb, pkts=pkts, pak=pak):
                        mm_scores_body(e, B, kts, qb, qw, pb)
                        return mm_ctx_body(e, B, pkts, pak, qw, nkt)

                    S.comp("tensor", mmsc2,
                           reads=("kh", "qh", pak, "vtm", "ones128"),
                           writes=pk + ("PCTX", "PDEN"))
                else:
                    S.comp("tensor", mmsc, reads=("kh", "qh"), writes=pk)

                def expf(e, kts=kts, qw=qw, at=at, pb=pb):
                    if len(kts) == 2 and qw == 512:
                        return e.activation(
                            out=at[:, 0:1024], in_=pb[:, 0:1024], func=AF.Exp,
                            bias=B["negB"][:], scale=B[cscale_key][:])
                    last = None
                    for j in range(len(kts)):
                        last = e.activation(
                            out=at[:, j * 512:j * 512 + qw],
                            in_=pb[:, j * 512:j * 512 + qw], func=AF.Exp,
                            bias=B["negB"][:], scale=B[cscale_key][:])
                    return last

                S.comp("scalar", expf, reads=pk + ("negB", cscale_key), writes=(ak,))
                prev = (kts, ak)
            emit_mha_ctx(S, B, prev[0], prev[1], qw, nkt)
            S.comp("scalar", lambda e, qw=qw: e.activation(
                out=B["lnd"][:, 0:qw], in_=B["PDEN"][:, 0:qw], func=AF.Ln,
                bias=0.0, scale=1.0),
                reads=("PDEN",), writes=("lnd",))
            S.comp("scalar", lambda e, qw=qw: e.activation(
                out=B["rden"][:, 0:qw], in_=B["lnd"][:, 0:qw], func=AF.Exp,
                bias=0.0, scale=B["c_neg1"][:]),
                reads=("lnd", "c_neg1"), writes=("rden",))
            S.comp("vector", lambda e, h=h, qb=qb, qw=qw: e.tensor_tensor(
                out=B["ctxfm"][:, h * ctx_stride + qb * 512:h * ctx_stride + qb * 512 + qw],
                in0=B["PCTX"][:, 0:qw], in1=B["rden"][:, 0:qw], op=ALU.mult),
                reads=("PCTX", "rden"), writes=("ctxfm",))
    for nt in range(n_nt):
        tw = min(512, T - nt * 512)
        for m in range(NCH):
            half = ((nt * NCH + m) % 2) * 512
            pk = "PSBIG_a" if (nt * NCH + m) % 2 == 0 else "PSBIG_b"

            def mmo(e, nt=nt, tw=tw, m=m, half=half):
                last = None
                for k in range(NCH):
                    last = e.matmul(
                        psum[:, half:half + tw],
                        w_out[:, k * D + 128 * m:k * D + 128 * m + 128],
                        B["ctxfm"][:, k * ctx_stride + nt * 512:
                                   k * ctx_stride + nt * 512 + tw],
                        start=(k == 0), stop=(k == NCH - 1))
                return last

            S.comp("tensor", mmo, reads=(wo_key, "ctxfm"), writes=(pk,))
            dst_write_fn(S, B, nt, tw, m, psum[:, half:half + tw], pk)


def mm_scores_body(e, B, kts, qb, qw, pb):
    last = None
    for j, kt in enumerate(kts):
        last = e.matmul(
            pb[:, j * 512:j * 512 + qw],
            B["kh"][:, kt * 128:(kt + 1) * 128],
            B["qh"][:, qb * 512:qb * 512 + qw], start=True, stop=True)
    return last


def mm_ctx_body(e, B, kts, ak, qw, nkt):
    at = B[ak]
    last = None
    for j, kt in enumerate(kts):
        e.matmul(B["PCTX"][:, 0:qw], B["vtm"][:, kt * 128:(kt + 1) * 128],
                 at[:, j * 512:j * 512 + qw],
                 start=(kt == 0), stop=(kt == nkt - 1))
        last = e.matmul(B["PDEN"][:, 0:qw], B["ones128"][:, 0:128],
                        at[:, j * 512:j * 512 + qw],
                        start=(kt == 0), stop=(kt == nkt - 1))
    return last


def emit_mha_ctx(S, B, kts, ak, qw, nkt):
    S.comp("tensor", lambda e, kts=kts, ak=ak: mm_ctx_body(e, B, kts, ak, qw, nkt),
           reads=(ak, "vtm", "ones128"), writes=("PCTX", "PDEN"))


def emit_init_consts(S, B, extra=()):
    S.comp("vector", lambda e: e.memset(B["ones128"][:], 1.0), writes=("ones128",))
    S.comp("vector", lambda e: e.memset(B["ones_row"][:], 1.0), writes=("ones_row",))
    S.comp("vector", lambda e: e.memset(B["c_eps"][:], 1e-5), writes=("c_eps",))
    S.comp("vector", lambda e: e.memset(B["c_nh"][:], -0.5), writes=("c_nh",))
    S.comp("vector", lambda e: e.memset(B["c_ph"][:], 0.5), writes=("c_ph",))
    S.comp("vector", lambda e: e.memset(B["c_neg1"][:], -1.0), writes=("c_neg1",))
    for name, v in extra:
        S.comp("vector", lambda e, name=name, v=v: e.memset(B[name][:], v),
               writes=(name,))


def alloc_common(B):
    B.sb("ones128", (128, 128), BF16)
    B.sb("ones_row", (1, 512), BF16)
    B.sb("c_eps", (128, 1), F32)
    B.sb("c_nh", (128, 1), F32)
    B.sb("c_ph", (128, 1), F32)
    B.sb("c_neg1", (128, 1), F32)
    B.sb("wstage0", (128, 512), F32)
    B.sb("wstage1", (128, 512), F32)
    B.ps("PSBIG", (128, 1024), F32)


def alloc_ln(B, width):
    if "sqbuf" not in B:
        B.sb("sqbuf", (128, NCH * width), BF16)
    B.sb("st_mu", (128, width), F32)
    B.sb("st_t1", (128, width), F32)
    B.sb("st_t2", (128, width), F32)
    B.sb("st_r", (128, width), F32)
    B.sb("st_mu_bf", (128, width), BF16)
    B.sb("st_r_bf", (128, width), BF16)
    B.sb("ln_t", (128, width), BF16)


def alloc_attn(B, T, lnw=0):
    B.sb("qh", (128, T), BF16)
    B.sb("kh", (128, T), BF16)
    B.sb("vtm", (128, T), BF16)
    if "sqbuf" not in B:
        B.sb("sqbuf", (128, max(T, NCH * lnw)), BF16)
    B.sb("b_nt", (128, 4), F32)
    B.sb("b_q", (128, 1), F32)
    B.sb("b_k", (128, 1), F32)
    B.sb("negB", (128, 1), F32)
    B.sb("attnT0", (128, 1024), BF16)
    B.sb("attnT1", (128, 1024), BF16)
    B.sb("lnd", (128, 512), F32)
    B.sb("rden", (128, 512), F32)
    B.sb("ctxfm", (128, NCH * T), BF16)
    B.ps("PS2", (128, 1024), F32)
    B.ps("PCTX", (128, 512), F32)
    B.ps("PDEN", (128, 512), F32)
    B.sb("c_sc", (128, 1), F32)



A_T = [512, 1024, 2048]
A_NCHUNK = [4, 2, 1]
A_INW = [1856, 1984, 2048]
A_COFF = [448, 960, 1984]


def build_A(level):
    nc = bass.Bass("TRN2", target_bir_lowering=False)
    es = ExitStack()
    T = A_T[level]
    n_chunks = A_NCHUNK[level]
    in_w = A_INW[level]
    scale = 1.0 / np.sqrt(128.0)

    x_in = nc.dram_tensor("x_in", [D, in_w], F32 if level == 0 else BF16,
                          kind="ExternalInput").ap()
    w_in_t = nc.dram_tensor("w_in_t", [D, 3 * D], F32, kind="ExternalInput").ap()
    b_in_p = nc.dram_tensor("b_in_p", [128, 24], F32, kind="ExternalInput").ap()
    b_in_f = nc.dram_tensor("b_in_f", [1, 3 * D], F32, kind="ExternalInput").ap()
    w_out_t = nc.dram_tensor("w_out_t", [D, D], F32, kind="ExternalInput").ap()
    b_out_p = nc.dram_tensor("b_out_p", [128, 8], F32, kind="ExternalInput").ap()
    if level == 2:
        x_own = nc.dram_tensor("x_own", [D, 2048], F32, kind="ExternalInput").ap()
        out_t = nc.dram_tensor("out", [D, 2048], F32, kind="ExternalOutput").ap()
    else:
        out_t = nc.dram_tensor("out", [D, 2048], BF16, kind="ExternalOutput").ap()

    with es:
        B = Buffers(nc, es)
        S = Sched(nc, es)
        alloc_common(B)
        lnw = 512 if level == 0 else 128
        alloc_attn(B, T, lnw if level in (0, 2) else 0)
        B.sb("w_in", (128, NCH * 3 * D), BF16)
        B.sb("w_out", (128, NCH * D), BF16)
        B.sb("b_in_p", (128, 24), F32)
        B.sb("b_in_f_bf", (1, 3 * D), BF16)
        B.sb("b_out_p", (128, 8), F32)
        B.sb("src", (128, NCH * in_w), BF16)
        B.sb("dstbuf", (128, NCH * 2048), BF16)
        if level in (0, 2):
            alloc_ln(B, lnw)
            B.sb("xg", (128, NCH * lnw), F32)
        if level == 2:
            B.sb("res_t", (128, lnw), F32)

        emit_init_consts(S, B, [("c_sc", scale)])
        emit_weight_load(S, B, w_in_t, B["w_in"], NCH, 3 * D, "w_in")
        emit_weight_load(S, B, w_out_t, B["w_out"], NCH, D, "w_out")
        S.dma([(B["b_in_p"][:], b_in_p), (B["b_out_p"][:], b_out_p)],
              writes=("b_in_p", "b_out_p"))
        emit_flat_bias_load(S, B, b_in_f, B["b_in_f_bf"], 3 * D, "b_in_f_bf")

        if level == 0:
            B.sb("xbf", (128, NCH * 512), BF16)
            c0 = 0
            while c0 < in_w:
                gw = min(512, in_w - c0)
                S.dma([(B["xg"][:, k * gw:(k + 1) * gw],
                        x_in[128 * k:128 * (k + 1), c0:c0 + gw]) for k in range(NCH)],
                      writes=("xg",))
                S.comp("vector", lambda e, gw=gw: e.tensor_copy(
                    out=B["xbf"][:, 0:NCH * gw], in_=B["xg"][:, 0:NCH * gw]),
                    reads=("xg",), writes=("xbf",))
                emit_ln_stats(S, B, lambda k, gw=gw: B["xbf"][:, k * gw:(k + 1) * gw],
                              ("xbf",), gw)
                emit_ln_apply_bf(S, B, lambda k, gw=gw: B["xbf"][:, k * gw:(k + 1) * gw],
                                 ("xbf",), gw,
                                 lambda k, c0=c0, gw=gw: B["src"][:, k * in_w + c0:
                                                                  k * in_w + c0 + gw],
                                 "src")
                c0 += gw
        else:
            S.dma([(B["src"][:, k * in_w:(k + 1) * in_w],
                    x_in[128 * k:128 * (k + 1), :]) for k in range(NCH)],
                  writes=("src",))

        for j in range(n_chunks):
            def dst_write(S_, B_, nt, tw, m, psum_ap, pk, j=j):
                S_.comp("scalar", lambda e, nt=nt, tw=tw, m=m, psum_ap=psum_ap, j=j:
                        e.add(out=B_["dstbuf"][:, m * 2048 + T * j + nt * 512:
                                               m * 2048 + T * j + nt * 512 + tw],
                              in_=psum_ap, add=B_["b_out_p"][:, m:m + 1]),
                        reads=(pk, "b_out_p"), writes=("dstbuf",))

            emit_mha(S, B, B["src"], "src", A_COFF[level] * j, T, dst_write,
                     scale=scale)

        if level < 2:
            S.dma([(out_t[128 * k:128 * (k + 1), :],
                    B["dstbuf"][:, k * 2048:(k + 1) * 2048]) for k in range(NCH)],
                  reads=("dstbuf",), writes=("out",))
        else:
            for g0 in range(0, 2048, lnw):
                S.dma([(B["xg"][:, k * lnw:(k + 1) * lnw],
                        x_own[128 * k:128 * (k + 1), g0:g0 + lnw]) for k in range(NCH)],
                      writes=("xg",))
                emit_ln_stats(S, B, lambda k, g0=g0: B["dstbuf"][:, k * 2048 + g0:
                                                                 k * 2048 + g0 + lnw],
                              ("dstbuf",), lnw)
                for k in range(NCH):
                    S.comp("vector", lambda e, k=k, g0=g0: e.tensor_tensor(
                        out=B["ln_t"][:, 0:lnw],
                        in0=B["dstbuf"][:, k * 2048 + g0:k * 2048 + g0 + lnw],
                        in1=B["st_mu_bf"][:, 0:lnw], op=ALU.subtract),
                        reads=("dstbuf", "st_mu_bf"), writes=("ln_t",))
                    S.comp("vector", lambda e, k=k: e.tensor_tensor(
                        out=B["res_t"][:, 0:lnw], in0=B["ln_t"][:, 0:lnw],
                        in1=B["st_r_bf"][:, 0:lnw], op=ALU.mult),
                        reads=("ln_t", "st_r_bf"), writes=("res_t",))
                    S.comp("vector", lambda e, k=k: e.tensor_tensor(
                        out=B["res_t"][:, 0:lnw], in0=B["res_t"][:, 0:lnw],
                        in1=B["xg"][:, k * lnw:(k + 1) * lnw], op=ALU.add),
                        reads=("res_t", "xg"), writes=("res_t",))
                    S.dma([(out_t[128 * k:128 * (k + 1), g0:g0 + lnw],
                            B["res_t"][:, 0:lnw])],
                          reads=("res_t",), writes=("out",))

        S.barrier_wait("sync", ("out",))
        S.finalize()
    return nc



def _fmT(a):
    return np.ascontiguousarray(np.asarray(a, dtype=np.float32).T)


def _fmT_bf(a):
    return np.ascontiguousarray(np.asarray(a).T)


def _bias_p(v):
    v = np.asarray(v, dtype=np.float32)
    return np.ascontiguousarray(v.reshape(-1, 128).T)


def _bias_f(v):
    return np.ascontiguousarray(np.asarray(v, dtype=np.float32).reshape(1, -1))


def _bias_p24(bin_):
    return _bias_p(bin_)


def run_A(level, x_fm_f32, x_fm_prev_bf, Win, bin_, Wout, bout):
    nc = build_A(level)
    w_in_t = _fmT(Win)
    w_out_t = _fmT(Wout)
    b_in_p = _bias_p24(bin_)
    b_in_f = _bias_f(bin_)
    b_out_p = _bias_p(bout)
    in_maps = []
    in_w = A_INW[level]
    for c in range(8):
        b, q = c // 4, c % 4
        m = {"w_in_t": w_in_t, "b_in_p": b_in_p, "b_in_f": b_in_f,
             "w_out_t": w_out_t, "b_out_p": b_out_p}
        if level == 0:
            s0 = 1792 * q
            m["x_in"] = np.ascontiguousarray(x_fm_f32[b][:, s0:s0 + in_w])
        elif level == 1:
            s0 = 1920 * q
            m["x_in"] = np.ascontiguousarray(x_fm_prev_bf[b][:, s0:s0 + in_w])
        else:
            s0 = 1984 * q
            m["x_in"] = np.ascontiguousarray(x_fm_prev_bf[b][:, s0:s0 + in_w])
            m["x_own"] = np.ascontiguousarray(x_fm_f32[b][:, 2048 * q:2048 * (q + 1)])
        in_maps.append(m)
    res = run_bass_kernel_spmd(nc, in_maps, core_ids=list(range(8)))
    odt = np.float32 if level == 2 else BF
    out = np.empty((2, D, 8192), dtype=odt)
    for c in range(8):
        b, q = c // 4, c % 4
        out[b][:, 2048 * q:2048 * (q + 1)] = np.asarray(res.results[c]["out"])
    return out





def emit_tail_group(S, B, chunk_ap_fn, src_keys, lnw, res_dram_ap, out_dram_ap, g0):
    S.dma([(B["xg"][:, k * lnw:(k + 1) * lnw],
            res_dram_ap[128 * k:128 * (k + 1), g0:g0 + lnw]) for k in range(NCH)],
          writes=("xg",))
    emit_ln_stats(S, B, chunk_ap_fn, src_keys, lnw)
    for k in range(NCH):
        S.comp("vector", lambda e, k=k: e.tensor_tensor(
            out=B["ln_t"][:, 0:lnw], in0=chunk_ap_fn(k),
            in1=B["st_mu_bf"][:, 0:lnw], op=ALU.subtract),
            reads=tuple(src_keys) + ("st_mu_bf",), writes=("ln_t",))
        S.comp("vector", lambda e, k=k: e.tensor_tensor(
            out=B["res_t"][:, 0:lnw], in0=B["ln_t"][:, 0:lnw],
            in1=B["st_r_bf"][:, 0:lnw], op=ALU.mult),
            reads=("ln_t", "st_r_bf"), writes=("res_t",))
        S.comp("vector", lambda e, k=k: e.tensor_tensor(
            out=B["res_t"][:, 0:lnw], in0=B["res_t"][:, 0:lnw],
            in1=B["xg"][:, k * lnw:(k + 1) * lnw], op=ALU.add),
            reads=("res_t", "xg"), writes=("res_t",))
        S.dma([(out_dram_ap[128 * k:128 * (k + 1), g0:g0 + lnw],
                B["res_t"][:, 0:lnw])],
              reads=("res_t",), writes=("out",))



B_TH = 1024
B_HH = [(0, 6), (6, 5)]


def build_B():
    nc = bass.Bass("TRN2", target_bir_lowering=False)
    es = ExitStack()
    H = 1408
    x1 = nc.dram_tensor("x1", [D, 2048], F32, kind="ExternalInput").ap()
    gw_t = nc.dram_tensor("gw_t", [D, 8], F32, kind="ExternalInput").ap()
    gb_b = nc.dram_tensor("gb_b", [128, 8], F32, kind="ExternalInput").ap()
    w1a = nc.dram_tensor("w1a", [9 * D, H], F32, kind="ExternalInput").ap()
    w3a = nc.dram_tensor("w3a", [9 * D, H], F32, kind="ExternalInput").ap()
    w2a = nc.dram_tensor("w2a", [9 * H, D], F32, kind="ExternalInput").ap()
    b1a = nc.dram_tensor("b1a", [9 * 128, 11], F32, kind="ExternalInput").ap()
    b3a = nc.dram_tensor("b3a", [9 * 128, 11], F32, kind="ExternalInput").ap()
    b2d = nc.dram_tensor("b2d", [8, D], F32, kind="ExternalInput").ap()
    sb2f = nc.dram_tensor("sb2f", [1, D], F32, kind="ExternalInput").ap()
    ident = nc.dram_tensor("identf", [128, 128], F32, kind="ExternalInput").ap()
    out_t = nc.dram_tensor("out", [D, 2048], F32, kind="ExternalOutput").ap()

    with es:
        B = Buffers(nc, es)
        S = Sched(nc, es)
        alloc_common(B)
        alloc_ln(B, 512)
        B.sb("y", (128, NCH * B_TH), BF16)
        B.sb("acc", (128, NCH * B_TH), F32)
        B.sb("accbf", (128, NCH * 512), BF16)
        B.sb("xg", (128, NCH * 512), F32)
        B.sb("res_t", (128, 512), F32)
        B.sb("ident_f32", (128, 128), F32)
        B.ps("PSTR", (128, 128), F32)
        for p in range(2):
            B.sb(f"w1h{p}", (128, NCH * 768), BF16)
            B.sb(f"w3h{p}", (128, NCH * 768), BF16)
            B.sb(f"w2h{p}", (128, 6 * D), BF16)
            B.sb(f"b1h{p}", (128, 11), F32)
            B.sb(f"b3h{p}", (128, 11), F32)
        B.sb("hbuf", (128, 6 * B_TH), BF16)
        B.sb("g1sb0", (128, 512), BF16)
        B.sb("g3sb0", (128, 512), BF16)
        B.sb("g1sb1", (128, 512), BF16)
        B.sb("g3sb1", (128, 512), BF16)
        B.ps("PS2", (128, 1024), F32)
        B.sb("gw_sb", (128, NCH * 8), F32)
        B.sb("gwcs", (128, 8), F32)
        B.sb("gb_sb", (128, 8), F32)
        B.sb("b2bf", (8, D), BF16)
        B.sb("b2st", (8, D), F32)
        B.sb("sb2bf", (1, D), BF16)
        B.sb("st_mu_tm", (128, 8), F32)
        B.sb("st_negmu_tm", (128, 8), F32)
        B.sb("st_r_tm", (128, 8), F32)
        B.sb("sc", (128, 8), F32)
        B.sb("sc2", (128, 8), F32)
        B.sb("m1", (128, 1), F32)
        B.sb("rw", (128, 8), F32)
        B.sb("rwT", (8, B_TH), BF16)
        B.sb("rwrow", (1, B_TH), BF16)
        B.sb("rw_b", (128, B_TH), BF16)

        emit_init_consts(S, B)
        S.dma([(B["gb_sb"][:], gb_b), (B["ident_f32"][:], ident),
               (B["b2st"][:], b2d)],
              writes=("gb_sb", "ident_f32", "b2st"))
        S.comp("vector", lambda e: e.tensor_copy(out=B["b2bf"][:], in_=B["b2st"][:]),
               reads=("b2st",), writes=("b2bf",))
        emit_flat_bias_load(S, B, sb2f, B["sb2bf"], D, "sb2bf")
        for k in range(NCH):
            S.dma([(B["gw_sb"][:, k * 8:(k + 1) * 8], gw_t[128 * k:128 * (k + 1), :])],
                  writes=("gw_sb",))
        B.sb("ones_f32", (128, 128), F32)
        S.comp("vector", lambda e: e.memset(B["ones_f32"][:], 1.0), writes=("ones_f32",))

        def mm_gwcs(e):
            last = None
            for k in range(NCH):
                last = e.matmul(B["PSBIG"][:, 0:8], B["ones_f32"][:],
                                B["gw_sb"][:, k * 8:(k + 1) * 8],
                                start=(k == 0), stop=(k == NCH - 1))
            return last

        S.comp("tensor", mm_gwcs, reads=("gw_sb", "ones_f32"), writes=("PSBIG_a",))
        S.comp("vector", lambda e: e.tensor_copy(out=B["gwcs"][:], in_=B["PSBIG"][:, 0:8]),
               reads=("PSBIG_a",), writes=("gwcs",))

        psum = B["PSBIG"]
        for th in range(2):
            t0 = th * B_TH
            for g in range(2):
                g0 = t0 + g * 512
                S.dma([(B["xg"][:, k * 512:(k + 1) * 512],
                        x1[128 * k:128 * (k + 1), g0:g0 + 512]) for k in range(NCH)],
                      writes=("xg",))
                S.comp("vector", lambda e: e.tensor_copy(
                    out=B["accbf"][:, 0:NCH * 512], in_=B["xg"][:, 0:NCH * 512]),
                    reads=("xg",), writes=("accbf",))
                emit_ln_stats(S, B, lambda k: B["accbf"][:, k * 512:(k + 1) * 512],
                              ("accbf",), 512)
                emit_stats_tm(S, B, 512, g * 4)
                emit_ln_apply_bf(S, B, lambda k: B["accbf"][:, k * 512:(k + 1) * 512],
                                 ("accbf",), 512,
                                 lambda k, g=g: B["y"][:, k * B_TH + g * 512:
                                                       k * B_TH + g * 512 + 512],
                                 "y")
                for tt in range(4):
                    gt = g * 4 + tt

                    def mm_gate(e, g=g, tt=tt):
                        last = None
                        for k in range(NCH):
                            last = e.matmul(
                                psum[:, 0:8],
                                B["xg"][:, k * 512 + tt * 128:k * 512 + tt * 128 + 128],
                                B["gw_sb"][:, k * 8:(k + 1) * 8],
                                start=(k == 0), stop=(k == NCH - 1))
                        return last

                    S.comp("tensor", mm_gate, reads=("xg", "gw_sb"), writes=("PSBIG_a",))
                    S.comp("vector", lambda e, gt=gt: e.tensor_scalar(
                        out=B["st_negmu_tm"][:, gt:gt + 1], in0=B["st_mu_tm"][:, gt:gt + 1],
                        scalar1=-1.0, scalar2=None, op0=ALU.mult),
                        reads=("st_mu_tm",), writes=("st_negmu_tm",))
                    S.comp("vector", lambda e, gt=gt: e.scalar_tensor_tensor(
                        out=B["sc"][:], in0=B["gwcs"][:],
                        scalar=B["st_negmu_tm"][:, gt:gt + 1], in1=psum[:, 0:8],
                        op0=ALU.mult, op1=ALU.add),
                        reads=("gwcs", "st_negmu_tm", "PSBIG_a"), writes=("sc",))
                    S.comp("vector", lambda e, gt=gt: e.tensor_scalar(
                        out=B["sc"][:], in0=B["sc"][:],
                        scalar1=B["st_r_tm"][:, gt:gt + 1], scalar2=None, op0=ALU.mult),
                        reads=("sc", "st_r_tm"), writes=("sc",))
                    S.comp("vector", lambda e: e.reduce_max(
                        out=B["m1"][:], in_=B["sc"][:], axis=AX.X, negate=True),
                        reads=("sc",), writes=("m1",))
                    S.comp("scalar", lambda e: e.activation(
                        out=B["sc2"][:], in_=B["sc"][:], func=AF.Exp,
                        bias=B["m1"][:], scale=1.0),
                        reads=("sc", "m1"), writes=("sc2",))
                    S.comp("vector", lambda e: e.reduce_sum(
                        out=B["m1"][:], in_=B["sc2"][:], axis=AX.X),
                        reads=("sc2",), writes=("m1",))
                    S.comp("vector", lambda e: e.reciprocal(
                        out=B["m1"][:], in_=B["m1"][:]),
                        reads=("m1",), writes=("m1",))
                    S.comp("vector", lambda e: e.tensor_scalar(
                        out=B["sc"][:], in0=B["sc2"][:], scalar1=B["m1"][:],
                        scalar2=None, op0=ALU.mult),
                        reads=("sc2", "m1"), writes=("sc",))
                    S.comp("vector", lambda e: e.tensor_tensor(
                        out=B["sc2"][:], in0=B["sc"][:], in1=B["gb_sb"][:], op=ALU.add),
                        reads=("sc", "gb_sb"), writes=("sc2",))
                    S.comp("vector", lambda e: e.reduce_max(
                        out=B["m1"][:], in_=B["sc2"][:], axis=AX.X),
                        reads=("sc2",), writes=("m1",))
                    S.comp("vector", lambda e: e.tensor_scalar(
                        out=B["rw"][:], in0=B["sc2"][:], scalar1=B["m1"][:],
                        scalar2=None, op0=ALU.is_ge),
                        reads=("sc2", "m1"), writes=("rw",))
                    S.comp("vector", lambda e: e.scalar_tensor_tensor(
                        out=B["rw"][:], in0=B["rw"][:], scalar=-1e30, in1=B["sc2"][:],
                        op0=ALU.mult, op1=ALU.add),
                        reads=("rw", "sc2"), writes=("rw",))
                    S.comp("vector", lambda e: e.reduce_max(
                        out=B["m1"][:], in_=B["rw"][:], axis=AX.X),
                        reads=("rw",), writes=("m1",))
                    S.comp("vector", lambda e: e.tensor_scalar(
                        out=B["rw"][:], in0=B["sc2"][:], scalar1=B["m1"][:],
                        scalar2=None, op0=ALU.is_ge),
                        reads=("sc2", "m1"), writes=("rw",))
                    S.comp("vector", lambda e: e.tensor_tensor(
                        out=B["rw"][:], in0=B["rw"][:], in1=B["sc"][:], op=ALU.mult),
                        reads=("rw", "sc"), writes=("rw",))
                    S.comp("tensor", lambda e: e.transpose(
                        B["PSTR"][0:8, 0:128], B["rw"][:], B["ident_f32"][:]),
                        reads=("rw", "ident_f32"), writes=("PSTR",))
                    S.comp("vector", lambda e, g=g, tt=tt: e.tensor_copy(
                        out=B["rwT"][0:8, (g * 4 + tt) * 128:(g * 4 + tt) * 128 + 128],
                        in_=B["PSTR"][0:8, 0:128]),
                        reads=("PSTR",), writes=("rwT",))
            def plan_wload(e_i, hh_i):
                p = (e_i * 2 + hh_i) % 2
                hc0, hnc = B_HH[hh_i]
                hw = hnc * 128
                w1k, w3k, w2k = f"w1h{p}", f"w3h{p}", f"w2h{p}"
                b1k, b3k = f"b1h{p}", f"b3h{p}"
                for k in range(NCH):
                    for p0 in range(0, hw, 512):
                        pw = min(512, hw - p0)
                        stg = "wstage0" if (k + p0) % 2 == 0 else "wstage1"
                        S.dma([(B[stg][:, 0:pw],
                                w1a[e_i * D + 128 * k:e_i * D + 128 * (k + 1),
                                    hc0 * 128 + p0:hc0 * 128 + p0 + pw])],
                              writes=(stg,))
                        S.comp("gpsimd", lambda e, k=k, p0=p0, pw=pw, stg=stg, w1k=w1k: e.tensor_copy(
                            out=B[w1k][:, k * 768 + p0:k * 768 + p0 + pw],
                            in_=B[stg][:, 0:pw]),
                            reads=(stg,), writes=(w1k,))
                        stg2 = "wstage0" if (k + p0) % 2 == 1 else "wstage1"
                        S.dma([(B[stg2][:, 0:pw],
                                w3a[e_i * D + 128 * k:e_i * D + 128 * (k + 1),
                                    hc0 * 128 + p0:hc0 * 128 + p0 + pw])],
                              writes=(stg2,))
                        S.comp("gpsimd", lambda e, k=k, p0=p0, pw=pw, stg2=stg2, w3k=w3k: e.tensor_copy(
                            out=B[w3k][:, k * 768 + p0:k * 768 + p0 + pw],
                            in_=B[stg2][:, 0:pw]),
                            reads=(stg2,), writes=(w3k,))
                for j in range(hnc):
                    for p0 in range(0, D, 512):
                        stg = "wstage0" if (j + p0 // 512) % 2 == 0 else "wstage1"
                        S.dma([(B[stg][:, 0:512],
                                w2a[e_i * H + (hc0 + j) * 128:e_i * H + (hc0 + j) * 128 + 128,
                                    p0:p0 + 512])],
                              writes=(stg,))
                        S.comp("gpsimd", lambda e, j=j, p0=p0, stg=stg, w2k=w2k: e.tensor_copy(
                            out=B[w2k][:, j * D + p0:j * D + p0 + 512],
                            in_=B[stg][:, 0:512]),
                            reads=(stg,), writes=(w2k,))
                S.dma([(B[b1k][:, 0:11], b1a[e_i * 128:(e_i + 1) * 128, :]),
                       (B[b3k][:, 0:11], b3a[e_i * 128:(e_i + 1) * 128, :])],
                      writes=(b1k, b3k))

            def plan_rwb(e_i):
                S.dma([(B["rwrow"][0:1, :], B["rwT"][e_i:e_i + 1, :])],
                      reads=("rwT",), writes=("rwrow",))
                for nt in range(2):
                    S.comp("tensor", lambda e, nt=nt: e.matmul(
                        psum[:, 0:512], B["ones_row"][0:1, 0:128],
                        B["rwrow"][0:1, nt * 512:nt * 512 + 512],
                        start=True, stop=True),
                        reads=("rwrow", "ones_row"), writes=("PSBIG_a",))
                    S.comp("vector", lambda e, nt=nt: e.tensor_copy(
                        out=B["rw_b"][:, nt * 512:nt * 512 + 512],
                        in_=psum[:, 0:512]),
                        reads=("PSBIG_a",), writes=("rw_b",))

            def plan_compute(e_i, hh_i):
                p = (e_i * 2 + hh_i) % 2
                hc0, hnc = B_HH[hh_i]
                w1k, w3k, w2k = f"w1h{p}", f"w3h{p}", f"w2h{p}"
                b1k, b3k = f"b1h{p}", f"b3h{p}"
                for mc in range(hnc):
                    for nt in range(2):
                        par = (mc * 2 + nt) % 2
                        pb = psum if par == 0 else B["PS2"]
                        pka = "PSBIG_a" if par == 0 else "PS2_a"
                        pkb = "PSBIG_b" if par == 0 else "PS2_b"
                        g1k = f"g1sb{par}"
                        g3k = f"g3sb{par}"

                        def mm1(e, mc=mc, nt=nt, w1k=w1k, pb=pb):
                            last = None
                            for k in range(NCH):
                                last = e.matmul(
                                    pb[:, 0:512],
                                    B[w1k][:, k * 768 + 128 * mc:k * 768 + 128 * mc + 128],
                                    B["y"][:, k * B_TH + nt * 512:k * B_TH + nt * 512 + 512],
                                    start=(k == 0), stop=(k == NCH - 1))
                            return last

                        S.comp("tensor", mm1, reads=(w1k, "y"), writes=(pka,))

                        def mm3(e, mc=mc, nt=nt, w3k=w3k, pb=pb):
                            last = None
                            for k in range(NCH):
                                last = e.matmul(
                                    pb[:, 512:1024],
                                    B[w3k][:, k * 768 + 128 * mc:k * 768 + 128 * mc + 128],
                                    B["y"][:, k * B_TH + nt * 512:k * B_TH + nt * 512 + 512],
                                    start=(k == 0), stop=(k == NCH - 1))
                            return last

                        S.comp("tensor", mm3, reads=(w3k, "y"), writes=(pkb,))
                        S.comp("scalar", lambda e, mc=mc, b1k=b1k, hc0=hc0, pb=pb, g1k=g1k: e.add(
                            out=B[g1k][:], in_=pb[:, 0:512],
                            add=B[b1k][:, hc0 + mc:hc0 + mc + 1]),
                            reads=(pka, b1k), writes=(g1k,))
                        S.comp("scalar", lambda e, mc=mc, b3k=b3k, hc0=hc0, pb=pb, g3k=g3k: e.add(
                            out=B[g3k][:], in_=pb[:, 512:1024],
                            add=B[b3k][:, hc0 + mc:hc0 + mc + 1]),
                            reads=(pkb, b3k), writes=(g3k,))
                        S.comp("vector", lambda e, g1k=g1k, g3k=g3k: e.tensor_tensor(
                            out=B[g1k][:], in0=B[g1k][:], in1=B[g3k][:],
                            op=ALU.mult),
                            reads=(g1k, g3k), writes=(g1k,))
                        if e_i < 8:
                            S.comp("scalar", lambda e, g1k=g1k, g3k=g3k: e.activation(
                                out=B[g3k][:], in_=B[g1k][:], func=AF.Silu,
                                bias=0.0, scale=1.0),
                                reads=(g1k,), writes=(g3k,))
                            S.comp("vector", lambda e, mc=mc, nt=nt, g3k=g3k: e.tensor_tensor(
                                out=B["hbuf"][:, mc * B_TH + nt * 512:mc * B_TH + nt * 512 + 512],
                                in0=B[g3k][:], in1=B["rw_b"][:, nt * 512:nt * 512 + 512],
                                op=ALU.mult),
                                reads=(g3k, "rw_b"), writes=("hbuf",))
                        else:
                            S.comp("scalar", lambda e, mc=mc, nt=nt, g1k=g1k: e.activation(
                                out=B["hbuf"][:, mc * B_TH + nt * 512:mc * B_TH + nt * 512 + 512],
                                in_=B[g1k][:], func=AF.Silu, bias=0.0, scale=1.0),
                                reads=(g1k,), writes=("hbuf",))
                for nt in range(2):
                    for m in range(NCH):
                        par = (nt * NCH + m) % 2
                        pb = psum if par == 0 else B["PS2"]
                        half = 0 if True else 0
                        pk = "PSBIG_a" if par == 0 else "PS2_a"

                        def mm2b(e, nt=nt, m=m, hnc=hnc, w2k=w2k, pb=pb,
                                 add_bias=(e_i == 8 and hh_i == 0)):
                            for j in range(hnc):
                                is_last = (j == hnc - 1) and not add_bias
                                e.matmul(
                                    pb[:, 0:512],
                                    B[w2k][:, j * D + 128 * m:j * D + 128 * m + 128],
                                    B["hbuf"][:, j * B_TH + nt * 512:j * B_TH + nt * 512 + 512],
                                    start=(j == 0), stop=is_last)
                            last = None
                            if add_bias:
                                e.matmul(pb[:, 0:512],
                                         B["b2bf"][0:8, 128 * m:128 * m + 128],
                                         B["rwT"][0:8, nt * 512:nt * 512 + 512],
                                         start=False, stop=False)
                                last = e.matmul(
                                    pb[:, 0:512],
                                    B["sb2bf"][0:1, 128 * m:128 * m + 128],
                                    B["ones_row"][0:1, 0:512],
                                    start=False, stop=True)
                            return last

                        S.comp("tensor", mm2b,
                               reads=(w2k, "hbuf", "b2bf", "rwT", "sb2bf", "ones_row"),
                               writes=(pk,))
                        if e_i == 0 and hh_i == 0:
                            S.comp("vector", lambda e, nt=nt, m=m, pb=pb: e.tensor_copy(
                                out=B["acc"][:, m * B_TH + nt * 512:m * B_TH + nt * 512 + 512],
                                in_=pb[:, 0:512]),
                                reads=(pk,), writes=("acc",))
                        else:
                            S.comp("vector", lambda e, nt=nt, m=m, pb=pb: e.tensor_tensor(
                                out=B["acc"][:, m * B_TH + nt * 512:m * B_TH + nt * 512 + 512],
                                in0=B["acc"][:, m * B_TH + nt * 512:m * B_TH + nt * 512 + 512],
                                in1=pb[:, 0:512], op=ALU.add),
                                reads=("acc", pk), writes=("acc",))

            steps = [(e_i, hh_i) for e_i in range(9) for hh_i in range(2)]
            plan_wload(*steps[0])
            for si, (e_i, hh_i) in enumerate(steps):
                if hh_i == 0 and e_i < 8:
                    plan_rwb(e_i)
                if si + 1 < len(steps):
                    plan_wload(*steps[si + 1])
                plan_compute(e_i, hh_i)
            for g in range(2):
                g0 = t0 + g * 512
                for k in range(NCH):
                    S.comp("vector", lambda e, k=k, g=g: e.tensor_copy(
                        out=B["accbf"][:, k * 512:(k + 1) * 512],
                        in_=B["acc"][:, k * B_TH + g * 512:k * B_TH + g * 512 + 512]),
                        reads=("acc",), writes=("accbf",))
                emit_tail_group(S, B, lambda k: B["accbf"][:, k * 512:(k + 1) * 512],
                                ("accbf",), 512, x1, out_t, g0)

        S.barrier_wait("sync", ("out",))
        S.finalize()
    return nc


def run_B(x1_fm, gate_w, gate_b, w1, b1, w3, b3, w2, b2, sw1, sb1, sw3, sb3,
          sw2, sb2):
    nc = build_B()
    H = 1408
    w1a = np.ascontiguousarray(np.concatenate(
        [np.asarray(w1, np.float32).reshape(8 * 1024, H),
         np.asarray(sw1, np.float32)], axis=0))
    w3a = np.ascontiguousarray(np.concatenate(
        [np.asarray(w3, np.float32).reshape(8 * 1024, H),
         np.asarray(sw3, np.float32)], axis=0))
    w2a = np.ascontiguousarray(np.concatenate(
        [np.asarray(w2, np.float32).reshape(8 * H, 1024),
         np.asarray(sw2, np.float32)], axis=0))
    b1a = np.ascontiguousarray(np.concatenate(
        [np.stack([_bias_p(np.asarray(b1, np.float32)[e]) for e in range(8)]),
         _bias_p(np.asarray(sb1, np.float32))[None]], axis=0).reshape(9 * 128, 11))
    b3a = np.ascontiguousarray(np.concatenate(
        [np.stack([_bias_p(np.asarray(b3, np.float32)[e]) for e in range(8)]),
         _bias_p(np.asarray(sb3, np.float32))[None]], axis=0).reshape(9 * 128, 11))
    m_common = {
        "gw_t": _fmT(gate_w), "gb_b": np.ascontiguousarray(
            np.broadcast_to(np.asarray(gate_b, np.float32)[None, :], (128, 8))),
        "w1a": w1a, "w3a": w3a, "w2a": w2a, "b1a": b1a, "b3a": b3a,
        "b2d": np.ascontiguousarray(np.asarray(b2, np.float32)),
        "sb2f": _bias_f(sb2), "identf": np.eye(128, dtype=np.float32),
    }
    in_maps = []
    for c in range(8):
        b, q = c // 4, c % 4
        m = dict(m_common)
        m["x1"] = np.ascontiguousarray(x1_fm[b][:, 2048 * q:2048 * (q + 1)])
        in_maps.append(m)
    res = run_bass_kernel_spmd(nc, in_maps, core_ids=list(range(8)))
    out = np.empty((2, D, 8192), dtype=np.float32)
    for c in range(8):
        b, q = c // 4, c % 4
        out[b][:, 2048 * q:2048 * (q + 1)] = np.asarray(res.results[c]["out"])
    return out



def build_C1():
    nc = bass.Bass("TRN2", target_bir_lowering=False)
    es = ExitStack()
    S_LEN = 8192
    x2 = nc.dram_tensor("x2", [D, S_LEN], F32, kind="ExternalInput").ap()
    wq = nc.dram_tensor("wq", [D, 256], F32, kind="ExternalInput").ap()
    wk = nc.dram_tensor("wk", [D, 256], F32, kind="ExternalInput").ap()
    wv = nc.dram_tensor("wv", [D, 256], F32, kind="ExternalInput").ap()
    thr = nc.dram_tensor("thr", [1, S_LEN], F32, kind="ExternalInput").ap()
    out_t = nc.dram_tensor("out", [256, S_LEN], BF16, kind="ExternalOutput").ap()

    with es:
        B = Buffers(nc, es)
        S = Sched(nc, es)
        alloc_common(B)
        alloc_ln(B, 512)
        B.sb("qfm", (128, 2 * S_LEN), BF16)
        B.sb("kfm", (128, 2 * S_LEN), BF16)
        B.sb("vtm", (128, 64 * 256), BF16)
        B.sb("xg", (128, NCH * 512), F32)
        B.sb("y2g", (128, NCH * 512), BF16)
        B.sb("wq_sb", (128, NCH * 256), BF16)
        B.sb("wk_sb", (128, NCH * 256), BF16)
        B.sb("wv_sb", (128, NCH * 256), BF16)
        B.sb("attnT0", (128, 1024), BF16)
        B.sb("attnT1", (128, 1024), BF16)
        B.sb("thr_pc", (1, 512), F32)
        B.sb("thrb", (128, 512), F32)
        B.sb("kcol", (128, 64), F32)
        B.sb("mask_sb", (128, 512), BF16)
        B.sb("ctxbuf0", (128, 512), BF16)
        B.sb("ctxbuf1", (128, 512), BF16)
        B.sb("lnd", (128, 512), F32)
        B.sb("rden", (128, 512), F32)
        B.sb("ones_f32", (128, 128), F32)
        B.sb("b_nt", (128, 16), F32)
        B.sb("bq4", (128, 4), F32)
        B.sb("bk4", (128, 4), F32)
        B.sb("negB4", (128, 4), F32)
        B.sb("c_sc", (128, 1), F32)
        B.ps("PS2", (128, 1024), F32)
        B.ps("PCTX", (128, 512), F32)
        B.ps("PDEN", (128, 512), F32)
        psum = B["PSBIG"]

        emit_init_consts(S, B, [("c_sc", 0.125)])
        S.comp("vector", lambda e: e.memset(B["ones_f32"][:], 1.0), writes=("ones_f32",))
        S.comp("gpsimd", lambda e: e.iota(B["kcol"][:], pattern=[[128, 64]],
                                          base=0, channel_multiplier=1,
                                          allow_small_or_imprecise_dtypes=True),
               writes=("kcol",))
        emit_weight_load(S, B, wq, B["wq_sb"], NCH, 256, "wq_sb")
        emit_weight_load(S, B, wk, B["wk_sb"], NCH, 256, "wk_sb")
        emit_weight_load(S, B, wv, B["wv_sb"], NCH, 256, "wv_sb")

        for g in range(16):
            g0 = g * 512
            S.dma([(B["xg"][:, k * 512:(k + 1) * 512],
                    x2[128 * k:128 * (k + 1), g0:g0 + 512]) for k in range(NCH)],
                  writes=("xg",))
            S.comp("vector", lambda e: e.tensor_copy(
                out=B["y2g"][:, 0:NCH * 512], in_=B["xg"][:, 0:NCH * 512]),
                reads=("xg",), writes=("y2g",))
            emit_ln_stats(S, B, lambda k: B["y2g"][:, k * 512:(k + 1) * 512],
                          ("y2g",), 512)
            emit_ln_apply_bf(S, B, lambda k: B["y2g"][:, k * 512:(k + 1) * 512],
                             ("y2g",), 512,
                             lambda k: B["y2g"][:, k * 512:(k + 1) * 512],
                             "y2g")
            for which, wkey in (("qfm", "wq_sb"), ("kfm", "wk_sb")):
                for mh in range(2):
                    half = (mh % 2) * 512
                    pk = "PSBIG_a" if mh % 2 == 0 else "PSBIG_b"

                    def mmqk(e, mh=mh, half=half, wkey=wkey):
                        last = None
                        for k in range(NCH):
                            last = e.matmul(
                                psum[:, half:half + 512],
                                B[wkey][:, k * 256 + 128 * mh:k * 256 + 128 * mh + 128],
                                B["y2g"][:, k * 512:(k + 1) * 512],
                                start=(k == 0), stop=(k == NCH - 1))
                        return last

                    S.comp("tensor", mmqk, reads=(wkey, "y2g"), writes=(pk,))
                    S.comp("scalar", lambda e, which=which, mh=mh, half=half, g0=g0: e.copy(
                        out=B[which][:, mh * 8192 + g0:mh * 8192 + g0 + 512],
                        in_=psum[:, half:half + 512]),
                        reads=(pk,), writes=(which,))
            for tt in range(4):
                half = (tt % 2) * 512
                pk = "PSBIG_a" if tt % 2 == 0 else "PSBIG_b"

                def mmv(e, tt=tt, half=half):
                    last = None
                    for k in range(NCH):
                        last = e.matmul(
                            psum[:, half:half + 256],
                            B["y2g"][:, k * 512 + tt * 128:k * 512 + tt * 128 + 128],
                            B["wv_sb"][:, k * 256:(k + 1) * 256],
                            start=(k == 0), stop=(k == NCH - 1))
                    return last

                S.comp("tensor", mmv, reads=("wv_sb", "y2g"), writes=(pk,))
                S.comp("scalar", lambda e, g=g, tt=tt, half=half: e.copy(
                    out=B["vtm"][:, (g * 4 + tt) * 256:(g * 4 + tt) * 256 + 256],
                    in_=psum[:, half:half + 256]),
                    reads=(pk,), writes=("vtm",))

        for h in range(4):
            hp, hl = h // 2, h % 2
            p0 = 64 * hl
            for which, bcol in (("qfm", "bq4"), ("kfm", "bk4")):
                for half_tok in range(2):
                    c0 = hp * 8192 + half_tok * 4096
                    S.comp("gpsimd", lambda e, which=which, p0=p0, c0=c0: e.tensor_mul(
                        out=B["sqbuf"][0:64, 0:4096],
                        in0=B[which][p0:p0 + 64, c0:c0 + 4096],
                        in1=B[which][p0:p0 + 64, c0:c0 + 4096]),
                        reads=(which,), writes=("sqbuf",))
                    for nt in range(8):
                        halfp = (nt % 2) * 512
                        pk = "PSBIG_a" if nt % 2 == 0 else "PSBIG_b"
                        S.comp("tensor", lambda e, nt=nt, halfp=halfp: e.matmul(
                            psum[:, halfp:halfp + 512], B["ones128"][0:64, 0:128],
                            B["sqbuf"][0:64, nt * 512:nt * 512 + 512],
                            start=True, stop=True),
                            reads=("sqbuf", "ones128"), writes=(pk,))
                        S.comp("vector", lambda e, nt=nt, halfp=halfp, half_tok=half_tok: e.reduce_max(
                            out=B["b_nt"][:, half_tok * 8 + nt:half_tok * 8 + nt + 1],
                            in_=psum[:, halfp:halfp + 512], axis=AX.X),
                            reads=(pk,), writes=("b_nt",))
                S.comp("vector", lambda e, bcol=bcol, h=h: e.reduce_max(
                    out=B[bcol][:, h:h + 1], in_=B["b_nt"][:, 0:16], axis=AX.X),
                    reads=("b_nt",), writes=(bcol,))
            S.comp("vector", lambda e, h=h: e.tensor_tensor(
                out=B["bq4"][:, h:h + 1], in0=B["bq4"][:, h:h + 1],
                in1=B["bk4"][:, h:h + 1], op=ALU.mult),
                reads=("bq4", "bk4"), writes=("bq4",))
            S.comp("scalar", lambda e, h=h: e.activation(
                out=B["bk4"][:, h:h + 1], in_=B["bq4"][:, h:h + 1], func=AF.Ln,
                bias=0.0, scale=1.0),
                reads=("bq4",), writes=("bk4",))
            S.comp("scalar", lambda e, h=h: e.activation(
                out=B["bq4"][:, h:h + 1], in_=B["bk4"][:, h:h + 1], func=AF.Exp,
                bias=0.0, scale=B["c_ph"][:]),
                reads=("bk4", "c_ph"), writes=("bq4",))
            S.comp("vector", lambda e, h=h: e.tensor_scalar(
                out=B["negB4"][:, h:h + 1], in0=B["bq4"][:, h:h + 1],
                scalar1=-0.125, scalar2=None, op0=ALU.mult),
                reads=("bq4",), writes=("negB4",))

        for hp in range(2):
            for qb in range(16):
                nkt = 4 * qb + 4
                S.dma([(B["thr_pc"][0:1, :], thr[0:1, qb * 512:qb * 512 + 512])],
                      writes=("thr_pc",))
                S.comp("tensor", lambda e: e.matmul(
                    psum[:, 0:512], B["ones_f32"][0:1, 0:128], B["thr_pc"][0:1, :],
                    start=True, stop=True),
                    reads=("thr_pc", "ones_f32"), writes=("PSBIG_a",))
                S.comp("vector", lambda e: e.tensor_copy(
                    out=B["thrb"][:], in_=psum[:, 0:512]),
                    reads=("PSBIG_a",), writes=("thrb",))
                for hl in range(2):
                    h = 2 * hp + hl
                    p0 = 64 * hl
                    prev = None
                    for pr in range(nkt // 2):
                        kts = [2 * pr, 2 * pr + 1]
                        pb, pk = (psum, ("PSBIG_a", "PSBIG_b")) if pr % 2 == 0 else                                  (B["PS2"], ("PS2_a", "PS2_b"))
                        ak = "attnT0" if pr % 2 == 0 else "attnT1"
                        at = B[ak]

                        def sc_body(e, kts=kts, qb=qb, hp=hp, p0=p0, pb=pb):
                            last = None
                            for j, kt in enumerate(kts):
                                last = e.matmul(
                                    pb[:, j * 512:(j + 1) * 512],
                                    B["kfm"][p0:p0 + 64, hp * 8192 + kt * 128:hp * 8192 + kt * 128 + 128],
                                    B["qfm"][p0:p0 + 64, hp * 8192 + qb * 512:hp * 8192 + qb * 512 + 512],
                                    start=True, stop=True)
                            return last

                        if prev is not None:
                            pkts, pak = prev

                            def mmsc2(e, sc_body=sc_body, pkts=pkts, pak=pak,
                                      p0=p0, h=h):
                                sc_body(e)
                                return c1_ctx_body(e, B, pkts, pak, p0, h, nkt)

                            S.comp("tensor", mmsc2,
                                   reads=("kfm", "qfm", pak, "vtm", "ones128"),
                                   writes=pk + ("PCTX", "PDEN"))
                        else:
                            S.comp("tensor", sc_body, reads=("kfm", "qfm"), writes=pk)
                        S.comp("scalar", lambda e, at=at, h=h, pb=pb: e.activation(
                            out=at[:, 0:1024], in_=pb[:, 0:1024], func=AF.Exp,
                            bias=B["negB4"][:, h:h + 1], scale=B["c_sc"][:]),
                            reads=pk + ("negB4", "c_sc"), writes=(ak,))
                        for j, kt in enumerate(kts):
                            if kt >= max(4 * qb, 1):
                                S.comp("vector", lambda e, kt=kt: e.tensor_scalar(
                                    out=B["mask_sb"][:], in0=B["thrb"][:],
                                    scalar1=B["kcol"][:, kt:kt + 1], scalar2=None,
                                    op0=ALU.is_ge),
                                    reads=("thrb", "kcol"), writes=("mask_sb",))
                                S.comp("vector", lambda e, at=at, j=j: e.tensor_tensor(
                                    out=at[:, j * 512:(j + 1) * 512],
                                    in0=at[:, j * 512:(j + 1) * 512], in1=B["mask_sb"][:],
                                    op=ALU.mult),
                                    reads=(ak, "mask_sb"), writes=(ak,))
                        prev = (kts, ak)
                    emit_c1_ctx(S, B, prev[0], prev[1], p0, h, nkt)
                cb = "ctxbuf0" if qb % 2 == 0 else "ctxbuf1"
                S.comp("scalar", lambda e: e.activation(
                    out=B["lnd"][:], in_=B["PDEN"][:, 0:512], func=AF.Ln,
                    bias=0.0, scale=1.0),
                    reads=("PDEN",), writes=("lnd",))
                S.comp("scalar", lambda e: e.activation(
                    out=B["rden"][:], in_=B["lnd"][:], func=AF.Exp,
                    bias=0.0, scale=B["c_neg1"][:]),
                    reads=("lnd", "c_neg1"), writes=("rden",))
                S.comp("vector", lambda e, cb=cb: e.tensor_tensor(
                    out=B[cb][:], in0=B["PCTX"][:, 0:512], in1=B["rden"][:],
                    op=ALU.mult),
                    reads=("PCTX", "rden"), writes=(cb,))
                S.dma([(out_t[hp * 128:(hp + 1) * 128, qb * 512:qb * 512 + 512],
                        B[cb][:])],
                      reads=(cb,), writes=("out",))

        S.barrier_wait("sync", ("out",))
        S.finalize()
    return nc


def c1_ctx_body(e, B, kts, ak, p0, h, nkt):
    at = B[ak]
    last = None
    for j, kt in enumerate(kts):
        e.matmul(B["PCTX"][p0:p0 + 64, 0:512],
                 B["vtm"][:, kt * 256 + 64 * h:kt * 256 + 64 * h + 64],
                 at[:, j * 512:(j + 1) * 512],
                 start=(kt == 0), stop=(kt == nkt - 1))
        last = e.matmul(B["PDEN"][p0:p0 + 64, 0:512],
                        B["ones128"][:, 0:64],
                        at[:, j * 512:(j + 1) * 512],
                        start=(kt == 0), stop=(kt == nkt - 1))
    return last


def emit_c1_ctx(S, B, kts, ak, p0, h, nkt):
    S.comp("tensor", lambda e: c1_ctx_body(e, B, kts, ak, p0, h, nkt),
           reads=(ak, "vtm", "ones128"), writes=("PCTX", "PDEN"))


def run_C1(x2_fm, w_q, w_k, w_v):
    nc = build_C1()
    wqT = _fmT(w_q)
    wkT = _fmT(w_k)
    wvT = _fmT(w_v)
    thr = np.maximum(np.arange(8192, dtype=np.float32), 195.0).reshape(1, 8192)
    in_maps = []
    for c in range(8):
        b, hq = c // 4, c % 4
        in_maps.append({
            "x2": np.ascontiguousarray(x2_fm[b]),
            "wq": np.ascontiguousarray(wqT[:, 256 * hq:256 * (hq + 1)]),
            "wk": np.ascontiguousarray(wkT[:, 256 * hq:256 * (hq + 1)]),
            "wv": np.ascontiguousarray(wvT[:, 256 * hq:256 * (hq + 1)]),
            "thr": thr,
        })
    res = run_bass_kernel_spmd(nc, in_maps, core_ids=list(range(8)))
    ctx = np.empty((2, D, 8192), dtype=BF)
    for c in range(8):
        b, hq = c // 4, c % 4
        ctx[b][256 * hq:256 * (hq + 1), :] = np.asarray(res.results[c]["out"])
    return ctx



def build_C2():
    nc = bass.Bass("TRN2", target_bir_lowering=False)
    es = ExitStack()
    ctx = nc.dram_tensor("ctx", [D, 2048], BF16, kind="ExternalInput").ap()
    x2 = nc.dram_tensor("x2", [D, 2048], F32, kind="ExternalInput").ap()
    wo = nc.dram_tensor("wo", [D, D], F32, kind="ExternalInput").ap()
    bo = nc.dram_tensor("bo", [128, 8], F32, kind="ExternalInput").ap()
    out_t = nc.dram_tensor("out", [D, 2048], F32, kind="ExternalOutput").ap()

    with es:
        B = Buffers(nc, es)
        S = Sched(nc, es)
        alloc_common(B)
        alloc_ln(B, 512)
        B.sb("ctx_sb", (128, NCH * 2048), BF16)
        B.sb("wo_sb", (128, NCH * D), BF16)
        B.sb("bo_sb", (128, 8), F32)
        B.sb("dstbuf", (128, NCH * 2048), BF16)
        B.sb("xg", (128, NCH * 512), F32)
        B.sb("res_t", (128, 512), F32)
        psum = B["PSBIG"]

        emit_init_consts(S, B)
        emit_weight_load(S, B, wo, B["wo_sb"], NCH, D, "wo_sb")
        S.dma([(B["bo_sb"][:], bo)], writes=("bo_sb",))
        S.dma([(B["ctx_sb"][:, k * 2048:(k + 1) * 2048],
                ctx[128 * k:128 * (k + 1), :]) for k in range(NCH)],
              writes=("ctx_sb",))
        for nt in range(4):
            for m in range(NCH):
                half = ((nt * NCH + m) % 2) * 512
                pk = "PSBIG_a" if (nt * NCH + m) % 2 == 0 else "PSBIG_b"

                def mmo(e, nt=nt, m=m, half=half):
                    last = None
                    for k in range(NCH):
                        last = e.matmul(
                            psum[:, half:half + 512],
                            B["wo_sb"][:, k * D + 128 * m:k * D + 128 * m + 128],
                            B["ctx_sb"][:, k * 2048 + nt * 512:k * 2048 + nt * 512 + 512],
                            start=(k == 0), stop=(k == NCH - 1))
                    return last

                S.comp("tensor", mmo, reads=("wo_sb", "ctx_sb"), writes=(pk,))
                S.comp("scalar", lambda e, nt=nt, m=m, half=half: e.add(
                    out=B["dstbuf"][:, m * 2048 + nt * 512:m * 2048 + nt * 512 + 512],
                    in_=psum[:, half:half + 512], add=B["bo_sb"][:, m:m + 1]),
                    reads=(pk, "bo_sb"), writes=("dstbuf",))
        for g in range(4):
            g0 = g * 512
            emit_tail_group(S, B, lambda k, g0=g0: B["dstbuf"][:, k * 2048 + g0:
                                                              k * 2048 + g0 + 512],
                            ("dstbuf",), 512, x2, out_t, g0)
        S.barrier_wait("sync", ("out",))
        S.finalize()
    return nc


def run_C2(ctx_fm_bf, x2_fm, w_o, b_o):
    nc = build_C2()
    woT = _fmT(w_o)
    bop = _bias_p(b_o)
    in_maps = []
    for c in range(8):
        b, q = c // 4, c % 4
        in_maps.append({
            "ctx": np.ascontiguousarray(ctx_fm_bf[b][:, 2048 * q:2048 * (q + 1)]),
            "x2": np.ascontiguousarray(x2_fm[b][:, 2048 * q:2048 * (q + 1)]),
            "wo": woT, "bo": bop,
        })
    res = run_bass_kernel_spmd(nc, in_maps, core_ids=list(range(8)))
    out = np.empty((2, D, 8192), dtype=np.float32)
    for c in range(8):
        b, q = c // 4, c % 4
        out[b][:, 2048 * q:2048 * (q + 1)] = np.asarray(res.results[c]["out"])
    return out



D_BLK = 6 * 128


def build_D():
    nc = bass.Bass("TRN2", target_bir_lowering=False)
    es = ExitStack()
    HID = 3072
    x3 = nc.dram_tensor("x3", [D, 2048], F32, kind="ExternalInput").ap()
    wg_t = nc.dram_tensor("wg_t", [D, HID], I32, kind="ExternalInput").ap()
    wu_t = nc.dram_tensor("wu_t", [D, HID], I32, kind="ExternalInput").ap()
    wd_t = nc.dram_tensor("wd_t", [HID, D], I32, kind="ExternalInput").ap()
    out_t = nc.dram_tensor("out", [D, 2048], F32, kind="ExternalOutput").ap()

    with es:
        B = Buffers(nc, es)
        S = Sched(nc, es)
        alloc_common(B)
        alloc_ln(B, 512)
        B.sb("y3", (128, NCH * 1024), BF16)
        B.sb("acc", (128, NCH * 1024), F32)
        B.sb("accbf", (128, NCH * 512), BF16)
        B.sb("xg", (128, NCH * 512), F32)
        B.sb("res_t", (128, 512), F32)
        B.sb("istage0", (128, 512), I32)
        B.sb("istage1", (128, 512), I32)
        for p in range(2):
            B.sb(f"wgb{p}", (128, NCH * 768), BF16)
            B.sb(f"wub{p}", (128, NCH * 768), BF16)
            B.sb(f"wdb{p}", (128, 6 * D), BF16)
        B.sb("hbuf", (128, 6 * 1024), BF16)
        B.sb("g1sb", (128, 512), BF16)
        B.sb("g3sb", (128, 512), BF16)
        B.sb("c127", (128, 1), F32)
        psum = B["PSBIG"]

        emit_init_consts(S, B, [("c127", 1.0 / 127.0)])

        for th in range(2):
            t0 = th * 1024
            for g in range(2):
                g0 = t0 + g * 512
                S.dma([(B["xg"][:, k * 512:(k + 1) * 512],
                        x3[128 * k:128 * (k + 1), g0:g0 + 512]) for k in range(NCH)],
                      writes=("xg",))
                S.comp("vector", lambda e: e.tensor_copy(
                    out=B["accbf"][:, 0:NCH * 512], in_=B["xg"][:, 0:NCH * 512]),
                    reads=("xg",), writes=("accbf",))
                emit_ln_stats(S, B, lambda k: B["accbf"][:, k * 512:(k + 1) * 512],
                              ("accbf",), 512)
                emit_ln_apply_bf(S, B, lambda k: B["accbf"][:, k * 512:(k + 1) * 512],
                                 ("accbf",), 512,
                                 lambda k, g=g: B["y3"][:, k * 1024 + g * 512:
                                                        k * 1024 + g * 512 + 512],
                                 "y3")
            for blk in range(4):
                p = blk % 2
                wgk, wuk, wdk = f"wgb{p}", f"wub{p}", f"wdb{p}"
                c0 = blk * 768
                for k in range(NCH):
                    for p0 in (0, 512):
                        pw = 512 if p0 == 0 else 256
                        stg = "istage0" if (k + p0 // 512) % 2 == 0 else "istage1"
                        S.dma([(B[stg][:, 0:pw],
                                wg_t[128 * k:128 * (k + 1), c0 + p0:c0 + p0 + pw])],
                              writes=(stg,))
                        S.comp("gpsimd", lambda e, k=k, p0=p0, pw=pw, stg=stg, wgk=wgk: e.tensor_copy(
                            out=B[wgk][:, k * 768 + p0:k * 768 + p0 + pw],
                            in_=B[stg][:, 0:pw]),
                            reads=(stg,), writes=(wgk,))
                        stg2 = "istage0" if (k + p0 // 512) % 2 == 1 else "istage1"
                        S.dma([(B[stg2][:, 0:pw],
                                wu_t[128 * k:128 * (k + 1), c0 + p0:c0 + p0 + pw])],
                              writes=(stg2,))
                        S.comp("gpsimd", lambda e, k=k, p0=p0, pw=pw, stg2=stg2, wuk=wuk: e.tensor_copy(
                            out=B[wuk][:, k * 768 + p0:k * 768 + p0 + pw],
                            in_=B[stg2][:, 0:pw]),
                            reads=(stg2,), writes=(wuk,))
                for j in range(6):
                    for p0 in (0, 512):
                        stg = "istage0" if (j + p0 // 512) % 2 == 0 else "istage1"
                        S.dma([(B[stg][:, 0:512],
                                wd_t[(blk * 6 + j) * 128:(blk * 6 + j) * 128 + 128,
                                     p0:p0 + 512])],
                              writes=(stg,))
                        S.comp("gpsimd", lambda e, j=j, p0=p0, stg=stg, wdk=wdk: e.tensor_copy(
                            out=B[wdk][:, j * D + p0:j * D + p0 + 512],
                            in_=B[stg][:, 0:512]),
                            reads=(stg,), writes=(wdk,))
                for mc in range(6):
                    for nt in range(2):
                        def mmg(e, mc=mc, nt=nt, wgk=wgk):
                            last = None
                            for k in range(NCH):
                                last = e.matmul(
                                    psum[:, 0:512],
                                    B[wgk][:, k * 768 + 128 * mc:k * 768 + 128 * mc + 128],
                                    B["y3"][:, k * 1024 + nt * 512:k * 1024 + nt * 512 + 512],
                                    start=(k == 0), stop=(k == NCH - 1))
                            return last

                        S.comp("tensor", mmg, reads=(wgk, "y3"), writes=("PSBIG_a",))
                        S.comp("scalar", lambda e: e.activation(
                            out=B["g1sb"][:], in_=psum[:, 0:512], func=AF.Silu,
                            bias=0.0, scale=B["c127"][:]),
                            reads=("PSBIG_a", "c127"), writes=("g1sb",))

                        def mmu(e, mc=mc, nt=nt, wuk=wuk):
                            last = None
                            for k in range(NCH):
                                last = e.matmul(
                                    psum[:, 512:1024],
                                    B[wuk][:, k * 768 + 128 * mc:k * 768 + 128 * mc + 128],
                                    B["y3"][:, k * 1024 + nt * 512:k * 1024 + nt * 512 + 512],
                                    start=(k == 0), stop=(k == NCH - 1))
                            return last

                        S.comp("tensor", mmu, reads=(wuk, "y3"), writes=("PSBIG_b",))
                        S.comp("scalar", lambda e: e.copy(
                            out=B["g3sb"][:], in_=psum[:, 512:1024]),
                            reads=("PSBIG_b",), writes=("g3sb",))
                        S.comp("vector", lambda e, mc=mc, nt=nt: e.tensor_tensor(
                            out=B["hbuf"][:, mc * 1024 + nt * 512:mc * 1024 + nt * 512 + 512],
                            in0=B["g1sb"][:], in1=B["g3sb"][:], op=ALU.mult),
                            reads=("g1sb", "g3sb"), writes=("hbuf",))
                for nt in range(2):
                    for m in range(NCH):
                        half = ((nt * NCH + m) % 2) * 512
                        pk = "PSBIG_a" if (nt * NCH + m) % 2 == 0 else "PSBIG_b"

                        def mmd(e, nt=nt, m=m, half=half, wdk=wdk):
                            last = None
                            for j in range(6):
                                last = e.matmul(
                                    psum[:, half:half + 512],
                                    B[wdk][:, j * D + 128 * m:j * D + 128 * m + 128],
                                    B["hbuf"][:, j * 1024 + nt * 512:j * 1024 + nt * 512 + 512],
                                    start=(j == 0), stop=(j == 5))
                            return last

                        S.comp("tensor", mmd, reads=(wdk, "hbuf"), writes=(pk,))
                        if blk == 0:
                            S.comp("vector", lambda e, nt=nt, m=m, half=half: e.tensor_scalar(
                                out=B["acc"][:, m * 1024 + nt * 512:m * 1024 + nt * 512 + 512],
                                in0=psum[:, half:half + 512],
                                scalar1=1.0 / (127.0 * 127.0), scalar2=None, op0=ALU.mult),
                                reads=(pk,), writes=("acc",))
                        else:
                            S.comp("vector", lambda e, nt=nt, m=m, half=half: e.scalar_tensor_tensor(
                                out=B["acc"][:, m * 1024 + nt * 512:m * 1024 + nt * 512 + 512],
                                in0=psum[:, half:half + 512], scalar=1.0 / (127.0 * 127.0),
                                in1=B["acc"][:, m * 1024 + nt * 512:m * 1024 + nt * 512 + 512],
                                op0=ALU.mult, op1=ALU.add),
                                reads=(pk, "acc"), writes=("acc",))
            for g in range(2):
                g0 = t0 + g * 512
                for k in range(NCH):
                    S.comp("vector", lambda e, k=k, g=g: e.tensor_copy(
                        out=B["accbf"][:, k * 512:(k + 1) * 512],
                        in_=B["acc"][:, k * 1024 + g * 512:k * 1024 + g * 512 + 512]),
                        reads=("acc",), writes=("accbf",))
                emit_tail_group(S, B, lambda k: B["accbf"][:, k * 512:(k + 1) * 512],
                                ("accbf",), 512, x3, out_t, g0)

        S.barrier_wait("sync", ("out",))
        S.finalize()
    return nc


def run_D(x3_fm, w_gate, w_up, w_down):
    nc = build_D()
    wg_t = np.ascontiguousarray(np.asarray(w_gate).T)
    wu_t = np.ascontiguousarray(np.asarray(w_up).T)
    wd_t = np.ascontiguousarray(np.asarray(w_down).T)
    in_maps = []
    for c in range(8):
        b, q = c // 4, c % 4
        in_maps.append({
            "x3": np.ascontiguousarray(x3_fm[b][:, 2048 * q:2048 * (q + 1)]),
            "wg_t": wg_t, "wu_t": wu_t, "wd_t": wd_t,
        })
    res = run_bass_kernel_spmd(nc, in_maps, core_ids=list(range(8)))
    out = np.empty((2, D, 8192), dtype=np.float32)
    for c in range(8):
        b, q = c // 4, c % 4
        out[b][:, 2048 * q:2048 * (q + 1)] = np.asarray(res.results[c]["out"])
    return out



def kernel(**inputs):
    g = lambda n: inputs[n]
    x = np.asarray(g("L0_attn_x"), np.float32)
    x_fm = np.ascontiguousarray(x.transpose(0, 2, 1))

    l0 = run_A(0, x_fm, None, g("L0_attn_Win0"), g("L0_attn_bin0"),
               g("L0_attn_Wout0"), g("L0_attn_bout0"))
    l1 = run_A(1, x_fm, l0, g("L0_attn_Win1"), g("L0_attn_bin1"),
               g("L0_attn_Wout1"), g("L0_attn_bout1"))
    x1 = run_A(2, x_fm, l1, g("L0_attn_Win2"), g("L0_attn_bin2"),
               g("L0_attn_Wout2"), g("L0_attn_bout2"))
    x2 = run_B(x1, g("L0_ffn_gate_w"), g("L0_ffn_gate_b"), g("L0_ffn_w1"),
               g("L0_ffn_b1"), g("L0_ffn_w3"), g("L0_ffn_b3"), g("L0_ffn_w2"),
               g("L0_ffn_b2"), g("L0_ffn_sw1"), g("L0_ffn_sb1"), g("L0_ffn_sw3"),
               g("L0_ffn_sb3"), g("L0_ffn_sw2"), g("L0_ffn_sb2"))
    ctx = run_C1(x2, g("L1_attn_w_q"), g("L1_attn_w_k"), g("L1_attn_w_v"))
    x3 = run_C2(ctx, x2, g("L1_attn_w_o"), g("L1_attn_b_o"))
    y = run_D(x3, g("L1_ffn_w_gate"), g("L1_ffn_w_up"), g("L1_ffn_w_down"))
    return np.ascontiguousarray(y.transpose(0, 2, 1))


# version 06
# speedup vs baseline: 1.3655x; 1.2682x over previous
import numpy as np
import ml_dtypes
from contextlib import ExitStack

import concourse.bass as bass
import concourse.mybir as mybir
from concourse.bass_utils import run_bass_kernel_spmd

F32 = mybir.dt.float32
BF16 = mybir.dt.bfloat16
I32 = mybir.dt.int32
AF = mybir.ActivationFunctionType
ALU = mybir.AluOpType
AX = mybir.AxisListType
BF = ml_dtypes.bfloat16

ENGS = ("sync", "tensor", "vector", "scalar", "gpsimd")
SELF_SER = ("vector", "scalar", "gpsimd")

D = 1024
NCH = 8



class Op:
    __slots__ = ("fn", "waits", "inc")

    def __init__(self, fn, waits, inc):
        self.fn = fn
        self.waits = waits
        self.inc = inc


class Sched:

    N_DMA_POOL = 8

    def __init__(self, nc, es):
        self.nc = nc
        self.ops = {e: [] for e in ENGS}
        self.val = {e: 0 for e in ENGS}
        self.seq = {e: es.enter_context(nc.semaphore("seq_" + e)) for e in ENGS}
        self.dpool = [es.enter_context(nc.semaphore(f"dpool{j}"))
                      for j in range(self.N_DMA_POOL)]
        self.dpool_tot = [0] * self.N_DMA_POOL
        self.dpool_idx = 0
        self.last_w = {}
        self.readers = {}
        self.waited = {e: {} for e in ENGS}

    def _sem_of(self, name):
        if name.startswith("@d"):
            return self.dpool[int(name[2:])]
        return self.seq[name]

    def _add_wait(self, eng, dep, waits):
        deng, dval = dep
        if deng == eng:
            return
        if self.waited[eng].get(deng, 0) >= dval:
            return
        self.waited[eng][deng] = dval
        waits.append((self._sem_of(deng), dval))

    def _book(self, eng, fn, reads, writes, inc, completion=None, pre_waits=()):
        waits = list(pre_waits)
        if eng in SELF_SER and self.val[eng] > 0:
            waits.append((self.seq[eng], self.val[eng]))
        for r in reads:
            lw = self.last_w.get(r)
            if lw is not None:
                self._add_wait(eng, lw, waits)
        for w in writes:
            lw = self.last_w.get(w)
            if lw is not None:
                self._add_wait(eng, lw, waits)
            for reng, rval in self.readers.get(w, {}).items():
                self._add_wait(eng, (reng, rval), waits)
        self.val[eng] += inc
        comp = completion if completion is not None else (eng, self.val[eng])
        for r in reads:
            self.readers.setdefault(r, {})[comp[0]] = comp[1]
        for w in writes:
            self.last_w[w] = comp
            self.readers[w] = {}
        self.ops[eng].append(Op(fn, waits, inc))

    def dma(self, transfers, reads=(), writes=(), queue="sync"):
        j = self.dpool_idx
        self.dpool_idx = (j + 1) % self.N_DMA_POOL
        sem = self.dpool[j]
        pre = []
        if self.dpool_tot[j] > 0:
            self._add_wait(queue, (f"@d{j}", self.dpool_tot[j]), pre)
        self.dpool_tot[j] += 16 * len(transfers)
        tot = self.dpool_tot[j]

        def fn(eng):
            for dst, src in transfers:
                eng.dma_start(out=dst, in_=src).then_inc(sem, 16)

        self._book(queue, fn, reads, writes, 0,
                   completion=(f"@d{j}", tot), pre_waits=pre)

    def comp(self, eng, fn, reads=(), writes=()):
        sem = self.seq[eng]

        def wrapped(e):
            last = fn(e)
            last.then_inc(sem, 1)

        self._book(eng, wrapped, reads, writes, 1)

    def barrier_wait(self, eng, keys):
        waits = []
        for r in keys:
            lw = self.last_w.get(r)
            if lw is not None:
                self._add_wait(eng, lw, waits)
        if eng in SELF_SER and self.val[eng] > 0:
            waits.append((self.seq[eng], self.val[eng]))
        self.ops[eng].append(Op(None, waits, 0))

    def finalize(self):
        nc = self.nc
        with nc.Block() as block:
            for e in ENGS:
                ops = self.ops[e]
                if not ops:
                    continue

                def runner(eng, ops=ops):
                    for op in ops:
                        for sem, v in op.waits:
                            eng.wait_ge(sem, v)
                        if op.fn is not None:
                            op.fn(eng)

                getattr(block, e)(runner)


class Buffers:
    def __init__(self, nc, es):
        self.nc = nc
        self.es = es
        self.t = {}

    def sb(self, name, shape, dtype):
        h = self.es.enter_context(self.nc.sbuf_tensor("sb_" + name, list(shape), dtype))
        self.t[name] = h
        return h

    def ps(self, name, shape, dtype=F32):
        h = self.es.enter_context(self.nc.psum_tensor("ps_" + name, list(shape), dtype))
        self.t[name] = h
        return h

    def __getitem__(self, name):
        return self.t[name]

    def __contains__(self, name):
        return name in self.t



def emit_weight_load(S, B, dram_ap, sb_bf, n_chunks, width, tag):
    piece = 512
    idx = 0
    for k in range(n_chunks):
        for p0 in range(0, width, piece):
            pw = min(piece, width - p0)
            stg = "wstage0" if idx % 2 == 0 else "wstage1"
            idx += 1
            S.dma([(B[stg][:, 0:pw], dram_ap[128 * k:128 * (k + 1), p0:p0 + pw])],
                  writes=(stg,))
            S.comp("gpsimd",
                   lambda e, k=k, p0=p0, pw=pw, stg=stg: e.tensor_copy(
                       out=sb_bf[:, width * k + p0:width * k + p0 + pw],
                       in_=B[stg][:, 0:pw]),
                   reads=(stg,), writes=(tag,))


def emit_flat_bias_load(S, B, dram_ap, sb_bf, width, tag):
    piece = 512
    idx = 0
    for p0 in range(0, width, piece):
        pw = min(piece, width - p0)
        stg = "wstage0" if idx % 2 == 0 else "wstage1"
        idx += 1
        S.dma([(B[stg][0:1, 0:pw], dram_ap[0:1, p0:p0 + pw])], writes=(stg,))
        S.comp("gpsimd", lambda e, p0=p0, pw=pw, stg=stg: e.tensor_copy(
            out=sb_bf[0:1, p0:p0 + pw], in_=B[stg][0:1, 0:pw]),
            reads=(stg,), writes=(tag,))


def emit_ln_stats(S, B, chunk_ap_fn, src_keys, width):
    psum = B["PSBIG"]

    def mm_sum(e):
        last = None
        for k in range(NCH):
            last = e.matmul(psum[:, 0:width], B["ones128"][:, 0:128],
                            chunk_ap_fn(k), start=(k == 0), stop=(k == NCH - 1))
        return last

    S.comp("tensor", mm_sum, reads=tuple(src_keys) + ("ones128",), writes=("PSBIG_a",))

    def sq(e):
        last = None
        for k in range(NCH):
            last = e.tensor_mul(out=B["sqbuf"][:, k * width:(k + 1) * width],
                                in0=chunk_ap_fn(k), in1=chunk_ap_fn(k))
        return last

    S.comp("gpsimd", sq, reads=tuple(src_keys), writes=("sqbuf",))

    def mm_sq(e):
        last = None
        for k in range(NCH):
            last = e.matmul(psum[:, 512:512 + width], B["ones128"][:, 0:128],
                            B["sqbuf"][:, k * width:(k + 1) * width],
                            start=(k == 0), stop=(k == NCH - 1))
        return last

    S.comp("tensor", mm_sq, reads=("sqbuf", "ones128"), writes=("PSBIG_b",))
    S.comp("vector", lambda e: e.tensor_scalar(out=B["st_mu"][:, 0:width],
                                               in0=psum[:, 0:width],
                                               scalar1=1.0 / D, scalar2=None, op0=ALU.mult),
           reads=("PSBIG_a",), writes=("st_mu",))
    S.comp("vector", lambda e: e.tensor_scalar(out=B["st_t1"][:, 0:width],
                                               in0=psum[:, 512:512 + width],
                                               scalar1=1.0 / D, scalar2=None, op0=ALU.mult),
           reads=("PSBIG_b",), writes=("st_t1",))
    S.comp("vector", lambda e: e.tensor_tensor(out=B["st_t2"][:, 0:width],
                                               in0=B["st_mu"][:, 0:width],
                                               in1=B["st_mu"][:, 0:width], op=ALU.mult),
           reads=("st_mu",), writes=("st_t2",))
    S.comp("vector", lambda e: e.tensor_tensor(out=B["st_t1"][:, 0:width],
                                               in0=B["st_t1"][:, 0:width],
                                               in1=B["st_t2"][:, 0:width], op=ALU.subtract),
           reads=("st_t1", "st_t2"), writes=("st_t1",))
    S.comp("scalar", lambda e: e.activation(out=B["st_t2"][:, 0:width],
                                            in_=B["st_t1"][:, 0:width],
                                            func=AF.Ln, bias=B["c_eps"][:], scale=1.0),
           reads=("st_t1", "c_eps"), writes=("st_t2",))
    S.comp("scalar", lambda e: e.activation(out=B["st_r"][:, 0:width],
                                            in_=B["st_t2"][:, 0:width],
                                            func=AF.Exp, bias=0.0, scale=B["c_nh"][:]),
           reads=("st_t2", "c_nh"), writes=("st_r",))
    S.comp("vector", lambda e: e.tensor_copy(out=B["st_mu_bf"][:, 0:width],
                                             in_=B["st_mu"][:, 0:width]),
           reads=("st_mu",), writes=("st_mu_bf",))
    S.comp("vector", lambda e: e.tensor_copy(out=B["st_r_bf"][:, 0:width],
                                             in_=B["st_r"][:, 0:width]),
           reads=("st_r",), writes=("st_r_bf",))


def emit_stats_tm(S, B, width, col0):
    ntt = width // 128
    for tt in range(ntt):
        S.comp("tensor", lambda e, tt=tt: e.transpose(
            B["PSTR"][:], B["st_mu"][:, tt * 128:(tt + 1) * 128], B["ident_f32"][:]),
            reads=("st_mu", "ident_f32"), writes=("PSTR",))
        S.comp("vector", lambda e, tt=tt: e.tensor_copy(
            out=B["st_mu_tm"][:, col0 + tt:col0 + tt + 1], in_=B["PSTR"][:, 0:1]),
            reads=("PSTR",), writes=("st_mu_tm",))
        S.comp("tensor", lambda e, tt=tt: e.transpose(
            B["PSTR"][:], B["st_r"][:, tt * 128:(tt + 1) * 128], B["ident_f32"][:]),
            reads=("st_r", "ident_f32", "st_mu_tm"), writes=("PSTR",))
        S.comp("vector", lambda e, tt=tt: e.tensor_copy(
            out=B["st_r_tm"][:, col0 + tt:col0 + tt + 1], in_=B["PSTR"][:, 0:1]),
            reads=("PSTR",), writes=("st_r_tm",))


def emit_ln_apply_bf(S, B, chunk_ap_fn, src_keys, width, dst_ap_fn, dst_key):
    for k in range(NCH):
        S.comp("vector", lambda e, k=k: e.tensor_tensor(
            out=B["ln_t"][:, 0:width], in0=chunk_ap_fn(k),
            in1=B["st_mu_bf"][:, 0:width], op=ALU.subtract),
            reads=tuple(src_keys) + ("st_mu_bf",), writes=("ln_t",))
        S.comp("vector", lambda e, k=k: e.tensor_tensor(
            out=dst_ap_fn(k), in0=B["ln_t"][:, 0:width],
            in1=B["st_r_bf"][:, 0:width], op=ALU.mult),
            reads=("ln_t", "st_r_bf"), writes=(dst_key,))


def emit_mha(S, B, src_bf, src_key, src_col0, T, dst_write_fn,
             w_key="w_in", wo_key="w_out", scale=1.0 / 32.0, cscale_key="c_sc",
             n_heads=8):
    src_stride = src_bf.shape[1] // NCH
    n_nt = (T + 511) // 512
    ntt = T // 128
    psum = B["PSBIG"]
    w_in = B["w_in"]
    w_out = B["w_out"]
    ctx_stride = B["ctxfm"].shape[1] // n_heads

    for h in range(n_heads):
        for which, off, bias_col in (("qh", 128 * h, h), ("kh", D + 128 * h, 8 + h)):
            for nt in range(n_nt):
                tw = min(512, T - nt * 512)
                half = (nt % 2) * 512
                pk = "PSBIG_a" if nt % 2 == 0 else "PSBIG_b"

                def mmq(e, off=off, nt=nt, tw=tw, half=half):
                    last = None
                    for k in range(NCH):
                        last = e.matmul(
                            psum[:, half:half + tw],
                            w_in[:, k * 3 * D + off:k * 3 * D + off + 128],
                            src_bf[:, k * src_stride + src_col0 + nt * 512:
                                   k * src_stride + src_col0 + nt * 512 + tw],
                            start=(k == 0), stop=(k == NCH - 1))
                    return last

                S.comp("tensor", mmq, reads=(w_key, src_key), writes=(pk,))
                S.comp("scalar", lambda e, which=which, nt=nt, tw=tw, half=half,
                       bias_col=bias_col: e.add(
                           out=B[which][:, nt * 512:nt * 512 + tw],
                           in_=psum[:, half:half + tw],
                           add=B["b_in_p"][:, bias_col:bias_col + 1]),
                       reads=(pk, "b_in_p"), writes=(which,))
        for tg in range(0, ntt, 4):
            tts = [t for t in range(tg, min(tg + 4, ntt))]
            par = (tg // 4) % 2
            pb = psum if par == 0 else B["PS2"]
            pk = ("PSBIG_a", "PSBIG_b") if par == 0 else ("PS2_a", "PS2_b")

            def mmv(e, tts=tts, h=h, pb=pb):
                last = None
                for i, tt in enumerate(tts):
                    for k in range(NCH):
                        e.matmul(
                            pb[:, i * 128:i * 128 + 128],
                            src_bf[:, k * src_stride + src_col0 + tt * 128:
                                   k * src_stride + src_col0 + tt * 128 + 128],
                            w_in[:, k * 3 * D + 2 * D + 128 * h:
                                 k * 3 * D + 2 * D + 128 * h + 128],
                            start=(k == 0), stop=False)
                    last = e.matmul(pb[:, i * 128:i * 128 + 128],
                                    B["ones_row"][0:1, 0:128],
                                    B["b_in_f_bf"][0:1, 2 * D + 128 * h:2 * D + 128 * h + 128],
                                    start=False, stop=True)
                return last

            S.comp("tensor", mmv, reads=(w_key, src_key, "ones_row", "b_in_f_bf"),
                   writes=pk)
            S.comp("scalar", lambda e, tg=tg, tts=tts, pb=pb: e.copy(
                out=B["vtm"][:, tg * 128:tg * 128 + len(tts) * 128],
                in_=pb[:, 0:len(tts) * 128]),
                reads=pk, writes=("vtm",))
        for which, bcol in (("qh", "b_q"), ("kh", "b_k")):
            S.comp("gpsimd", lambda e, which=which: e.tensor_mul(
                out=B["sqbuf"][:, 0:T], in0=B[which][:, 0:T], in1=B[which][:, 0:T]),
                reads=(which,), writes=("sqbuf",))
            for nt in range(n_nt):
                tw = min(512, T - nt * 512)
                half = (nt % 2) * 512
                pk = "PSBIG_a" if nt % 2 == 0 else "PSBIG_b"
                S.comp("tensor", lambda e, nt=nt, tw=tw, half=half: e.matmul(
                    psum[:, half:half + tw], B["ones128"][:, 0:128],
                    B["sqbuf"][:, nt * 512:nt * 512 + tw], start=True, stop=True),
                    reads=("sqbuf", "ones128"), writes=(pk,))
                S.comp("vector", lambda e, nt=nt, tw=tw, half=half: e.reduce_max(
                    out=B["b_nt"][:, nt:nt + 1], in_=psum[:, half:half + tw], axis=AX.X),
                    reads=(pk,), writes=("b_nt",))
            S.comp("vector", lambda e, bcol=bcol: e.reduce_max(
                out=B[bcol][:], in_=B["b_nt"][:, 0:n_nt], axis=AX.X),
                reads=("b_nt",), writes=(bcol,))
        S.comp("vector", lambda e: e.tensor_tensor(out=B["b_q"][:], in0=B["b_q"][:],
                                                   in1=B["b_k"][:], op=ALU.mult),
               reads=("b_q", "b_k"), writes=("b_q",))
        S.comp("scalar", lambda e: e.activation(out=B["b_k"][:], in_=B["b_q"][:],
                                                func=AF.Ln, bias=0.0, scale=1.0),
               reads=("b_q",), writes=("b_k",))
        S.comp("scalar", lambda e: e.activation(out=B["b_q"][:], in_=B["b_k"][:],
                                                func=AF.Exp, bias=0.0, scale=B["c_ph"][:]),
               reads=("b_k", "c_ph"), writes=("b_q",))
        S.comp("vector", lambda e: e.tensor_scalar(out=B["negB"][:], in0=B["b_q"][:],
                                                   scalar1=-scale, scalar2=None,
                                                   op0=ALU.mult),
               reads=("b_q",), writes=("negB",))
        nkt = T // 128
        for qb in range(n_nt):
            qw = min(512, T - qb * 512)
            npair = (nkt + 1) // 2
            prev = None
            for pr in range(npair):
                kts = [2 * pr + j for j in range(2) if 2 * pr + j < nkt]
                pb, pk = (psum, ("PSBIG_a", "PSBIG_b")) if pr % 2 == 0 else                          (B["PS2"], ("PS2_a", "PS2_b"))
                ak = "attnT0" if pr % 2 == 0 else "attnT1"
                at = B[ak]

                def mmsc(e, kts=kts, qb=qb, qw=qw, pb=pb):
                    return mm_scores_body(e, B, kts, qb, qw, pb)

                S.comp("tensor", mmsc, reads=("kh", "qh"), writes=pk)

                def expf(e, kts=kts, qw=qw, at=at, pb=pb):
                    if len(kts) == 2 and qw == 512:
                        return e.activation(
                            out=at[:, 0:1024], in_=pb[:, 0:1024], func=AF.Exp,
                            bias=B["negB"][:], scale=B[cscale_key][:])
                    last = None
                    for j in range(len(kts)):
                        last = e.activation(
                            out=at[:, j * 512:j * 512 + qw],
                            in_=pb[:, j * 512:j * 512 + qw], func=AF.Exp,
                            bias=B["negB"][:], scale=B[cscale_key][:])
                    return last

                S.comp("scalar", expf, reads=pk + ("negB", cscale_key), writes=(ak,))
                if prev is not None:
                    emit_mha_ctx(S, B, prev[0], prev[1], qw, nkt)
                prev = (kts, ak)
            emit_mha_ctx(S, B, prev[0], prev[1], qw, nkt)
            S.comp("scalar", lambda e, qw=qw: e.activation(
                out=B["lnd"][:, 0:qw], in_=B["PDEN"][:, 0:qw], func=AF.Ln,
                bias=0.0, scale=1.0),
                reads=("PDEN",), writes=("lnd",))
            S.comp("scalar", lambda e, qw=qw: e.activation(
                out=B["rden"][:, 0:qw], in_=B["lnd"][:, 0:qw], func=AF.Exp,
                bias=0.0, scale=B["c_neg1"][:]),
                reads=("lnd", "c_neg1"), writes=("rden",))
            S.comp("vector", lambda e, h=h, qb=qb, qw=qw: e.tensor_tensor(
                out=B["ctxfm"][:, h * ctx_stride + qb * 512:h * ctx_stride + qb * 512 + qw],
                in0=B["PCTX"][:, 0:qw], in1=B["rden"][:, 0:qw], op=ALU.mult),
                reads=("PCTX", "rden"), writes=("ctxfm",))
    for nt in range(n_nt):
        tw = min(512, T - nt * 512)
        for m in range(NCH):
            half = ((nt * NCH + m) % 2) * 512
            pk = "PSBIG_a" if (nt * NCH + m) % 2 == 0 else "PSBIG_b"

            def mmo(e, nt=nt, tw=tw, m=m, half=half):
                last = None
                for k in range(NCH):
                    last = e.matmul(
                        psum[:, half:half + tw],
                        w_out[:, k * D + 128 * m:k * D + 128 * m + 128],
                        B["ctxfm"][:, k * ctx_stride + nt * 512:
                                   k * ctx_stride + nt * 512 + tw],
                        start=(k == 0), stop=(k == NCH - 1))
                return last

            S.comp("tensor", mmo, reads=(wo_key, "ctxfm"), writes=(pk,))
            dst_write_fn(S, B, nt, tw, m, psum[:, half:half + tw], pk)


def mm_scores_body(e, B, kts, qb, qw, pb):
    last = None
    for j, kt in enumerate(kts):
        last = e.matmul(
            pb[:, j * 512:j * 512 + qw],
            B["kh"][:, kt * 128:(kt + 1) * 128],
            B["qh"][:, qb * 512:qb * 512 + qw], start=True, stop=True)
    return last


def mm_ctx_body(e, B, kts, ak, qw, nkt):
    at = B[ak]
    last = None
    for j, kt in enumerate(kts):
        e.matmul(B["PCTX"][:, 0:qw], B["vtm"][:, kt * 128:(kt + 1) * 128],
                 at[:, j * 512:j * 512 + qw],
                 start=(kt == 0), stop=(kt == nkt - 1))
        last = e.matmul(B["PDEN"][:, 0:qw], B["ones128"][:, 0:128],
                        at[:, j * 512:j * 512 + qw],
                        start=(kt == 0), stop=(kt == nkt - 1))
    return last


def emit_mha_ctx(S, B, kts, ak, qw, nkt):
    S.comp("tensor", lambda e, kts=kts, ak=ak: mm_ctx_body(e, B, kts, ak, qw, nkt),
           reads=(ak, "vtm", "ones128"), writes=("PCTX", "PDEN"))


def emit_init_consts(S, B, extra=()):
    S.comp("vector", lambda e: e.memset(B["ones128"][:], 1.0), writes=("ones128",))
    S.comp("vector", lambda e: e.memset(B["ones_row"][:], 1.0), writes=("ones_row",))
    S.comp("vector", lambda e: e.memset(B["c_eps"][:], 1e-5), writes=("c_eps",))
    S.comp("vector", lambda e: e.memset(B["c_nh"][:], -0.5), writes=("c_nh",))
    S.comp("vector", lambda e: e.memset(B["c_ph"][:], 0.5), writes=("c_ph",))
    S.comp("vector", lambda e: e.memset(B["c_neg1"][:], -1.0), writes=("c_neg1",))
    for name, v in extra:
        S.comp("vector", lambda e, name=name, v=v: e.memset(B[name][:], v),
               writes=(name,))


def alloc_common(B):
    B.sb("ones128", (128, 128), BF16)
    B.sb("ones_row", (1, 512), BF16)
    B.sb("c_eps", (128, 1), F32)
    B.sb("c_nh", (128, 1), F32)
    B.sb("c_ph", (128, 1), F32)
    B.sb("c_neg1", (128, 1), F32)
    B.sb("wstage0", (128, 512), F32)
    B.sb("wstage1", (128, 512), F32)
    B.ps("PSBIG", (128, 1024), F32)


def alloc_ln(B, width):
    if "sqbuf" not in B:
        B.sb("sqbuf", (128, NCH * width), BF16)
    B.sb("st_mu", (128, width), F32)
    B.sb("st_t1", (128, width), F32)
    B.sb("st_t2", (128, width), F32)
    B.sb("st_r", (128, width), F32)
    B.sb("st_mu_bf", (128, width), BF16)
    B.sb("st_r_bf", (128, width), BF16)
    B.sb("ln_t", (128, width), BF16)


def alloc_attn(B, T, lnw=0):
    B.sb("qh", (128, T), BF16)
    B.sb("kh", (128, T), BF16)
    B.sb("vtm", (128, T), BF16)
    if "sqbuf" not in B:
        B.sb("sqbuf", (128, max(T, NCH * lnw)), BF16)
    B.sb("b_nt", (128, 4), F32)
    B.sb("b_q", (128, 1), F32)
    B.sb("b_k", (128, 1), F32)
    B.sb("negB", (128, 1), F32)
    B.sb("attnT0", (128, 1024), BF16)
    B.sb("attnT1", (128, 1024), BF16)
    B.sb("lnd", (128, 512), F32)
    B.sb("rden", (128, 512), F32)
    B.sb("ctxfm", (128, NCH * T), BF16)
    B.ps("PS2", (128, 1024), F32)
    B.ps("PCTX", (128, 512), F32)
    B.ps("PDEN", (128, 512), F32)
    B.sb("c_sc", (128, 1), F32)



A_T = [512, 1024, 2048]
A_NCHUNK = [4, 2, 1]
A_INW = [1856, 1984, 2048]
A_COFF = [448, 960, 1984]


def build_A(level):
    nc = bass.Bass("TRN2", target_bir_lowering=False)
    es = ExitStack()
    T = A_T[level]
    n_chunks = A_NCHUNK[level]
    in_w = A_INW[level]
    scale = 1.0 / np.sqrt(128.0)

    x_in = nc.dram_tensor("x_in", [D, in_w], F32 if level == 0 else BF16,
                          kind="ExternalInput").ap()
    w_in_t = nc.dram_tensor("w_in_t", [D, 3 * D], F32, kind="ExternalInput").ap()
    b_in_p = nc.dram_tensor("b_in_p", [128, 24], F32, kind="ExternalInput").ap()
    b_in_f = nc.dram_tensor("b_in_f", [1, 3 * D], F32, kind="ExternalInput").ap()
    w_out_t = nc.dram_tensor("w_out_t", [D, D], F32, kind="ExternalInput").ap()
    b_out_p = nc.dram_tensor("b_out_p", [128, 8], F32, kind="ExternalInput").ap()
    if level == 2:
        x_own = nc.dram_tensor("x_own", [D, 2048], F32, kind="ExternalInput").ap()
        out_t = nc.dram_tensor("out", [D, 2048], F32, kind="ExternalOutput").ap()
    else:
        out_t = nc.dram_tensor("out", [D, 2048], BF16, kind="ExternalOutput").ap()

    with es:
        B = Buffers(nc, es)
        S = Sched(nc, es)
        alloc_common(B)
        lnw = 512 if level == 0 else 128
        alloc_attn(B, T, lnw if level in (0, 2) else 0)
        B.sb("w_in", (128, NCH * 3 * D), BF16)
        B.sb("w_out", (128, NCH * D), BF16)
        B.sb("b_in_p", (128, 24), F32)
        B.sb("b_in_f_bf", (1, 3 * D), BF16)
        B.sb("b_out_p", (128, 8), F32)
        B.sb("src", (128, NCH * in_w), BF16)
        B.sb("dstbuf", (128, NCH * 2048), BF16)
        if level in (0, 2):
            alloc_ln(B, lnw)
            B.sb("xg", (128, NCH * lnw), F32)
        if level == 2:
            B.sb("res_t", (128, lnw), F32)

        emit_init_consts(S, B, [("c_sc", scale)])
        emit_weight_load(S, B, w_in_t, B["w_in"], NCH, 3 * D, "w_in")
        emit_weight_load(S, B, w_out_t, B["w_out"], NCH, D, "w_out")
        S.dma([(B["b_in_p"][:], b_in_p), (B["b_out_p"][:], b_out_p)],
              writes=("b_in_p", "b_out_p"))
        emit_flat_bias_load(S, B, b_in_f, B["b_in_f_bf"], 3 * D, "b_in_f_bf")

        if level == 0:
            B.sb("xbf", (128, NCH * 512), BF16)
            c0 = 0
            while c0 < in_w:
                gw = min(512, in_w - c0)
                S.dma([(B["xg"][:, k * gw:(k + 1) * gw],
                        x_in[128 * k:128 * (k + 1), c0:c0 + gw]) for k in range(NCH)],
                      writes=("xg",))
                S.comp("vector", lambda e, gw=gw: e.tensor_copy(
                    out=B["xbf"][:, 0:NCH * gw], in_=B["xg"][:, 0:NCH * gw]),
                    reads=("xg",), writes=("xbf",))
                emit_ln_stats(S, B, lambda k, gw=gw: B["xbf"][:, k * gw:(k + 1) * gw],
                              ("xbf",), gw)
                emit_ln_apply_bf(S, B, lambda k, gw=gw: B["xbf"][:, k * gw:(k + 1) * gw],
                                 ("xbf",), gw,
                                 lambda k, c0=c0, gw=gw: B["src"][:, k * in_w + c0:
                                                                  k * in_w + c0 + gw],
                                 "src")
                c0 += gw
        else:
            S.dma([(B["src"][:, k * in_w:(k + 1) * in_w],
                    x_in[128 * k:128 * (k + 1), :]) for k in range(NCH)],
                  writes=("src",))

        for j in range(n_chunks):
            def dst_write(S_, B_, nt, tw, m, psum_ap, pk, j=j):
                S_.comp("scalar", lambda e, nt=nt, tw=tw, m=m, psum_ap=psum_ap, j=j:
                        e.add(out=B_["dstbuf"][:, m * 2048 + T * j + nt * 512:
                                               m * 2048 + T * j + nt * 512 + tw],
                              in_=psum_ap, add=B_["b_out_p"][:, m:m + 1]),
                        reads=(pk, "b_out_p"), writes=("dstbuf",))

            emit_mha(S, B, B["src"], "src", A_COFF[level] * j, T, dst_write,
                     scale=scale)

        if level < 2:
            S.dma([(out_t[128 * k:128 * (k + 1), :],
                    B["dstbuf"][:, k * 2048:(k + 1) * 2048]) for k in range(NCH)],
                  reads=("dstbuf",), writes=("out",))
        else:
            for g0 in range(0, 2048, lnw):
                S.dma([(B["xg"][:, k * lnw:(k + 1) * lnw],
                        x_own[128 * k:128 * (k + 1), g0:g0 + lnw]) for k in range(NCH)],
                      writes=("xg",))
                emit_ln_stats(S, B, lambda k, g0=g0: B["dstbuf"][:, k * 2048 + g0:
                                                                 k * 2048 + g0 + lnw],
                              ("dstbuf",), lnw)
                for k in range(NCH):
                    S.comp("vector", lambda e, k=k, g0=g0: e.tensor_tensor(
                        out=B["ln_t"][:, 0:lnw],
                        in0=B["dstbuf"][:, k * 2048 + g0:k * 2048 + g0 + lnw],
                        in1=B["st_mu_bf"][:, 0:lnw], op=ALU.subtract),
                        reads=("dstbuf", "st_mu_bf"), writes=("ln_t",))
                    S.comp("vector", lambda e, k=k: e.tensor_tensor(
                        out=B["res_t"][:, 0:lnw], in0=B["ln_t"][:, 0:lnw],
                        in1=B["st_r_bf"][:, 0:lnw], op=ALU.mult),
                        reads=("ln_t", "st_r_bf"), writes=("res_t",))
                    S.comp("vector", lambda e, k=k: e.tensor_tensor(
                        out=B["res_t"][:, 0:lnw], in0=B["res_t"][:, 0:lnw],
                        in1=B["xg"][:, k * lnw:(k + 1) * lnw], op=ALU.add),
                        reads=("res_t", "xg"), writes=("res_t",))
                    S.dma([(out_t[128 * k:128 * (k + 1), g0:g0 + lnw],
                            B["res_t"][:, 0:lnw])],
                          reads=("res_t",), writes=("out",))

        S.barrier_wait("sync", ("out",))
        S.finalize()
    return nc



def _fmT(a):
    return np.ascontiguousarray(np.asarray(a, dtype=np.float32).T)


def _fmT_bf(a):
    return np.ascontiguousarray(np.asarray(a).T)


def _bias_p(v):
    v = np.asarray(v, dtype=np.float32)
    return np.ascontiguousarray(v.reshape(-1, 128).T)


def _bias_f(v):
    return np.ascontiguousarray(np.asarray(v, dtype=np.float32).reshape(1, -1))


def _bias_p24(bin_):
    return _bias_p(bin_)


def run_A(level, x_fm_f32, x_fm_prev_bf, Win, bin_, Wout, bout):
    nc = build_A(level)
    w_in_t = _fmT(Win)
    w_out_t = _fmT(Wout)
    b_in_p = _bias_p24(bin_)
    b_in_f = _bias_f(bin_)
    b_out_p = _bias_p(bout)
    in_maps = []
    in_w = A_INW[level]
    for c in range(8):
        b, q = c // 4, c % 4
        m = {"w_in_t": w_in_t, "b_in_p": b_in_p, "b_in_f": b_in_f,
             "w_out_t": w_out_t, "b_out_p": b_out_p}
        if level == 0:
            s0 = 1792 * q
            m["x_in"] = np.ascontiguousarray(x_fm_f32[b][:, s0:s0 + in_w])
        elif level == 1:
            s0 = 1920 * q
            m["x_in"] = np.ascontiguousarray(x_fm_prev_bf[b][:, s0:s0 + in_w])
        else:
            s0 = 1984 * q
            m["x_in"] = np.ascontiguousarray(x_fm_prev_bf[b][:, s0:s0 + in_w])
            m["x_own"] = np.ascontiguousarray(x_fm_f32[b][:, 2048 * q:2048 * (q + 1)])
        in_maps.append(m)
    res = run_bass_kernel_spmd(nc, in_maps, core_ids=list(range(8)))
    odt = np.float32 if level == 2 else BF
    out = np.empty((2, D, 8192), dtype=odt)
    for c in range(8):
        b, q = c // 4, c % 4
        out[b][:, 2048 * q:2048 * (q + 1)] = np.asarray(res.results[c]["out"])
    return out





def emit_tail_group(S, B, chunk_ap_fn, src_keys, lnw, res_dram_ap, out_dram_ap, g0):
    S.dma([(B["xg"][:, k * lnw:(k + 1) * lnw],
            res_dram_ap[128 * k:128 * (k + 1), g0:g0 + lnw]) for k in range(NCH)],
          writes=("xg",))
    emit_ln_stats(S, B, chunk_ap_fn, src_keys, lnw)
    for k in range(NCH):
        S.comp("vector", lambda e, k=k: e.tensor_tensor(
            out=B["ln_t"][:, 0:lnw], in0=chunk_ap_fn(k),
            in1=B["st_mu_bf"][:, 0:lnw], op=ALU.subtract),
            reads=tuple(src_keys) + ("st_mu_bf",), writes=("ln_t",))
        S.comp("vector", lambda e, k=k: e.tensor_tensor(
            out=B["res_t"][:, 0:lnw], in0=B["ln_t"][:, 0:lnw],
            in1=B["st_r_bf"][:, 0:lnw], op=ALU.mult),
            reads=("ln_t", "st_r_bf"), writes=("res_t",))
        S.comp("vector", lambda e, k=k: e.tensor_tensor(
            out=B["res_t"][:, 0:lnw], in0=B["res_t"][:, 0:lnw],
            in1=B["xg"][:, k * lnw:(k + 1) * lnw], op=ALU.add),
            reads=("res_t", "xg"), writes=("res_t",))
        S.dma([(out_dram_ap[128 * k:128 * (k + 1), g0:g0 + lnw],
                B["res_t"][:, 0:lnw])],
              reads=("res_t",), writes=("out",))



B_TH = 1024
B_HH = [(0, 6), (6, 5)]


def build_B():
    nc = bass.Bass("TRN2", target_bir_lowering=False)
    es = ExitStack()
    H = 1408
    x1 = nc.dram_tensor("x1", [D, 2048], F32, kind="ExternalInput").ap()
    gw_t = nc.dram_tensor("gw_t", [D, 8], F32, kind="ExternalInput").ap()
    gb_b = nc.dram_tensor("gb_b", [128, 8], F32, kind="ExternalInput").ap()
    w1a = nc.dram_tensor("w1a", [9 * D, H], F32, kind="ExternalInput").ap()
    w3a = nc.dram_tensor("w3a", [9 * D, H], F32, kind="ExternalInput").ap()
    w2a = nc.dram_tensor("w2a", [9 * H, D], F32, kind="ExternalInput").ap()
    b1a = nc.dram_tensor("b1a", [9 * 128, 11], F32, kind="ExternalInput").ap()
    b3a = nc.dram_tensor("b3a", [9 * 128, 11], F32, kind="ExternalInput").ap()
    b2d = nc.dram_tensor("b2d", [8, D], F32, kind="ExternalInput").ap()
    sb2f = nc.dram_tensor("sb2f", [1, D], F32, kind="ExternalInput").ap()
    ident = nc.dram_tensor("identf", [128, 128], F32, kind="ExternalInput").ap()
    out_t = nc.dram_tensor("out", [D, 2048], F32, kind="ExternalOutput").ap()

    with es:
        B = Buffers(nc, es)
        S = Sched(nc, es)
        alloc_common(B)
        alloc_ln(B, 512)
        B.sb("y", (128, NCH * B_TH), BF16)
        B.sb("acc", (128, NCH * B_TH), F32)
        B.sb("accbf", (128, NCH * 512), BF16)
        B.sb("xg", (128, NCH * 512), F32)
        B.sb("res_t", (128, 512), F32)
        B.sb("ident_f32", (128, 128), F32)
        B.ps("PSTR", (128, 128), F32)
        for p in range(2):
            B.sb(f"w1h{p}", (128, NCH * 768), BF16)
            B.sb(f"w3h{p}", (128, NCH * 768), BF16)
            B.sb(f"w2h{p}", (128, 6 * D), BF16)
            B.sb(f"b1h{p}", (128, 11), F32)
            B.sb(f"b3h{p}", (128, 11), F32)
        B.sb("hbuf", (128, 6 * B_TH), BF16)
        B.sb("g1sb0", (128, 512), BF16)
        B.sb("g3sb0", (128, 512), BF16)
        B.sb("g1sb1", (128, 512), BF16)
        B.sb("g3sb1", (128, 512), BF16)
        B.ps("PS2", (128, 1024), F32)
        B.sb("gw_sb", (128, NCH * 8), F32)
        B.sb("gwcs", (128, 8), F32)
        B.sb("gb_sb", (128, 8), F32)
        B.sb("b2bf", (8, D), BF16)
        B.sb("b2st", (8, D), F32)
        B.sb("sb2bf", (1, D), BF16)
        B.sb("st_mu_tm", (128, 8), F32)
        B.sb("st_negmu_tm", (128, 8), F32)
        B.sb("st_r_tm", (128, 8), F32)
        B.sb("sc", (128, 8), F32)
        B.sb("sc2", (128, 8), F32)
        B.sb("m1", (128, 1), F32)
        B.sb("rw", (128, 8), F32)
        B.sb("rwT", (8, B_TH), BF16)
        B.sb("rwrow", (1, B_TH), BF16)
        B.sb("rw_b", (128, B_TH), BF16)

        emit_init_consts(S, B)
        S.dma([(B["gb_sb"][:], gb_b), (B["ident_f32"][:], ident),
               (B["b2st"][:], b2d)],
              writes=("gb_sb", "ident_f32", "b2st"))
        S.comp("vector", lambda e: e.tensor_copy(out=B["b2bf"][:], in_=B["b2st"][:]),
               reads=("b2st",), writes=("b2bf",))
        emit_flat_bias_load(S, B, sb2f, B["sb2bf"], D, "sb2bf")
        for k in range(NCH):
            S.dma([(B["gw_sb"][:, k * 8:(k + 1) * 8], gw_t[128 * k:128 * (k + 1), :])],
                  writes=("gw_sb",))
        B.sb("ones_f32", (128, 128), F32)
        S.comp("vector", lambda e: e.memset(B["ones_f32"][:], 1.0), writes=("ones_f32",))

        def mm_gwcs(e):
            last = None
            for k in range(NCH):
                last = e.matmul(B["PSBIG"][:, 0:8], B["ones_f32"][:],
                                B["gw_sb"][:, k * 8:(k + 1) * 8],
                                start=(k == 0), stop=(k == NCH - 1))
            return last

        S.comp("tensor", mm_gwcs, reads=("gw_sb", "ones_f32"), writes=("PSBIG_a",))
        S.comp("vector", lambda e: e.tensor_copy(out=B["gwcs"][:], in_=B["PSBIG"][:, 0:8]),
               reads=("PSBIG_a",), writes=("gwcs",))

        psum = B["PSBIG"]
        for th in range(2):
            t0 = th * B_TH
            for g in range(2):
                g0 = t0 + g * 512
                S.dma([(B["xg"][:, k * 512:(k + 1) * 512],
                        x1[128 * k:128 * (k + 1), g0:g0 + 512]) for k in range(NCH)],
                      writes=("xg",))
                S.comp("vector", lambda e: e.tensor_copy(
                    out=B["accbf"][:, 0:NCH * 512], in_=B["xg"][:, 0:NCH * 512]),
                    reads=("xg",), writes=("accbf",))
                emit_ln_stats(S, B, lambda k: B["accbf"][:, k * 512:(k + 1) * 512],
                              ("accbf",), 512)
                emit_stats_tm(S, B, 512, g * 4)
                emit_ln_apply_bf(S, B, lambda k: B["accbf"][:, k * 512:(k + 1) * 512],
                                 ("accbf",), 512,
                                 lambda k, g=g: B["y"][:, k * B_TH + g * 512:
                                                       k * B_TH + g * 512 + 512],
                                 "y")
                for tt in range(4):
                    gt = g * 4 + tt

                    def mm_gate(e, g=g, tt=tt):
                        last = None
                        for k in range(NCH):
                            last = e.matmul(
                                psum[:, 0:8],
                                B["xg"][:, k * 512 + tt * 128:k * 512 + tt * 128 + 128],
                                B["gw_sb"][:, k * 8:(k + 1) * 8],
                                start=(k == 0), stop=(k == NCH - 1))
                        return last

                    S.comp("tensor", mm_gate, reads=("xg", "gw_sb"), writes=("PSBIG_a",))
                    S.comp("vector", lambda e, gt=gt: e.tensor_scalar(
                        out=B["st_negmu_tm"][:, gt:gt + 1], in0=B["st_mu_tm"][:, gt:gt + 1],
                        scalar1=-1.0, scalar2=None, op0=ALU.mult),
                        reads=("st_mu_tm",), writes=("st_negmu_tm",))
                    S.comp("vector", lambda e, gt=gt: e.scalar_tensor_tensor(
                        out=B["sc"][:], in0=B["gwcs"][:],
                        scalar=B["st_negmu_tm"][:, gt:gt + 1], in1=psum[:, 0:8],
                        op0=ALU.mult, op1=ALU.add),
                        reads=("gwcs", "st_negmu_tm", "PSBIG_a"), writes=("sc",))
                    S.comp("vector", lambda e, gt=gt: e.tensor_scalar(
                        out=B["sc"][:], in0=B["sc"][:],
                        scalar1=B["st_r_tm"][:, gt:gt + 1], scalar2=None, op0=ALU.mult),
                        reads=("sc", "st_r_tm"), writes=("sc",))
                    S.comp("vector", lambda e: e.reduce_max(
                        out=B["m1"][:], in_=B["sc"][:], axis=AX.X, negate=True),
                        reads=("sc",), writes=("m1",))
                    S.comp("scalar", lambda e: e.activation(
                        out=B["sc2"][:], in_=B["sc"][:], func=AF.Exp,
                        bias=B["m1"][:], scale=1.0),
                        reads=("sc", "m1"), writes=("sc2",))
                    S.comp("vector", lambda e: e.reduce_sum(
                        out=B["m1"][:], in_=B["sc2"][:], axis=AX.X),
                        reads=("sc2",), writes=("m1",))
                    S.comp("vector", lambda e: e.reciprocal(
                        out=B["m1"][:], in_=B["m1"][:]),
                        reads=("m1",), writes=("m1",))
                    S.comp("vector", lambda e: e.tensor_scalar(
                        out=B["sc"][:], in0=B["sc2"][:], scalar1=B["m1"][:],
                        scalar2=None, op0=ALU.mult),
                        reads=("sc2", "m1"), writes=("sc",))
                    S.comp("vector", lambda e: e.tensor_tensor(
                        out=B["sc2"][:], in0=B["sc"][:], in1=B["gb_sb"][:], op=ALU.add),
                        reads=("sc", "gb_sb"), writes=("sc2",))
                    S.comp("vector", lambda e: e.reduce_max(
                        out=B["m1"][:], in_=B["sc2"][:], axis=AX.X),
                        reads=("sc2",), writes=("m1",))
                    S.comp("vector", lambda e: e.tensor_scalar(
                        out=B["rw"][:], in0=B["sc2"][:], scalar1=B["m1"][:],
                        scalar2=None, op0=ALU.is_ge),
                        reads=("sc2", "m1"), writes=("rw",))
                    S.comp("vector", lambda e: e.scalar_tensor_tensor(
                        out=B["rw"][:], in0=B["rw"][:], scalar=-1e30, in1=B["sc2"][:],
                        op0=ALU.mult, op1=ALU.add),
                        reads=("rw", "sc2"), writes=("rw",))
                    S.comp("vector", lambda e: e.reduce_max(
                        out=B["m1"][:], in_=B["rw"][:], axis=AX.X),
                        reads=("rw",), writes=("m1",))
                    S.comp("vector", lambda e: e.tensor_scalar(
                        out=B["rw"][:], in0=B["sc2"][:], scalar1=B["m1"][:],
                        scalar2=None, op0=ALU.is_ge),
                        reads=("sc2", "m1"), writes=("rw",))
                    S.comp("vector", lambda e: e.tensor_tensor(
                        out=B["rw"][:], in0=B["rw"][:], in1=B["sc"][:], op=ALU.mult),
                        reads=("rw", "sc"), writes=("rw",))
                    S.comp("tensor", lambda e: e.transpose(
                        B["PSTR"][0:8, 0:128], B["rw"][:], B["ident_f32"][:]),
                        reads=("rw", "ident_f32"), writes=("PSTR",))
                    S.comp("vector", lambda e, g=g, tt=tt: e.tensor_copy(
                        out=B["rwT"][0:8, (g * 4 + tt) * 128:(g * 4 + tt) * 128 + 128],
                        in_=B["PSTR"][0:8, 0:128]),
                        reads=("PSTR",), writes=("rwT",))
            def plan_wload(e_i, hh_i):
                p = (e_i * 2 + hh_i) % 2
                hc0, hnc = B_HH[hh_i]
                hw = hnc * 128
                w1k, w3k, w2k = f"w1h{p}", f"w3h{p}", f"w2h{p}"
                b1k, b3k = f"b1h{p}", f"b3h{p}"
                for k in range(NCH):
                    for p0 in range(0, hw, 512):
                        pw = min(512, hw - p0)
                        stg = "wstage0" if (k + p0) % 2 == 0 else "wstage1"
                        S.dma([(B[stg][:, 0:pw],
                                w1a[e_i * D + 128 * k:e_i * D + 128 * (k + 1),
                                    hc0 * 128 + p0:hc0 * 128 + p0 + pw])],
                              writes=(stg,))
                        S.comp("gpsimd", lambda e, k=k, p0=p0, pw=pw, stg=stg, w1k=w1k: e.tensor_copy(
                            out=B[w1k][:, k * 768 + p0:k * 768 + p0 + pw],
                            in_=B[stg][:, 0:pw]),
                            reads=(stg,), writes=(w1k,))
                        stg2 = "wstage0" if (k + p0) % 2 == 1 else "wstage1"
                        S.dma([(B[stg2][:, 0:pw],
                                w3a[e_i * D + 128 * k:e_i * D + 128 * (k + 1),
                                    hc0 * 128 + p0:hc0 * 128 + p0 + pw])],
                              writes=(stg2,))
                        S.comp("gpsimd", lambda e, k=k, p0=p0, pw=pw, stg2=stg2, w3k=w3k: e.tensor_copy(
                            out=B[w3k][:, k * 768 + p0:k * 768 + p0 + pw],
                            in_=B[stg2][:, 0:pw]),
                            reads=(stg2,), writes=(w3k,))
                for j in range(hnc):
                    for p0 in range(0, D, 512):
                        stg = "wstage0" if (j + p0 // 512) % 2 == 0 else "wstage1"
                        S.dma([(B[stg][:, 0:512],
                                w2a[e_i * H + (hc0 + j) * 128:e_i * H + (hc0 + j) * 128 + 128,
                                    p0:p0 + 512])],
                              writes=(stg,))
                        S.comp("gpsimd", lambda e, j=j, p0=p0, stg=stg, w2k=w2k: e.tensor_copy(
                            out=B[w2k][:, j * D + p0:j * D + p0 + 512],
                            in_=B[stg][:, 0:512]),
                            reads=(stg,), writes=(w2k,))
                S.dma([(B[b1k][:, 0:11], b1a[e_i * 128:(e_i + 1) * 128, :]),
                       (B[b3k][:, 0:11], b3a[e_i * 128:(e_i + 1) * 128, :])],
                      writes=(b1k, b3k))

            def plan_rwb(e_i):
                S.dma([(B["rwrow"][0:1, :], B["rwT"][e_i:e_i + 1, :])],
                      reads=("rwT",), writes=("rwrow",))
                for nt in range(2):
                    S.comp("tensor", lambda e, nt=nt: e.matmul(
                        psum[:, 0:512], B["ones_row"][0:1, 0:128],
                        B["rwrow"][0:1, nt * 512:nt * 512 + 512],
                        start=True, stop=True),
                        reads=("rwrow", "ones_row"), writes=("PSBIG_a",))
                    S.comp("vector", lambda e, nt=nt: e.tensor_copy(
                        out=B["rw_b"][:, nt * 512:nt * 512 + 512],
                        in_=psum[:, 0:512]),
                        reads=("PSBIG_a",), writes=("rw_b",))

            def plan_compute(e_i, hh_i):
                p = (e_i * 2 + hh_i) % 2
                hc0, hnc = B_HH[hh_i]
                w1k, w3k, w2k = f"w1h{p}", f"w3h{p}", f"w2h{p}"
                b1k, b3k = f"b1h{p}", f"b3h{p}"
                for mc in range(hnc):
                    for nt in range(2):
                        par = (mc * 2 + nt) % 2
                        pb = psum if par == 0 else B["PS2"]
                        pka = "PSBIG_a" if par == 0 else "PS2_a"
                        pkb = "PSBIG_b" if par == 0 else "PS2_b"
                        g1k = f"g1sb{par}"
                        g3k = f"g3sb{par}"

                        def mm1(e, mc=mc, nt=nt, w1k=w1k, pb=pb):
                            last = None
                            for k in range(NCH):
                                last = e.matmul(
                                    pb[:, 0:512],
                                    B[w1k][:, k * 768 + 128 * mc:k * 768 + 128 * mc + 128],
                                    B["y"][:, k * B_TH + nt * 512:k * B_TH + nt * 512 + 512],
                                    start=(k == 0), stop=(k == NCH - 1))
                            return last

                        S.comp("tensor", mm1, reads=(w1k, "y"), writes=(pka,))

                        def mm3(e, mc=mc, nt=nt, w3k=w3k, pb=pb):
                            last = None
                            for k in range(NCH):
                                last = e.matmul(
                                    pb[:, 512:1024],
                                    B[w3k][:, k * 768 + 128 * mc:k * 768 + 128 * mc + 128],
                                    B["y"][:, k * B_TH + nt * 512:k * B_TH + nt * 512 + 512],
                                    start=(k == 0), stop=(k == NCH - 1))
                            return last

                        S.comp("tensor", mm3, reads=(w3k, "y"), writes=(pkb,))
                        S.comp("scalar", lambda e, mc=mc, b1k=b1k, hc0=hc0, pb=pb, g1k=g1k: e.add(
                            out=B[g1k][:], in_=pb[:, 0:512],
                            add=B[b1k][:, hc0 + mc:hc0 + mc + 1]),
                            reads=(pka, b1k), writes=(g1k,))
                        S.comp("scalar", lambda e, mc=mc, b3k=b3k, hc0=hc0, pb=pb, g3k=g3k: e.add(
                            out=B[g3k][:], in_=pb[:, 512:1024],
                            add=B[b3k][:, hc0 + mc:hc0 + mc + 1]),
                            reads=(pkb, b3k), writes=(g3k,))
                        S.comp("vector", lambda e, g1k=g1k, g3k=g3k: e.tensor_tensor(
                            out=B[g1k][:], in0=B[g1k][:], in1=B[g3k][:],
                            op=ALU.mult),
                            reads=(g1k, g3k), writes=(g1k,))
                        if e_i < 8:
                            S.comp("scalar", lambda e, g1k=g1k, g3k=g3k: e.activation(
                                out=B[g3k][:], in_=B[g1k][:], func=AF.Silu,
                                bias=0.0, scale=1.0),
                                reads=(g1k,), writes=(g3k,))
                            S.comp("vector", lambda e, mc=mc, nt=nt, g3k=g3k: e.tensor_tensor(
                                out=B["hbuf"][:, mc * B_TH + nt * 512:mc * B_TH + nt * 512 + 512],
                                in0=B[g3k][:], in1=B["rw_b"][:, nt * 512:nt * 512 + 512],
                                op=ALU.mult),
                                reads=(g3k, "rw_b"), writes=("hbuf",))
                        else:
                            S.comp("scalar", lambda e, mc=mc, nt=nt, g1k=g1k: e.activation(
                                out=B["hbuf"][:, mc * B_TH + nt * 512:mc * B_TH + nt * 512 + 512],
                                in_=B[g1k][:], func=AF.Silu, bias=0.0, scale=1.0),
                                reads=(g1k,), writes=("hbuf",))
                for nt in range(2):
                    for m in range(NCH):
                        par = (nt * NCH + m) % 2
                        pb = psum if par == 0 else B["PS2"]
                        half = 0 if True else 0
                        pk = "PSBIG_a" if par == 0 else "PS2_a"

                        def mm2b(e, nt=nt, m=m, hnc=hnc, w2k=w2k, pb=pb,
                                 add_bias=(e_i == 8 and hh_i == 0)):
                            last = None
                            for j in range(hnc):
                                is_last = (j == hnc - 1) and not add_bias
                                last = e.matmul(
                                    pb[:, 0:512],
                                    B[w2k][:, j * D + 128 * m:j * D + 128 * m + 128],
                                    B["hbuf"][:, j * B_TH + nt * 512:j * B_TH + nt * 512 + 512],
                                    start=(j == 0), stop=is_last)
                            if add_bias:
                                e.matmul(pb[:, 0:512],
                                         B["b2bf"][0:8, 128 * m:128 * m + 128],
                                         B["rwT"][0:8, nt * 512:nt * 512 + 512],
                                         start=False, stop=False)
                                last = e.matmul(
                                    pb[:, 0:512],
                                    B["sb2bf"][0:1, 128 * m:128 * m + 128],
                                    B["ones_row"][0:1, 0:512],
                                    start=False, stop=True)
                            return last

                        S.comp("tensor", mm2b,
                               reads=(w2k, "hbuf", "b2bf", "rwT", "sb2bf", "ones_row"),
                               writes=(pk,))
                        if e_i == 0 and hh_i == 0:
                            S.comp("vector", lambda e, nt=nt, m=m, pb=pb: e.tensor_copy(
                                out=B["acc"][:, m * B_TH + nt * 512:m * B_TH + nt * 512 + 512],
                                in_=pb[:, 0:512]),
                                reads=(pk,), writes=("acc",))
                        else:
                            S.comp("vector", lambda e, nt=nt, m=m, pb=pb: e.tensor_tensor(
                                out=B["acc"][:, m * B_TH + nt * 512:m * B_TH + nt * 512 + 512],
                                in0=B["acc"][:, m * B_TH + nt * 512:m * B_TH + nt * 512 + 512],
                                in1=pb[:, 0:512], op=ALU.add),
                                reads=("acc", pk), writes=("acc",))

            steps = [(e_i, hh_i) for e_i in range(9) for hh_i in range(2)]
            plan_wload(*steps[0])
            for si, (e_i, hh_i) in enumerate(steps):
                if hh_i == 0 and e_i < 8:
                    plan_rwb(e_i)
                if si + 1 < len(steps):
                    plan_wload(*steps[si + 1])
                plan_compute(e_i, hh_i)
            for g in range(2):
                g0 = t0 + g * 512
                for k in range(NCH):
                    S.comp("vector", lambda e, k=k, g=g: e.tensor_copy(
                        out=B["accbf"][:, k * 512:(k + 1) * 512],
                        in_=B["acc"][:, k * B_TH + g * 512:k * B_TH + g * 512 + 512]),
                        reads=("acc",), writes=("accbf",))
                emit_tail_group(S, B, lambda k: B["accbf"][:, k * 512:(k + 1) * 512],
                                ("accbf",), 512, x1, out_t, g0)

        S.barrier_wait("sync", ("out",))
        S.finalize()
    return nc


def run_B(x1_fm, gate_w, gate_b, w1, b1, w3, b3, w2, b2, sw1, sb1, sw3, sb3,
          sw2, sb2):
    nc = build_B()
    H = 1408
    w1a = np.ascontiguousarray(np.concatenate(
        [np.asarray(w1, np.float32).reshape(8 * 1024, H),
         np.asarray(sw1, np.float32)], axis=0))
    w3a = np.ascontiguousarray(np.concatenate(
        [np.asarray(w3, np.float32).reshape(8 * 1024, H),
         np.asarray(sw3, np.float32)], axis=0))
    w2a = np.ascontiguousarray(np.concatenate(
        [np.asarray(w2, np.float32).reshape(8 * H, 1024),
         np.asarray(sw2, np.float32)], axis=0))
    b1a = np.ascontiguousarray(np.concatenate(
        [np.stack([_bias_p(np.asarray(b1, np.float32)[e]) for e in range(8)]),
         _bias_p(np.asarray(sb1, np.float32))[None]], axis=0).reshape(9 * 128, 11))
    b3a = np.ascontiguousarray(np.concatenate(
        [np.stack([_bias_p(np.asarray(b3, np.float32)[e]) for e in range(8)]),
         _bias_p(np.asarray(sb3, np.float32))[None]], axis=0).reshape(9 * 128, 11))
    m_common = {
        "gw_t": _fmT(gate_w), "gb_b": np.ascontiguousarray(
            np.broadcast_to(np.asarray(gate_b, np.float32)[None, :], (128, 8))),
        "w1a": w1a, "w3a": w3a, "w2a": w2a, "b1a": b1a, "b3a": b3a,
        "b2d": np.ascontiguousarray(np.asarray(b2, np.float32)),
        "sb2f": _bias_f(sb2), "identf": np.eye(128, dtype=np.float32),
    }
    in_maps = []
    for c in range(8):
        b, q = c // 4, c % 4
        m = dict(m_common)
        m["x1"] = np.ascontiguousarray(x1_fm[b][:, 2048 * q:2048 * (q + 1)])
        in_maps.append(m)
    res = run_bass_kernel_spmd(nc, in_maps, core_ids=list(range(8)))
    out = np.empty((2, D, 8192), dtype=np.float32)
    for c in range(8):
        b, q = c // 4, c % 4
        out[b][:, 2048 * q:2048 * (q + 1)] = np.asarray(res.results[c]["out"])
    return out



def build_C1():
    nc = bass.Bass("TRN2", target_bir_lowering=False)
    es = ExitStack()
    S_LEN = 8192
    x2 = nc.dram_tensor("x2", [D, S_LEN], F32, kind="ExternalInput").ap()
    wq = nc.dram_tensor("wq", [D, 256], F32, kind="ExternalInput").ap()
    wk = nc.dram_tensor("wk", [D, 256], F32, kind="ExternalInput").ap()
    wv = nc.dram_tensor("wv", [D, 256], F32, kind="ExternalInput").ap()
    thr = nc.dram_tensor("thr", [1, S_LEN], F32, kind="ExternalInput").ap()
    out_t = nc.dram_tensor("out", [256, S_LEN], BF16, kind="ExternalOutput").ap()

    with es:
        B = Buffers(nc, es)
        S = Sched(nc, es)
        alloc_common(B)
        alloc_ln(B, 512)
        B.sb("qfm", (128, 2 * S_LEN), BF16)
        B.sb("kfm", (128, 2 * S_LEN), BF16)
        B.sb("vtm", (128, 64 * 256), BF16)
        B.sb("xg", (128, NCH * 512), F32)
        B.sb("y2g", (128, NCH * 512), BF16)
        B.sb("wq_sb", (128, NCH * 256), BF16)
        B.sb("wk_sb", (128, NCH * 256), BF16)
        B.sb("wv_sb", (128, NCH * 256), BF16)
        B.sb("attnT0", (128, 1024), BF16)
        B.sb("attnT1", (128, 1024), BF16)
        B.sb("thr_pc", (1, 512), F32)
        B.sb("thrb", (128, 512), F32)
        B.sb("kcol", (128, 64), F32)
        B.sb("mask_sb", (128, 512), BF16)
        B.sb("ctxbuf0", (128, 512), BF16)
        B.sb("ctxbuf1", (128, 512), BF16)
        B.sb("lnd", (128, 512), F32)
        B.sb("rden", (128, 512), F32)
        B.sb("ones_f32", (128, 128), F32)
        B.sb("b_nt", (128, 16), F32)
        B.sb("bq4", (128, 4), F32)
        B.sb("bk4", (128, 4), F32)
        B.sb("negB4", (128, 4), F32)
        B.sb("c_sc", (128, 1), F32)
        B.ps("PS2", (128, 1024), F32)
        B.ps("PCTX", (128, 512), F32)
        B.ps("PDEN", (128, 512), F32)
        psum = B["PSBIG"]

        emit_init_consts(S, B, [("c_sc", 0.125)])
        S.comp("vector", lambda e: e.memset(B["ones_f32"][:], 1.0), writes=("ones_f32",))
        S.comp("gpsimd", lambda e: e.iota(B["kcol"][:], pattern=[[128, 64]],
                                          base=0, channel_multiplier=1,
                                          allow_small_or_imprecise_dtypes=True),
               writes=("kcol",))
        emit_weight_load(S, B, wq, B["wq_sb"], NCH, 256, "wq_sb")
        emit_weight_load(S, B, wk, B["wk_sb"], NCH, 256, "wk_sb")
        emit_weight_load(S, B, wv, B["wv_sb"], NCH, 256, "wv_sb")

        for g in range(16):
            g0 = g * 512
            S.dma([(B["xg"][:, k * 512:(k + 1) * 512],
                    x2[128 * k:128 * (k + 1), g0:g0 + 512]) for k in range(NCH)],
                  writes=("xg",))
            S.comp("vector", lambda e: e.tensor_copy(
                out=B["y2g"][:, 0:NCH * 512], in_=B["xg"][:, 0:NCH * 512]),
                reads=("xg",), writes=("y2g",))
            emit_ln_stats(S, B, lambda k: B["y2g"][:, k * 512:(k + 1) * 512],
                          ("y2g",), 512)
            emit_ln_apply_bf(S, B, lambda k: B["y2g"][:, k * 512:(k + 1) * 512],
                             ("y2g",), 512,
                             lambda k: B["y2g"][:, k * 512:(k + 1) * 512],
                             "y2g")
            for which, wkey in (("qfm", "wq_sb"), ("kfm", "wk_sb")):
                for mh in range(2):
                    half = (mh % 2) * 512
                    pk = "PSBIG_a" if mh % 2 == 0 else "PSBIG_b"

                    def mmqk(e, mh=mh, half=half, wkey=wkey):
                        last = None
                        for k in range(NCH):
                            last = e.matmul(
                                psum[:, half:half + 512],
                                B[wkey][:, k * 256 + 128 * mh:k * 256 + 128 * mh + 128],
                                B["y2g"][:, k * 512:(k + 1) * 512],
                                start=(k == 0), stop=(k == NCH - 1))
                        return last

                    S.comp("tensor", mmqk, reads=(wkey, "y2g"), writes=(pk,))
                    S.comp("scalar", lambda e, which=which, mh=mh, half=half, g0=g0: e.copy(
                        out=B[which][:, mh * 8192 + g0:mh * 8192 + g0 + 512],
                        in_=psum[:, half:half + 512]),
                        reads=(pk,), writes=(which,))
            for tt in range(4):
                half = (tt % 2) * 512
                pk = "PSBIG_a" if tt % 2 == 0 else "PSBIG_b"

                def mmv(e, tt=tt, half=half):
                    last = None
                    for k in range(NCH):
                        last = e.matmul(
                            psum[:, half:half + 256],
                            B["y2g"][:, k * 512 + tt * 128:k * 512 + tt * 128 + 128],
                            B["wv_sb"][:, k * 256:(k + 1) * 256],
                            start=(k == 0), stop=(k == NCH - 1))
                    return last

                S.comp("tensor", mmv, reads=("wv_sb", "y2g"), writes=(pk,))
                S.comp("scalar", lambda e, g=g, tt=tt, half=half: e.copy(
                    out=B["vtm"][:, (g * 4 + tt) * 256:(g * 4 + tt) * 256 + 256],
                    in_=psum[:, half:half + 256]),
                    reads=(pk,), writes=("vtm",))

        for h in range(4):
            hp, hl = h // 2, h % 2
            p0 = 64 * hl
            for which, bcol in (("qfm", "bq4"), ("kfm", "bk4")):
                for half_tok in range(2):
                    c0 = hp * 8192 + half_tok * 4096
                    S.comp("gpsimd", lambda e, which=which, p0=p0, c0=c0: e.tensor_mul(
                        out=B["sqbuf"][0:64, 0:4096],
                        in0=B[which][p0:p0 + 64, c0:c0 + 4096],
                        in1=B[which][p0:p0 + 64, c0:c0 + 4096]),
                        reads=(which,), writes=("sqbuf",))
                    for nt in range(8):
                        halfp = (nt % 2) * 512
                        pk = "PSBIG_a" if nt % 2 == 0 else "PSBIG_b"
                        S.comp("tensor", lambda e, nt=nt, halfp=halfp: e.matmul(
                            psum[:, halfp:halfp + 512], B["ones128"][0:64, 0:128],
                            B["sqbuf"][0:64, nt * 512:nt * 512 + 512],
                            start=True, stop=True),
                            reads=("sqbuf", "ones128"), writes=(pk,))
                        S.comp("vector", lambda e, nt=nt, halfp=halfp, half_tok=half_tok: e.reduce_max(
                            out=B["b_nt"][:, half_tok * 8 + nt:half_tok * 8 + nt + 1],
                            in_=psum[:, halfp:halfp + 512], axis=AX.X),
                            reads=(pk,), writes=("b_nt",))
                S.comp("vector", lambda e, bcol=bcol, h=h: e.reduce_max(
                    out=B[bcol][:, h:h + 1], in_=B["b_nt"][:, 0:16], axis=AX.X),
                    reads=("b_nt",), writes=(bcol,))
            S.comp("vector", lambda e, h=h: e.tensor_tensor(
                out=B["bq4"][:, h:h + 1], in0=B["bq4"][:, h:h + 1],
                in1=B["bk4"][:, h:h + 1], op=ALU.mult),
                reads=("bq4", "bk4"), writes=("bq4",))
            S.comp("scalar", lambda e, h=h: e.activation(
                out=B["bk4"][:, h:h + 1], in_=B["bq4"][:, h:h + 1], func=AF.Ln,
                bias=0.0, scale=1.0),
                reads=("bq4",), writes=("bk4",))
            S.comp("scalar", lambda e, h=h: e.activation(
                out=B["bq4"][:, h:h + 1], in_=B["bk4"][:, h:h + 1], func=AF.Exp,
                bias=0.0, scale=B["c_ph"][:]),
                reads=("bk4", "c_ph"), writes=("bq4",))
            S.comp("vector", lambda e, h=h: e.tensor_scalar(
                out=B["negB4"][:, h:h + 1], in0=B["bq4"][:, h:h + 1],
                scalar1=-0.125, scalar2=None, op0=ALU.mult),
                reads=("bq4",), writes=("negB4",))

        for hp in range(2):
            for qb in range(16):
                nkt = 4 * qb + 4
                S.dma([(B["thr_pc"][0:1, :], thr[0:1, qb * 512:qb * 512 + 512])],
                      writes=("thr_pc",))
                S.comp("tensor", lambda e: e.matmul(
                    psum[:, 0:512], B["ones_f32"][0:1, 0:128], B["thr_pc"][0:1, :],
                    start=True, stop=True),
                    reads=("thr_pc", "ones_f32"), writes=("PSBIG_a",))
                S.comp("vector", lambda e: e.tensor_copy(
                    out=B["thrb"][:], in_=psum[:, 0:512]),
                    reads=("PSBIG_a",), writes=("thrb",))
                for hl in range(2):
                    h = 2 * hp + hl
                    p0 = 64 * hl
                    prev = None
                    for pr in range(nkt // 2):
                        kts = [2 * pr, 2 * pr + 1]
                        pb, pk = (psum, ("PSBIG_a", "PSBIG_b")) if pr % 2 == 0 else                                  (B["PS2"], ("PS2_a", "PS2_b"))
                        ak = "attnT0" if pr % 2 == 0 else "attnT1"
                        at = B[ak]

                        def sc_body(e, kts=kts, qb=qb, hp=hp, p0=p0, pb=pb):
                            last = None
                            for j, kt in enumerate(kts):
                                last = e.matmul(
                                    pb[:, j * 512:(j + 1) * 512],
                                    B["kfm"][p0:p0 + 64, hp * 8192 + kt * 128:hp * 8192 + kt * 128 + 128],
                                    B["qfm"][p0:p0 + 64, hp * 8192 + qb * 512:hp * 8192 + qb * 512 + 512],
                                    start=True, stop=True)
                            return last

                        S.comp("tensor", sc_body, reads=("kfm", "qfm"), writes=pk)
                        S.comp("scalar", lambda e, at=at, h=h, pb=pb: e.activation(
                            out=at[:, 0:1024], in_=pb[:, 0:1024], func=AF.Exp,
                            bias=B["negB4"][:, h:h + 1], scale=B["c_sc"][:]),
                            reads=pk + ("negB4", "c_sc"), writes=(ak,))
                        for j, kt in enumerate(kts):
                            if kt >= max(4 * qb, 1):
                                S.comp("vector", lambda e, kt=kt: e.tensor_scalar(
                                    out=B["mask_sb"][:], in0=B["thrb"][:],
                                    scalar1=B["kcol"][:, kt:kt + 1], scalar2=None,
                                    op0=ALU.is_ge),
                                    reads=("thrb", "kcol"), writes=("mask_sb",))
                                S.comp("vector", lambda e, at=at, j=j: e.tensor_tensor(
                                    out=at[:, j * 512:(j + 1) * 512],
                                    in0=at[:, j * 512:(j + 1) * 512], in1=B["mask_sb"][:],
                                    op=ALU.mult),
                                    reads=(ak, "mask_sb"), writes=(ak,))
                        if prev is not None:
                            emit_c1_ctx(S, B, prev[0], prev[1], p0, h, nkt)
                        prev = (kts, ak)
                    emit_c1_ctx(S, B, prev[0], prev[1], p0, h, nkt)
                cb = "ctxbuf0" if qb % 2 == 0 else "ctxbuf1"
                S.comp("scalar", lambda e: e.activation(
                    out=B["lnd"][:], in_=B["PDEN"][:, 0:512], func=AF.Ln,
                    bias=0.0, scale=1.0),
                    reads=("PDEN",), writes=("lnd",))
                S.comp("scalar", lambda e: e.activation(
                    out=B["rden"][:], in_=B["lnd"][:], func=AF.Exp,
                    bias=0.0, scale=B["c_neg1"][:]),
                    reads=("lnd", "c_neg1"), writes=("rden",))
                S.comp("vector", lambda e, cb=cb: e.tensor_tensor(
                    out=B[cb][:], in0=B["PCTX"][:, 0:512], in1=B["rden"][:],
                    op=ALU.mult),
                    reads=("PCTX", "rden"), writes=(cb,))
                S.dma([(out_t[hp * 128:(hp + 1) * 128, qb * 512:qb * 512 + 512],
                        B[cb][:])],
                      reads=(cb,), writes=("out",))

        S.barrier_wait("sync", ("out",))
        S.finalize()
    return nc


def c1_ctx_body(e, B, kts, ak, p0, h, nkt):
    at = B[ak]
    last = None
    for j, kt in enumerate(kts):
        e.matmul(B["PCTX"][p0:p0 + 64, 0:512],
                 B["vtm"][:, kt * 256 + 64 * h:kt * 256 + 64 * h + 64],
                 at[:, j * 512:(j + 1) * 512],
                 start=(kt == 0), stop=(kt == nkt - 1))
        last = e.matmul(B["PDEN"][p0:p0 + 64, 0:512],
                        B["ones128"][:, 0:64],
                        at[:, j * 512:(j + 1) * 512],
                        start=(kt == 0), stop=(kt == nkt - 1))
    return last


def emit_c1_ctx(S, B, kts, ak, p0, h, nkt):
    S.comp("tensor", lambda e: c1_ctx_body(e, B, kts, ak, p0, h, nkt),
           reads=(ak, "vtm", "ones128"), writes=("PCTX", "PDEN"))


def run_C1(x2_fm, w_q, w_k, w_v):
    nc = build_C1()
    wqT = _fmT(w_q)
    wkT = _fmT(w_k)
    wvT = _fmT(w_v)
    thr = np.maximum(np.arange(8192, dtype=np.float32), 195.0).reshape(1, 8192)
    in_maps = []
    for c in range(8):
        b, hq = c // 4, c % 4
        in_maps.append({
            "x2": np.ascontiguousarray(x2_fm[b]),
            "wq": np.ascontiguousarray(wqT[:, 256 * hq:256 * (hq + 1)]),
            "wk": np.ascontiguousarray(wkT[:, 256 * hq:256 * (hq + 1)]),
            "wv": np.ascontiguousarray(wvT[:, 256 * hq:256 * (hq + 1)]),
            "thr": thr,
        })
    res = run_bass_kernel_spmd(nc, in_maps, core_ids=list(range(8)))
    ctx = np.empty((2, D, 8192), dtype=BF)
    for c in range(8):
        b, hq = c // 4, c % 4
        ctx[b][256 * hq:256 * (hq + 1), :] = np.asarray(res.results[c]["out"])
    return ctx



def build_C2():
    nc = bass.Bass("TRN2", target_bir_lowering=False)
    es = ExitStack()
    ctx = nc.dram_tensor("ctx", [D, 2048], BF16, kind="ExternalInput").ap()
    x2 = nc.dram_tensor("x2", [D, 2048], F32, kind="ExternalInput").ap()
    wo = nc.dram_tensor("wo", [D, D], F32, kind="ExternalInput").ap()
    bo = nc.dram_tensor("bo", [128, 8], F32, kind="ExternalInput").ap()
    out_t = nc.dram_tensor("out", [D, 2048], F32, kind="ExternalOutput").ap()

    with es:
        B = Buffers(nc, es)
        S = Sched(nc, es)
        alloc_common(B)
        alloc_ln(B, 512)
        B.sb("ctx_sb", (128, NCH * 2048), BF16)
        B.sb("wo_sb", (128, NCH * D), BF16)
        B.sb("bo_sb", (128, 8), F32)
        B.sb("dstbuf", (128, NCH * 2048), BF16)
        B.sb("xg", (128, NCH * 512), F32)
        B.sb("res_t", (128, 512), F32)
        psum = B["PSBIG"]

        emit_init_consts(S, B)
        emit_weight_load(S, B, wo, B["wo_sb"], NCH, D, "wo_sb")
        S.dma([(B["bo_sb"][:], bo)], writes=("bo_sb",))
        S.dma([(B["ctx_sb"][:, k * 2048:(k + 1) * 2048],
                ctx[128 * k:128 * (k + 1), :]) for k in range(NCH)],
              writes=("ctx_sb",))
        for nt in range(4):
            for m in range(NCH):
                half = ((nt * NCH + m) % 2) * 512
                pk = "PSBIG_a" if (nt * NCH + m) % 2 == 0 else "PSBIG_b"

                def mmo(e, nt=nt, m=m, half=half):
                    last = None
                    for k in range(NCH):
                        last = e.matmul(
                            psum[:, half:half + 512],
                            B["wo_sb"][:, k * D + 128 * m:k * D + 128 * m + 128],
                            B["ctx_sb"][:, k * 2048 + nt * 512:k * 2048 + nt * 512 + 512],
                            start=(k == 0), stop=(k == NCH - 1))
                    return last

                S.comp("tensor", mmo, reads=("wo_sb", "ctx_sb"), writes=(pk,))
                S.comp("scalar", lambda e, nt=nt, m=m, half=half: e.add(
                    out=B["dstbuf"][:, m * 2048 + nt * 512:m * 2048 + nt * 512 + 512],
                    in_=psum[:, half:half + 512], add=B["bo_sb"][:, m:m + 1]),
                    reads=(pk, "bo_sb"), writes=("dstbuf",))
        for g in range(4):
            g0 = g * 512
            emit_tail_group(S, B, lambda k, g0=g0: B["dstbuf"][:, k * 2048 + g0:
                                                              k * 2048 + g0 + 512],
                            ("dstbuf",), 512, x2, out_t, g0)
        S.barrier_wait("sync", ("out",))
        S.finalize()
    return nc


def run_C2(ctx_fm_bf, x2_fm, w_o, b_o):
    nc = build_C2()
    woT = _fmT(w_o)
    bop = _bias_p(b_o)
    in_maps = []
    for c in range(8):
        b, q = c // 4, c % 4
        in_maps.append({
            "ctx": np.ascontiguousarray(ctx_fm_bf[b][:, 2048 * q:2048 * (q + 1)]),
            "x2": np.ascontiguousarray(x2_fm[b][:, 2048 * q:2048 * (q + 1)]),
            "wo": woT, "bo": bop,
        })
    res = run_bass_kernel_spmd(nc, in_maps, core_ids=list(range(8)))
    out = np.empty((2, D, 8192), dtype=np.float32)
    for c in range(8):
        b, q = c // 4, c % 4
        out[b][:, 2048 * q:2048 * (q + 1)] = np.asarray(res.results[c]["out"])
    return out



D_BLK = 6 * 128


def build_D():
    nc = bass.Bass("TRN2", target_bir_lowering=False)
    es = ExitStack()
    HID = 3072
    x3 = nc.dram_tensor("x3", [D, 2048], F32, kind="ExternalInput").ap()
    wg_t = nc.dram_tensor("wg_t", [D, HID], I32, kind="ExternalInput").ap()
    wu_t = nc.dram_tensor("wu_t", [D, HID], I32, kind="ExternalInput").ap()
    wd_t = nc.dram_tensor("wd_t", [HID, D], I32, kind="ExternalInput").ap()
    out_t = nc.dram_tensor("out", [D, 2048], F32, kind="ExternalOutput").ap()

    with es:
        B = Buffers(nc, es)
        S = Sched(nc, es)
        alloc_common(B)
        alloc_ln(B, 512)
        B.sb("y3", (128, NCH * 1024), BF16)
        B.sb("acc", (128, NCH * 1024), F32)
        B.sb("accbf", (128, NCH * 512), BF16)
        B.sb("xg", (128, NCH * 512), F32)
        B.sb("res_t", (128, 512), F32)
        B.sb("istage0", (128, 512), I32)
        B.sb("istage1", (128, 512), I32)
        for p in range(2):
            B.sb(f"wgb{p}", (128, NCH * 768), BF16)
            B.sb(f"wub{p}", (128, NCH * 768), BF16)
            B.sb(f"wdb{p}", (128, 6 * D), BF16)
        B.sb("hbuf", (128, 6 * 1024), BF16)
        B.sb("g1sb", (128, 512), BF16)
        B.sb("g3sb", (128, 512), BF16)
        B.sb("c127", (128, 1), F32)
        psum = B["PSBIG"]

        emit_init_consts(S, B, [("c127", 1.0 / 127.0)])

        for th in range(2):
            t0 = th * 1024
            for g in range(2):
                g0 = t0 + g * 512
                S.dma([(B["xg"][:, k * 512:(k + 1) * 512],
                        x3[128 * k:128 * (k + 1), g0:g0 + 512]) for k in range(NCH)],
                      writes=("xg",))
                S.comp("vector", lambda e: e.tensor_copy(
                    out=B["accbf"][:, 0:NCH * 512], in_=B["xg"][:, 0:NCH * 512]),
                    reads=("xg",), writes=("accbf",))
                emit_ln_stats(S, B, lambda k: B["accbf"][:, k * 512:(k + 1) * 512],
                              ("accbf",), 512)
                emit_ln_apply_bf(S, B, lambda k: B["accbf"][:, k * 512:(k + 1) * 512],
                                 ("accbf",), 512,
                                 lambda k, g=g: B["y3"][:, k * 1024 + g * 512:
                                                        k * 1024 + g * 512 + 512],
                                 "y3")
            for blk in range(4):
                p = blk % 2
                wgk, wuk, wdk = f"wgb{p}", f"wub{p}", f"wdb{p}"
                c0 = blk * 768
                for k in range(NCH):
                    for p0 in (0, 512):
                        pw = 512 if p0 == 0 else 256
                        stg = "istage0" if (k + p0 // 512) % 2 == 0 else "istage1"
                        S.dma([(B[stg][:, 0:pw],
                                wg_t[128 * k:128 * (k + 1), c0 + p0:c0 + p0 + pw])],
                              writes=(stg,))
                        S.comp("gpsimd", lambda e, k=k, p0=p0, pw=pw, stg=stg, wgk=wgk: e.tensor_copy(
                            out=B[wgk][:, k * 768 + p0:k * 768 + p0 + pw],
                            in_=B[stg][:, 0:pw]),
                            reads=(stg,), writes=(wgk,))
                        stg2 = "istage0" if (k + p0 // 512) % 2 == 1 else "istage1"
                        S.dma([(B[stg2][:, 0:pw],
                                wu_t[128 * k:128 * (k + 1), c0 + p0:c0 + p0 + pw])],
                              writes=(stg2,))
                        S.comp("gpsimd", lambda e, k=k, p0=p0, pw=pw, stg2=stg2, wuk=wuk: e.tensor_copy(
                            out=B[wuk][:, k * 768 + p0:k * 768 + p0 + pw],
                            in_=B[stg2][:, 0:pw]),
                            reads=(stg2,), writes=(wuk,))
                for j in range(6):
                    for p0 in (0, 512):
                        stg = "istage0" if (j + p0 // 512) % 2 == 0 else "istage1"
                        S.dma([(B[stg][:, 0:512],
                                wd_t[(blk * 6 + j) * 128:(blk * 6 + j) * 128 + 128,
                                     p0:p0 + 512])],
                              writes=(stg,))
                        S.comp("gpsimd", lambda e, j=j, p0=p0, stg=stg, wdk=wdk: e.tensor_copy(
                            out=B[wdk][:, j * D + p0:j * D + p0 + 512],
                            in_=B[stg][:, 0:512]),
                            reads=(stg,), writes=(wdk,))
                for mc in range(6):
                    for nt in range(2):
                        def mmg(e, mc=mc, nt=nt, wgk=wgk):
                            last = None
                            for k in range(NCH):
                                last = e.matmul(
                                    psum[:, 0:512],
                                    B[wgk][:, k * 768 + 128 * mc:k * 768 + 128 * mc + 128],
                                    B["y3"][:, k * 1024 + nt * 512:k * 1024 + nt * 512 + 512],
                                    start=(k == 0), stop=(k == NCH - 1))
                            return last

                        S.comp("tensor", mmg, reads=(wgk, "y3"), writes=("PSBIG_a",))
                        S.comp("scalar", lambda e: e.activation(
                            out=B["g1sb"][:], in_=psum[:, 0:512], func=AF.Silu,
                            bias=0.0, scale=B["c127"][:]),
                            reads=("PSBIG_a", "c127"), writes=("g1sb",))

                        def mmu(e, mc=mc, nt=nt, wuk=wuk):
                            last = None
                            for k in range(NCH):
                                last = e.matmul(
                                    psum[:, 512:1024],
                                    B[wuk][:, k * 768 + 128 * mc:k * 768 + 128 * mc + 128],
                                    B["y3"][:, k * 1024 + nt * 512:k * 1024 + nt * 512 + 512],
                                    start=(k == 0), stop=(k == NCH - 1))
                            return last

                        S.comp("tensor", mmu, reads=(wuk, "y3"), writes=("PSBIG_b",))
                        S.comp("scalar", lambda e: e.copy(
                            out=B["g3sb"][:], in_=psum[:, 512:1024]),
                            reads=("PSBIG_b",), writes=("g3sb",))
                        S.comp("vector", lambda e, mc=mc, nt=nt: e.tensor_tensor(
                            out=B["hbuf"][:, mc * 1024 + nt * 512:mc * 1024 + nt * 512 + 512],
                            in0=B["g1sb"][:], in1=B["g3sb"][:], op=ALU.mult),
                            reads=("g1sb", "g3sb"), writes=("hbuf",))
                for nt in range(2):
                    for m in range(NCH):
                        half = ((nt * NCH + m) % 2) * 512
                        pk = "PSBIG_a" if (nt * NCH + m) % 2 == 0 else "PSBIG_b"

                        def mmd(e, nt=nt, m=m, half=half, wdk=wdk):
                            last = None
                            for j in range(6):
                                last = e.matmul(
                                    psum[:, half:half + 512],
                                    B[wdk][:, j * D + 128 * m:j * D + 128 * m + 128],
                                    B["hbuf"][:, j * 1024 + nt * 512:j * 1024 + nt * 512 + 512],
                                    start=(j == 0), stop=(j == 5))
                            return last

                        S.comp("tensor", mmd, reads=(wdk, "hbuf"), writes=(pk,))
                        if blk == 0:
                            S.comp("vector", lambda e, nt=nt, m=m, half=half: e.tensor_scalar(
                                out=B["acc"][:, m * 1024 + nt * 512:m * 1024 + nt * 512 + 512],
                                in0=psum[:, half:half + 512],
                                scalar1=1.0 / (127.0 * 127.0), scalar2=None, op0=ALU.mult),
                                reads=(pk,), writes=("acc",))
                        else:
                            S.comp("vector", lambda e, nt=nt, m=m, half=half: e.scalar_tensor_tensor(
                                out=B["acc"][:, m * 1024 + nt * 512:m * 1024 + nt * 512 + 512],
                                in0=psum[:, half:half + 512], scalar=1.0 / (127.0 * 127.0),
                                in1=B["acc"][:, m * 1024 + nt * 512:m * 1024 + nt * 512 + 512],
                                op0=ALU.mult, op1=ALU.add),
                                reads=(pk, "acc"), writes=("acc",))
            for g in range(2):
                g0 = t0 + g * 512
                for k in range(NCH):
                    S.comp("vector", lambda e, k=k, g=g: e.tensor_copy(
                        out=B["accbf"][:, k * 512:(k + 1) * 512],
                        in_=B["acc"][:, k * 1024 + g * 512:k * 1024 + g * 512 + 512]),
                        reads=("acc",), writes=("accbf",))
                emit_tail_group(S, B, lambda k: B["accbf"][:, k * 512:(k + 1) * 512],
                                ("accbf",), 512, x3, out_t, g0)

        S.barrier_wait("sync", ("out",))
        S.finalize()
    return nc


def run_D(x3_fm, w_gate, w_up, w_down):
    nc = build_D()
    wg_t = np.ascontiguousarray(np.asarray(w_gate).T)
    wu_t = np.ascontiguousarray(np.asarray(w_up).T)
    wd_t = np.ascontiguousarray(np.asarray(w_down).T)
    in_maps = []
    for c in range(8):
        b, q = c // 4, c % 4
        in_maps.append({
            "x3": np.ascontiguousarray(x3_fm[b][:, 2048 * q:2048 * (q + 1)]),
            "wg_t": wg_t, "wu_t": wu_t, "wd_t": wd_t,
        })
    res = run_bass_kernel_spmd(nc, in_maps, core_ids=list(range(8)))
    out = np.empty((2, D, 8192), dtype=np.float32)
    for c in range(8):
        b, q = c // 4, c % 4
        out[b][:, 2048 * q:2048 * (q + 1)] = np.asarray(res.results[c]["out"])
    return out



def kernel(**inputs):
    g = lambda n: inputs[n]
    x = np.asarray(g("L0_attn_x"), np.float32)
    x_fm = np.ascontiguousarray(x.transpose(0, 2, 1))

    l0 = run_A(0, x_fm, None, g("L0_attn_Win0"), g("L0_attn_bin0"),
               g("L0_attn_Wout0"), g("L0_attn_bout0"))
    l1 = run_A(1, x_fm, l0, g("L0_attn_Win1"), g("L0_attn_bin1"),
               g("L0_attn_Wout1"), g("L0_attn_bout1"))
    x1 = run_A(2, x_fm, l1, g("L0_attn_Win2"), g("L0_attn_bin2"),
               g("L0_attn_Wout2"), g("L0_attn_bout2"))
    x2 = run_B(x1, g("L0_ffn_gate_w"), g("L0_ffn_gate_b"), g("L0_ffn_w1"),
               g("L0_ffn_b1"), g("L0_ffn_w3"), g("L0_ffn_b3"), g("L0_ffn_w2"),
               g("L0_ffn_b2"), g("L0_ffn_sw1"), g("L0_ffn_sb1"), g("L0_ffn_sw3"),
               g("L0_ffn_sb3"), g("L0_ffn_sw2"), g("L0_ffn_sb2"))
    ctx = run_C1(x2, g("L1_attn_w_q"), g("L1_attn_w_k"), g("L1_attn_w_v"))
    x3 = run_C2(ctx, x2, g("L1_attn_w_o"), g("L1_attn_b_o"))
    y = run_D(x3, g("L1_ffn_w_gate"), g("L1_ffn_w_up"), g("L1_ffn_w_down"))
    return np.ascontiguousarray(y.transpose(0, 2, 1))


# version 07 (final)
# speedup vs baseline: 1.3839x; 1.0135x over previous
import numpy as np
import ml_dtypes
from contextlib import ExitStack

import concourse.bass as bass
import concourse.mybir as mybir
from concourse.bass_utils import run_bass_kernel_spmd

F32 = mybir.dt.float32
BF16 = mybir.dt.bfloat16
I32 = mybir.dt.int32
AF = mybir.ActivationFunctionType
ALU = mybir.AluOpType
AX = mybir.AxisListType
BF = ml_dtypes.bfloat16

ENGS = ("sync", "tensor", "vector", "scalar", "gpsimd")
SELF_SER = ("vector", "scalar", "gpsimd")

D = 1024
NCH = 8



class Op:
    __slots__ = ("fn", "waits", "inc")

    def __init__(self, fn, waits, inc):
        self.fn = fn
        self.waits = waits
        self.inc = inc


class Sched:

    N_DMA_POOL = 8

    def __init__(self, nc, es):
        self.nc = nc
        self.ops = {e: [] for e in ENGS}
        self.val = {e: 0 for e in ENGS}
        self.seq = {e: es.enter_context(nc.semaphore("seq_" + e)) for e in ENGS}
        self.dpool = [es.enter_context(nc.semaphore(f"dpool{j}"))
                      for j in range(self.N_DMA_POOL)]
        self.dpool_tot = [0] * self.N_DMA_POOL
        self.dpool_idx = 0
        self.last_w = {}
        self.readers = {}
        self.waited = {e: {} for e in ENGS}

    def _sem_of(self, name):
        if name.startswith("@d"):
            return self.dpool[int(name[2:])]
        return self.seq[name]

    def _add_wait(self, eng, dep, waits):
        deng, dval = dep
        if deng == eng:
            return
        if self.waited[eng].get(deng, 0) >= dval:
            return
        self.waited[eng][deng] = dval
        waits.append((self._sem_of(deng), dval))

    def _book(self, eng, fn, reads, writes, inc, completion=None, pre_waits=()):
        waits = list(pre_waits)
        if eng in SELF_SER and self.val[eng] > 0:
            waits.append((self.seq[eng], self.val[eng]))
        for r in reads:
            lw = self.last_w.get(r)
            if lw is not None:
                self._add_wait(eng, lw, waits)
        for w in writes:
            lw = self.last_w.get(w)
            if lw is not None:
                self._add_wait(eng, lw, waits)
            for reng, rval in self.readers.get(w, {}).items():
                self._add_wait(eng, (reng, rval), waits)
        self.val[eng] += inc
        comp = completion if completion is not None else (eng, self.val[eng])
        for r in reads:
            self.readers.setdefault(r, {})[comp[0]] = comp[1]
        for w in writes:
            self.last_w[w] = comp
            self.readers[w] = {}
        self.ops[eng].append(Op(fn, waits, inc))

    def dma(self, transfers, reads=(), writes=(), queue="sync"):
        j = self.dpool_idx
        self.dpool_idx = (j + 1) % self.N_DMA_POOL
        sem = self.dpool[j]
        pre = []
        if self.dpool_tot[j] > 0:
            self._add_wait(queue, (f"@d{j}", self.dpool_tot[j]), pre)
        self.dpool_tot[j] += 16 * len(transfers)
        tot = self.dpool_tot[j]

        def fn(eng):
            for dst, src in transfers:
                eng.dma_start(out=dst, in_=src).then_inc(sem, 16)

        self._book(queue, fn, reads, writes, 0,
                   completion=(f"@d{j}", tot), pre_waits=pre)

    def comp(self, eng, fn, reads=(), writes=()):
        sem = self.seq[eng]

        def wrapped(e):
            last = fn(e)
            last.then_inc(sem, 1)

        self._book(eng, wrapped, reads, writes, 1)

    def barrier_wait(self, eng, keys):
        waits = []
        for r in keys:
            lw = self.last_w.get(r)
            if lw is not None:
                self._add_wait(eng, lw, waits)
        if eng in SELF_SER and self.val[eng] > 0:
            waits.append((self.seq[eng], self.val[eng]))
        self.ops[eng].append(Op(None, waits, 0))

    def finalize(self):
        nc = self.nc
        with nc.Block() as block:
            for e in ENGS:
                ops = self.ops[e]
                if not ops:
                    continue

                def runner(eng, ops=ops):
                    for op in ops:
                        for sem, v in op.waits:
                            eng.wait_ge(sem, v)
                        if op.fn is not None:
                            op.fn(eng)

                getattr(block, e)(runner)


class Buffers:
    def __init__(self, nc, es):
        self.nc = nc
        self.es = es
        self.t = {}

    def sb(self, name, shape, dtype):
        h = self.es.enter_context(self.nc.sbuf_tensor("sb_" + name, list(shape), dtype))
        self.t[name] = h
        return h

    def ps(self, name, shape, dtype=F32):
        h = self.es.enter_context(self.nc.psum_tensor("ps_" + name, list(shape), dtype))
        self.t[name] = h
        return h

    def __getitem__(self, name):
        return self.t[name]

    def __contains__(self, name):
        return name in self.t



def emit_weight_load(S, B, dram_ap, sb_bf, n_chunks, width, tag):
    piece = 512
    idx = 0
    for k in range(n_chunks):
        for p0 in range(0, width, piece):
            pw = min(piece, width - p0)
            stg = "wstage0" if idx % 2 == 0 else "wstage1"
            idx += 1
            S.dma([(B[stg][:, 0:pw], dram_ap[128 * k:128 * (k + 1), p0:p0 + pw])],
                  writes=(stg,))
            S.comp("gpsimd",
                   lambda e, k=k, p0=p0, pw=pw, stg=stg: e.tensor_copy(
                       out=sb_bf[:, width * k + p0:width * k + p0 + pw],
                       in_=B[stg][:, 0:pw]),
                   reads=(stg,), writes=(tag,))


def emit_flat_bias_load(S, B, dram_ap, sb_bf, width, tag):
    piece = 512
    idx = 0
    for p0 in range(0, width, piece):
        pw = min(piece, width - p0)
        stg = "wstage0" if idx % 2 == 0 else "wstage1"
        idx += 1
        S.dma([(B[stg][0:1, 0:pw], dram_ap[0:1, p0:p0 + pw])], writes=(stg,))
        S.comp("gpsimd", lambda e, p0=p0, pw=pw, stg=stg: e.tensor_copy(
            out=sb_bf[0:1, p0:p0 + pw], in_=B[stg][0:1, 0:pw]),
            reads=(stg,), writes=(tag,))


def emit_ln_stats(S, B, chunk_ap_fn, src_keys, width):
    psum = B["PSBIG"]

    def mm_sum(e):
        last = None
        for k in range(NCH):
            last = e.matmul(psum[:, 0:width], B["ones128"][:, 0:128],
                            chunk_ap_fn(k), start=(k == 0), stop=(k == NCH - 1))
        return last

    S.comp("tensor", mm_sum, reads=tuple(src_keys) + ("ones128",), writes=("PSBIG_a",))

    def sq(e):
        last = None
        for k in range(NCH):
            last = e.tensor_mul(out=B["sqbuf"][:, k * width:(k + 1) * width],
                                in0=chunk_ap_fn(k), in1=chunk_ap_fn(k))
        return last

    S.comp("gpsimd", sq, reads=tuple(src_keys), writes=("sqbuf",))

    def mm_sq(e):
        last = None
        for k in range(NCH):
            last = e.matmul(psum[:, 512:512 + width], B["ones128"][:, 0:128],
                            B["sqbuf"][:, k * width:(k + 1) * width],
                            start=(k == 0), stop=(k == NCH - 1))
        return last

    S.comp("tensor", mm_sq, reads=("sqbuf", "ones128"), writes=("PSBIG_b",))
    S.comp("vector", lambda e: e.tensor_scalar(out=B["st_mu"][:, 0:width],
                                               in0=psum[:, 0:width],
                                               scalar1=1.0 / D, scalar2=None, op0=ALU.mult),
           reads=("PSBIG_a",), writes=("st_mu",))
    S.comp("vector", lambda e: e.tensor_scalar(out=B["st_t1"][:, 0:width],
                                               in0=psum[:, 512:512 + width],
                                               scalar1=1.0 / D, scalar2=None, op0=ALU.mult),
           reads=("PSBIG_b",), writes=("st_t1",))
    S.comp("vector", lambda e: e.tensor_tensor(out=B["st_t2"][:, 0:width],
                                               in0=B["st_mu"][:, 0:width],
                                               in1=B["st_mu"][:, 0:width], op=ALU.mult),
           reads=("st_mu",), writes=("st_t2",))
    S.comp("vector", lambda e: e.tensor_tensor(out=B["st_t1"][:, 0:width],
                                               in0=B["st_t1"][:, 0:width],
                                               in1=B["st_t2"][:, 0:width], op=ALU.subtract),
           reads=("st_t1", "st_t2"), writes=("st_t1",))
    S.comp("scalar", lambda e: e.activation(out=B["st_t2"][:, 0:width],
                                            in_=B["st_t1"][:, 0:width],
                                            func=AF.Ln, bias=B["c_eps"][:], scale=1.0),
           reads=("st_t1", "c_eps"), writes=("st_t2",))
    S.comp("scalar", lambda e: e.activation(out=B["st_r"][:, 0:width],
                                            in_=B["st_t2"][:, 0:width],
                                            func=AF.Exp, bias=0.0, scale=B["c_nh"][:]),
           reads=("st_t2", "c_nh"), writes=("st_r",))
    S.comp("vector", lambda e: e.tensor_copy(out=B["st_mu_bf"][:, 0:width],
                                             in_=B["st_mu"][:, 0:width]),
           reads=("st_mu",), writes=("st_mu_bf",))
    S.comp("vector", lambda e: e.tensor_copy(out=B["st_r_bf"][:, 0:width],
                                             in_=B["st_r"][:, 0:width]),
           reads=("st_r",), writes=("st_r_bf",))


def emit_stats_tm(S, B, width, col0):
    ntt = width // 128
    for tt in range(ntt):
        S.comp("tensor", lambda e, tt=tt: e.transpose(
            B["PSTR"][:], B["st_mu"][:, tt * 128:(tt + 1) * 128], B["ident_f32"][:]),
            reads=("st_mu", "ident_f32"), writes=("PSTR",))
        S.comp("vector", lambda e, tt=tt: e.tensor_copy(
            out=B["st_mu_tm"][:, col0 + tt:col0 + tt + 1], in_=B["PSTR"][:, 0:1]),
            reads=("PSTR",), writes=("st_mu_tm",))
        S.comp("tensor", lambda e, tt=tt: e.transpose(
            B["PSTR"][:], B["st_r"][:, tt * 128:(tt + 1) * 128], B["ident_f32"][:]),
            reads=("st_r", "ident_f32", "st_mu_tm"), writes=("PSTR",))
        S.comp("vector", lambda e, tt=tt: e.tensor_copy(
            out=B["st_r_tm"][:, col0 + tt:col0 + tt + 1], in_=B["PSTR"][:, 0:1]),
            reads=("PSTR",), writes=("st_r_tm",))


def emit_ln_apply_bf(S, B, chunk_ap_fn, src_keys, width, dst_ap_fn, dst_key):
    for k in range(NCH):
        S.comp("vector", lambda e, k=k: e.tensor_tensor(
            out=B["ln_t"][:, 0:width], in0=chunk_ap_fn(k),
            in1=B["st_mu_bf"][:, 0:width], op=ALU.subtract),
            reads=tuple(src_keys) + ("st_mu_bf",), writes=("ln_t",))
        S.comp("vector", lambda e, k=k: e.tensor_tensor(
            out=dst_ap_fn(k), in0=B["ln_t"][:, 0:width],
            in1=B["st_r_bf"][:, 0:width], op=ALU.mult),
            reads=("ln_t", "st_r_bf"), writes=(dst_key,))


def emit_mha(S, B, src_bf, src_key, src_col0, T, dst_write_fn,
             w_key="w_in", wo_key="w_out", scale=1.0 / 32.0, cscale_key="c_sc",
             n_heads=8):
    src_stride = src_bf.shape[1] // NCH
    n_nt = (T + 511) // 512
    ntt = T // 128
    psum = B["PSBIG"]
    w_in = B["w_in"]
    w_out = B["w_out"]
    ctx_stride = B["ctxfm"].shape[1] // n_heads

    for h in range(n_heads):
        for which, off, bias_col in (("qh", 128 * h, h), ("kh", D + 128 * h, 8 + h)):
            for nt in range(n_nt):
                tw = min(512, T - nt * 512)
                half = (nt % 2) * 512
                pk = "PSBIG_a" if nt % 2 == 0 else "PSBIG_b"

                def mmq(e, off=off, nt=nt, tw=tw, half=half):
                    last = None
                    for k in range(NCH):
                        last = e.matmul(
                            psum[:, half:half + tw],
                            w_in[:, k * 3 * D + off:k * 3 * D + off + 128],
                            src_bf[:, k * src_stride + src_col0 + nt * 512:
                                   k * src_stride + src_col0 + nt * 512 + tw],
                            start=(k == 0), stop=(k == NCH - 1))
                    return last

                S.comp("tensor", mmq, reads=(w_key, src_key), writes=(pk,))
                S.comp("scalar", lambda e, which=which, nt=nt, tw=tw, half=half,
                       bias_col=bias_col: e.add(
                           out=B[which][:, nt * 512:nt * 512 + tw],
                           in_=psum[:, half:half + tw],
                           add=B["b_in_p"][:, bias_col:bias_col + 1]),
                       reads=(pk, "b_in_p"), writes=(which,))
        for tg in range(0, ntt, 4):
            tts = [t for t in range(tg, min(tg + 4, ntt))]
            par = (tg // 4) % 2
            pb = psum if par == 0 else B["PS2"]
            pk = ("PSBIG_a", "PSBIG_b") if par == 0 else ("PS2_a", "PS2_b")

            def mmv(e, tts=tts, h=h, pb=pb):
                last = None
                for i, tt in enumerate(tts):
                    for k in range(NCH):
                        e.matmul(
                            pb[:, i * 128:i * 128 + 128],
                            src_bf[:, k * src_stride + src_col0 + tt * 128:
                                   k * src_stride + src_col0 + tt * 128 + 128],
                            w_in[:, k * 3 * D + 2 * D + 128 * h:
                                 k * 3 * D + 2 * D + 128 * h + 128],
                            start=(k == 0), stop=False)
                    last = e.matmul(pb[:, i * 128:i * 128 + 128],
                                    B["ones_row"][0:1, 0:128],
                                    B["b_in_f_bf"][0:1, 2 * D + 128 * h:2 * D + 128 * h + 128],
                                    start=False, stop=True)
                return last

            S.comp("tensor", mmv, reads=(w_key, src_key, "ones_row", "b_in_f_bf"),
                   writes=pk)
            S.comp("scalar", lambda e, tg=tg, tts=tts, pb=pb: e.copy(
                out=B["vtm"][:, tg * 128:tg * 128 + len(tts) * 128],
                in_=pb[:, 0:len(tts) * 128]),
                reads=pk, writes=("vtm",))
        for which, bcol in (("qh", "b_q"), ("kh", "b_k")):
            S.comp("gpsimd", lambda e, which=which: e.tensor_mul(
                out=B["sqbuf"][:, 0:T], in0=B[which][:, 0:T], in1=B[which][:, 0:T]),
                reads=(which,), writes=("sqbuf",))
            for nt in range(n_nt):
                tw = min(512, T - nt * 512)
                half = (nt % 2) * 512
                pk = "PSBIG_a" if nt % 2 == 0 else "PSBIG_b"
                S.comp("tensor", lambda e, nt=nt, tw=tw, half=half: e.matmul(
                    psum[:, half:half + tw], B["ones128"][:, 0:128],
                    B["sqbuf"][:, nt * 512:nt * 512 + tw], start=True, stop=True),
                    reads=("sqbuf", "ones128"), writes=(pk,))
                S.comp("vector", lambda e, nt=nt, tw=tw, half=half: e.reduce_max(
                    out=B["b_nt"][:, nt:nt + 1], in_=psum[:, half:half + tw], axis=AX.X),
                    reads=(pk,), writes=("b_nt",))
            S.comp("vector", lambda e, bcol=bcol: e.reduce_max(
                out=B[bcol][:], in_=B["b_nt"][:, 0:n_nt], axis=AX.X),
                reads=("b_nt",), writes=(bcol,))
        S.comp("vector", lambda e: e.tensor_tensor(out=B["b_q"][:], in0=B["b_q"][:],
                                                   in1=B["b_k"][:], op=ALU.mult),
               reads=("b_q", "b_k"), writes=("b_q",))
        S.comp("scalar", lambda e: e.activation(out=B["b_k"][:], in_=B["b_q"][:],
                                                func=AF.Ln, bias=0.0, scale=1.0),
               reads=("b_q",), writes=("b_k",))
        S.comp("scalar", lambda e: e.activation(out=B["b_q"][:], in_=B["b_k"][:],
                                                func=AF.Exp, bias=0.0, scale=B["c_ph"][:]),
               reads=("b_k", "c_ph"), writes=("b_q",))
        S.comp("vector", lambda e: e.tensor_scalar(out=B["negB"][:], in0=B["b_q"][:],
                                                   scalar1=-scale, scalar2=None,
                                                   op0=ALU.mult),
               reads=("b_q",), writes=("negB",))
        nkt = T // 128
        PSETS = ((psum, ("PSBIG_a", "PSBIG_b")), (B["PS2"], ("PS2_a", "PS2_b")),
                 (B["PS3"], ("PS3_a", "PS3_b")))
        for qb in range(n_nt):
            qw = min(512, T - qb * 512)
            npair = (nkt + 1) // 2
            pend = []
            for pr in range(npair):
                kts = [2 * pr + j for j in range(2) if 2 * pr + j < nkt]
                pb, pk = PSETS[pr % 3]
                ak = f"attnT{pr % 3}"
                at = B[ak]

                def mmsc(e, kts=kts, qb=qb, qw=qw, pb=pb):
                    return mm_scores_body(e, B, kts, qb, qw, pb)

                S.comp("tensor", mmsc, reads=("kh", "qh"), writes=pk)

                def expf(e, kts=kts, qw=qw, at=at, pb=pb):
                    if len(kts) == 2 and qw == 512:
                        return e.activation(
                            out=at[:, 0:1024], in_=pb[:, 0:1024], func=AF.Exp,
                            bias=B["negB"][:], scale=B[cscale_key][:])
                    last = None
                    for j in range(len(kts)):
                        last = e.activation(
                            out=at[:, j * 512:j * 512 + qw],
                            in_=pb[:, j * 512:j * 512 + qw], func=AF.Exp,
                            bias=B["negB"][:], scale=B[cscale_key][:])
                    return last

                S.comp("scalar", expf, reads=pk + ("negB", cscale_key), writes=(ak,))
                pend.append((kts, ak))
                if len(pend) > 2:
                    emit_mha_ctx(S, B, pend[0][0], pend[0][1], qw, nkt)
                    pend.pop(0)
            for kk, aa in pend:
                emit_mha_ctx(S, B, kk, aa, qw, nkt)
            S.comp("scalar", lambda e, qw=qw: e.activation(
                out=B["lnd"][:, 0:qw], in_=B["PDEN"][:, 0:qw], func=AF.Ln,
                bias=0.0, scale=1.0),
                reads=("PDEN",), writes=("lnd",))
            S.comp("scalar", lambda e, qw=qw: e.activation(
                out=B["rden"][:, 0:qw], in_=B["lnd"][:, 0:qw], func=AF.Exp,
                bias=0.0, scale=B["c_neg1"][:]),
                reads=("lnd", "c_neg1"), writes=("rden",))
            S.comp("vector", lambda e, h=h, qb=qb, qw=qw: e.tensor_tensor(
                out=B["ctxfm"][:, h * ctx_stride + qb * 512:h * ctx_stride + qb * 512 + qw],
                in0=B["PCTX"][:, 0:qw], in1=B["rden"][:, 0:qw], op=ALU.mult),
                reads=("PCTX", "rden"), writes=("ctxfm",))
    for nt in range(n_nt):
        tw = min(512, T - nt * 512)
        for m in range(NCH):
            half = ((nt * NCH + m) % 2) * 512
            pk = "PSBIG_a" if (nt * NCH + m) % 2 == 0 else "PSBIG_b"

            def mmo(e, nt=nt, tw=tw, m=m, half=half):
                last = None
                for k in range(NCH):
                    last = e.matmul(
                        psum[:, half:half + tw],
                        w_out[:, k * D + 128 * m:k * D + 128 * m + 128],
                        B["ctxfm"][:, k * ctx_stride + nt * 512:
                                   k * ctx_stride + nt * 512 + tw],
                        start=(k == 0), stop=(k == NCH - 1))
                return last

            S.comp("tensor", mmo, reads=(wo_key, "ctxfm"), writes=(pk,))
            dst_write_fn(S, B, nt, tw, m, psum[:, half:half + tw], pk)


def mm_scores_body(e, B, kts, qb, qw, pb):
    last = None
    for j, kt in enumerate(kts):
        last = e.matmul(
            pb[:, j * 512:j * 512 + qw],
            B["kh"][:, kt * 128:(kt + 1) * 128],
            B["qh"][:, qb * 512:qb * 512 + qw], start=True, stop=True)
    return last


def mm_ctx_body(e, B, kts, ak, qw, nkt):
    at = B[ak]
    last = None
    for j, kt in enumerate(kts):
        e.matmul(B["PCTX"][:, 0:qw], B["vtm"][:, kt * 128:(kt + 1) * 128],
                 at[:, j * 512:j * 512 + qw],
                 start=(kt == 0), stop=(kt == nkt - 1))
        last = e.matmul(B["PDEN"][:, 0:qw], B["ones128"][:, 0:128],
                        at[:, j * 512:j * 512 + qw],
                        start=(kt == 0), stop=(kt == nkt - 1))
    return last


def emit_mha_ctx(S, B, kts, ak, qw, nkt):
    S.comp("tensor", lambda e, kts=kts, ak=ak: mm_ctx_body(e, B, kts, ak, qw, nkt),
           reads=(ak, "vtm", "ones128"), writes=("PCTX", "PDEN"))


def emit_init_consts(S, B, extra=()):
    S.comp("vector", lambda e: e.memset(B["ones128"][:], 1.0), writes=("ones128",))
    S.comp("vector", lambda e: e.memset(B["ones_row"][:], 1.0), writes=("ones_row",))
    S.comp("vector", lambda e: e.memset(B["c_eps"][:], 1e-5), writes=("c_eps",))
    S.comp("vector", lambda e: e.memset(B["c_nh"][:], -0.5), writes=("c_nh",))
    S.comp("vector", lambda e: e.memset(B["c_ph"][:], 0.5), writes=("c_ph",))
    S.comp("vector", lambda e: e.memset(B["c_neg1"][:], -1.0), writes=("c_neg1",))
    for name, v in extra:
        S.comp("vector", lambda e, name=name, v=v: e.memset(B[name][:], v),
               writes=(name,))


def alloc_common(B):
    B.sb("ones128", (128, 128), BF16)
    B.sb("ones_row", (1, 512), BF16)
    B.sb("c_eps", (128, 1), F32)
    B.sb("c_nh", (128, 1), F32)
    B.sb("c_ph", (128, 1), F32)
    B.sb("c_neg1", (128, 1), F32)
    B.sb("wstage0", (128, 512), F32)
    B.sb("wstage1", (128, 512), F32)
    B.ps("PSBIG", (128, 1024), F32)


def alloc_ln(B, width):
    if "sqbuf" not in B:
        B.sb("sqbuf", (128, NCH * width), BF16)
    B.sb("st_mu", (128, width), F32)
    B.sb("st_t1", (128, width), F32)
    B.sb("st_t2", (128, width), F32)
    B.sb("st_r", (128, width), F32)
    B.sb("st_mu_bf", (128, width), BF16)
    B.sb("st_r_bf", (128, width), BF16)
    B.sb("ln_t", (128, width), BF16)


def alloc_attn(B, T, lnw=0):
    B.sb("qh", (128, T), BF16)
    B.sb("kh", (128, T), BF16)
    B.sb("vtm", (128, T), BF16)
    if "sqbuf" not in B:
        B.sb("sqbuf", (128, max(T, NCH * lnw)), BF16)
    B.sb("b_nt", (128, 4), F32)
    B.sb("b_q", (128, 1), F32)
    B.sb("b_k", (128, 1), F32)
    B.sb("negB", (128, 1), F32)
    B.sb("attnT0", (128, 1024), BF16)
    B.sb("attnT1", (128, 1024), BF16)
    B.sb("attnT2", (128, 1024), BF16)
    B.sb("lnd", (128, 512), F32)
    B.sb("rden", (128, 512), F32)
    B.sb("ctxfm", (128, NCH * T), BF16)
    B.ps("PS2", (128, 1024), F32)
    B.ps("PS3", (128, 1024), F32)
    B.ps("PCTX", (128, 512), F32)
    B.ps("PDEN", (128, 512), F32)
    B.sb("c_sc", (128, 1), F32)



A_T = [512, 1024, 2048]
A_NCHUNK = [4, 2, 1]
A_INW = [1856, 1984, 2048]
A_COFF = [448, 960, 1984]


def build_A(level):
    nc = bass.Bass("TRN2", target_bir_lowering=False)
    es = ExitStack()
    T = A_T[level]
    n_chunks = A_NCHUNK[level]
    in_w = A_INW[level]
    scale = 1.0 / np.sqrt(128.0)

    x_in = nc.dram_tensor("x_in", [D, in_w], F32 if level == 0 else BF16,
                          kind="ExternalInput").ap()
    w_in_t = nc.dram_tensor("w_in_t", [D, 3 * D], F32, kind="ExternalInput").ap()
    b_in_p = nc.dram_tensor("b_in_p", [128, 24], F32, kind="ExternalInput").ap()
    b_in_f = nc.dram_tensor("b_in_f", [1, 3 * D], F32, kind="ExternalInput").ap()
    w_out_t = nc.dram_tensor("w_out_t", [D, D], F32, kind="ExternalInput").ap()
    b_out_p = nc.dram_tensor("b_out_p", [128, 8], F32, kind="ExternalInput").ap()
    if level == 2:
        x_own = nc.dram_tensor("x_own", [D, 2048], F32, kind="ExternalInput").ap()
        out_t = nc.dram_tensor("out", [D, 2048], F32, kind="ExternalOutput").ap()
    else:
        out_t = nc.dram_tensor("out", [D, 2048], BF16, kind="ExternalOutput").ap()

    with es:
        B = Buffers(nc, es)
        S = Sched(nc, es)
        alloc_common(B)
        lnw = 512 if level == 0 else 128
        alloc_attn(B, T, lnw if level in (0, 2) else 0)
        B.sb("w_in", (128, NCH * 3 * D), BF16)
        B.sb("w_out", (128, NCH * D), BF16)
        B.sb("b_in_p", (128, 24), F32)
        B.sb("b_in_f_bf", (1, 3 * D), BF16)
        B.sb("b_out_p", (128, 8), F32)
        B.sb("src", (128, NCH * in_w), BF16)
        B.sb("dstbuf", (128, NCH * 2048), BF16)
        if level in (0, 2):
            alloc_ln(B, lnw)
            B.sb("xg", (128, NCH * lnw), F32)
        if level == 2:
            B.sb("res_t", (128, lnw), F32)

        emit_init_consts(S, B, [("c_sc", scale)])
        emit_weight_load(S, B, w_in_t, B["w_in"], NCH, 3 * D, "w_in")
        emit_weight_load(S, B, w_out_t, B["w_out"], NCH, D, "w_out")
        S.dma([(B["b_in_p"][:], b_in_p), (B["b_out_p"][:], b_out_p)],
              writes=("b_in_p", "b_out_p"))
        emit_flat_bias_load(S, B, b_in_f, B["b_in_f_bf"], 3 * D, "b_in_f_bf")

        if level == 0:
            B.sb("xbf", (128, NCH * 512), BF16)
            c0 = 0
            while c0 < in_w:
                gw = min(512, in_w - c0)
                S.dma([(B["xg"][:, k * gw:(k + 1) * gw],
                        x_in[128 * k:128 * (k + 1), c0:c0 + gw]) for k in range(NCH)],
                      writes=("xg",))
                S.comp("vector", lambda e, gw=gw: e.tensor_copy(
                    out=B["xbf"][:, 0:NCH * gw], in_=B["xg"][:, 0:NCH * gw]),
                    reads=("xg",), writes=("xbf",))
                emit_ln_stats(S, B, lambda k, gw=gw: B["xbf"][:, k * gw:(k + 1) * gw],
                              ("xbf",), gw)
                emit_ln_apply_bf(S, B, lambda k, gw=gw: B["xbf"][:, k * gw:(k + 1) * gw],
                                 ("xbf",), gw,
                                 lambda k, c0=c0, gw=gw: B["src"][:, k * in_w + c0:
                                                                  k * in_w + c0 + gw],
                                 "src")
                c0 += gw
        else:
            S.dma([(B["src"][:, k * in_w:(k + 1) * in_w],
                    x_in[128 * k:128 * (k + 1), :]) for k in range(NCH)],
                  writes=("src",))

        for j in range(n_chunks):
            def dst_write(S_, B_, nt, tw, m, psum_ap, pk, j=j):
                S_.comp("scalar", lambda e, nt=nt, tw=tw, m=m, psum_ap=psum_ap, j=j:
                        e.add(out=B_["dstbuf"][:, m * 2048 + T * j + nt * 512:
                                               m * 2048 + T * j + nt * 512 + tw],
                              in_=psum_ap, add=B_["b_out_p"][:, m:m + 1]),
                        reads=(pk, "b_out_p"), writes=("dstbuf",))

            emit_mha(S, B, B["src"], "src", A_COFF[level] * j, T, dst_write,
                     scale=scale)

        if level < 2:
            S.dma([(out_t[128 * k:128 * (k + 1), :],
                    B["dstbuf"][:, k * 2048:(k + 1) * 2048]) for k in range(NCH)],
                  reads=("dstbuf",), writes=("out",))
        else:
            for g0 in range(0, 2048, lnw):
                S.dma([(B["xg"][:, k * lnw:(k + 1) * lnw],
                        x_own[128 * k:128 * (k + 1), g0:g0 + lnw]) for k in range(NCH)],
                      writes=("xg",))
                emit_ln_stats(S, B, lambda k, g0=g0: B["dstbuf"][:, k * 2048 + g0:
                                                                 k * 2048 + g0 + lnw],
                              ("dstbuf",), lnw)
                for k in range(NCH):
                    S.comp("vector", lambda e, k=k, g0=g0: e.tensor_tensor(
                        out=B["ln_t"][:, 0:lnw],
                        in0=B["dstbuf"][:, k * 2048 + g0:k * 2048 + g0 + lnw],
                        in1=B["st_mu_bf"][:, 0:lnw], op=ALU.subtract),
                        reads=("dstbuf", "st_mu_bf"), writes=("ln_t",))
                    S.comp("vector", lambda e, k=k: e.tensor_tensor(
                        out=B["res_t"][:, 0:lnw], in0=B["ln_t"][:, 0:lnw],
                        in1=B["st_r_bf"][:, 0:lnw], op=ALU.mult),
                        reads=("ln_t", "st_r_bf"), writes=("res_t",))
                    S.comp("vector", lambda e, k=k: e.tensor_tensor(
                        out=B["res_t"][:, 0:lnw], in0=B["res_t"][:, 0:lnw],
                        in1=B["xg"][:, k * lnw:(k + 1) * lnw], op=ALU.add),
                        reads=("res_t", "xg"), writes=("res_t",))
                    S.dma([(out_t[128 * k:128 * (k + 1), g0:g0 + lnw],
                            B["res_t"][:, 0:lnw])],
                          reads=("res_t",), writes=("out",))

        S.barrier_wait("sync", ("out",))
        S.finalize()
    return nc



def _fmT(a):
    return np.ascontiguousarray(np.asarray(a, dtype=np.float32).T)


def _fmT_bf(a):
    return np.ascontiguousarray(np.asarray(a).T)


def _bias_p(v):
    v = np.asarray(v, dtype=np.float32)
    return np.ascontiguousarray(v.reshape(-1, 128).T)


def _bias_f(v):
    return np.ascontiguousarray(np.asarray(v, dtype=np.float32).reshape(1, -1))


def _bias_p24(bin_):
    return _bias_p(bin_)


def run_A(level, x_fm_f32, x_fm_prev_bf, Win, bin_, Wout, bout):
    nc = build_A(level)
    w_in_t = _fmT(Win)
    w_out_t = _fmT(Wout)
    b_in_p = _bias_p24(bin_)
    b_in_f = _bias_f(bin_)
    b_out_p = _bias_p(bout)
    in_maps = []
    in_w = A_INW[level]
    for c in range(8):
        b, q = c // 4, c % 4
        m = {"w_in_t": w_in_t, "b_in_p": b_in_p, "b_in_f": b_in_f,
             "w_out_t": w_out_t, "b_out_p": b_out_p}
        if level == 0:
            s0 = 1792 * q
            m["x_in"] = np.ascontiguousarray(x_fm_f32[b][:, s0:s0 + in_w])
        elif level == 1:
            s0 = 1920 * q
            m["x_in"] = np.ascontiguousarray(x_fm_prev_bf[b][:, s0:s0 + in_w])
        else:
            s0 = 1984 * q
            m["x_in"] = np.ascontiguousarray(x_fm_prev_bf[b][:, s0:s0 + in_w])
            m["x_own"] = np.ascontiguousarray(x_fm_f32[b][:, 2048 * q:2048 * (q + 1)])
        in_maps.append(m)
    res = run_bass_kernel_spmd(nc, in_maps, core_ids=list(range(8)))
    odt = np.float32 if level == 2 else BF
    out = np.empty((2, D, 8192), dtype=odt)
    for c in range(8):
        b, q = c // 4, c % 4
        out[b][:, 2048 * q:2048 * (q + 1)] = np.asarray(res.results[c]["out"])
    return out





def emit_tail_group(S, B, chunk_ap_fn, src_keys, lnw, res_dram_ap, out_dram_ap, g0):
    S.dma([(B["xg"][:, k * lnw:(k + 1) * lnw],
            res_dram_ap[128 * k:128 * (k + 1), g0:g0 + lnw]) for k in range(NCH)],
          writes=("xg",))
    emit_ln_stats(S, B, chunk_ap_fn, src_keys, lnw)
    for k in range(NCH):
        S.comp("vector", lambda e, k=k: e.tensor_tensor(
            out=B["ln_t"][:, 0:lnw], in0=chunk_ap_fn(k),
            in1=B["st_mu_bf"][:, 0:lnw], op=ALU.subtract),
            reads=tuple(src_keys) + ("st_mu_bf",), writes=("ln_t",))
        S.comp("vector", lambda e, k=k: e.tensor_tensor(
            out=B["res_t"][:, 0:lnw], in0=B["ln_t"][:, 0:lnw],
            in1=B["st_r_bf"][:, 0:lnw], op=ALU.mult),
            reads=("ln_t", "st_r_bf"), writes=("res_t",))
        S.comp("vector", lambda e, k=k: e.tensor_tensor(
            out=B["res_t"][:, 0:lnw], in0=B["res_t"][:, 0:lnw],
            in1=B["xg"][:, k * lnw:(k + 1) * lnw], op=ALU.add),
            reads=("res_t", "xg"), writes=("res_t",))
        S.dma([(out_dram_ap[128 * k:128 * (k + 1), g0:g0 + lnw],
                B["res_t"][:, 0:lnw])],
              reads=("res_t",), writes=("out",))



B_TH = 1024
B_HH = [(0, 6), (6, 5)]


def build_B():
    nc = bass.Bass("TRN2", target_bir_lowering=False)
    es = ExitStack()
    H = 1408
    x1 = nc.dram_tensor("x1", [D, 2048], F32, kind="ExternalInput").ap()
    gw_t = nc.dram_tensor("gw_t", [D, 8], F32, kind="ExternalInput").ap()
    gb_b = nc.dram_tensor("gb_b", [128, 8], F32, kind="ExternalInput").ap()
    w1a = nc.dram_tensor("w1a", [9 * D, H], F32, kind="ExternalInput").ap()
    w3a = nc.dram_tensor("w3a", [9 * D, H], F32, kind="ExternalInput").ap()
    w2a = nc.dram_tensor("w2a", [9 * H, D], F32, kind="ExternalInput").ap()
    b1a = nc.dram_tensor("b1a", [9 * 128, 11], F32, kind="ExternalInput").ap()
    b3a = nc.dram_tensor("b3a", [9 * 128, 11], F32, kind="ExternalInput").ap()
    b2d = nc.dram_tensor("b2d", [8, D], F32, kind="ExternalInput").ap()
    sb2f = nc.dram_tensor("sb2f", [1, D], F32, kind="ExternalInput").ap()
    ident = nc.dram_tensor("identf", [128, 128], F32, kind="ExternalInput").ap()
    out_t = nc.dram_tensor("out", [D, 2048], F32, kind="ExternalOutput").ap()

    with es:
        B = Buffers(nc, es)
        S = Sched(nc, es)
        alloc_common(B)
        alloc_ln(B, 512)
        B.sb("y", (128, NCH * B_TH), BF16)
        B.sb("acc", (128, NCH * B_TH), F32)
        B.sb("accbf", (128, NCH * 256), BF16)
        B.sb("xg", (128, NCH * 256), F32)
        B.sb("res_t", (128, 512), F32)
        B.sb("ident_f32", (128, 128), F32)
        B.ps("PSTR", (128, 128), F32)
        for p in range(2):
            B.sb(f"w1h{p}", (128, NCH * 768), BF16)
            B.sb(f"w3h{p}", (128, NCH * 768), BF16)
            B.sb(f"w2h{p}", (128, 6 * D), BF16)
            B.sb(f"b1h{p}", (128, 11), F32)
            B.sb(f"b3h{p}", (128, 11), F32)
        B.sb("hbuf0", (128, 6 * B_TH), BF16)
        B.sb("hbuf1", (128, 6 * B_TH), BF16)
        B.sb("g1sb0", (128, 512), BF16)
        B.sb("g3sb0", (128, 512), BF16)
        B.ps("PS2", (128, 1024), F32)
        B.sb("gw_sb", (128, NCH * 8), F32)
        B.sb("gwcs", (128, 8), F32)
        B.sb("gb_sb", (128, 8), F32)
        B.sb("b2bf", (8, D), BF16)
        B.sb("sb2bf", (1, D), BF16)
        B.sb("st_mu_tm", (128, 8), F32)
        B.sb("st_negmu_tm", (128, 8), F32)
        B.sb("st_r_tm", (128, 8), F32)
        B.sb("sc", (128, 8), F32)
        B.sb("sc2", (128, 8), F32)
        B.sb("m1", (128, 1), F32)
        B.sb("rw", (128, 8), F32)
        B.sb("rwT", (8, B_TH), BF16)
        B.sb("rwrow", (1, B_TH), BF16)
        B.sb("rw_b", (128, B_TH), BF16)

        emit_init_consts(S, B)
        S.dma([(B["gb_sb"][:], gb_b), (B["ident_f32"][:], ident)],
              writes=("gb_sb", "ident_f32"))
        for p0 in (0, 512):
            stg = "wstage0" if p0 == 0 else "wstage1"
            S.dma([(B[stg][0:8, 0:512], b2d[0:8, p0:p0 + 512])], writes=(stg,))
            S.comp("gpsimd", lambda e, p0=p0, stg=stg: e.tensor_copy(
                out=B["b2bf"][0:8, p0:p0 + 512], in_=B[stg][0:8, 0:512]),
                reads=(stg,), writes=("b2bf",))
        emit_flat_bias_load(S, B, sb2f, B["sb2bf"], D, "sb2bf")
        for k in range(NCH):
            S.dma([(B["gw_sb"][:, k * 8:(k + 1) * 8], gw_t[128 * k:128 * (k + 1), :])],
                  writes=("gw_sb",))
        B.sb("ones_f32", (128, 128), F32)
        S.comp("vector", lambda e: e.memset(B["ones_f32"][:], 1.0), writes=("ones_f32",))

        def mm_gwcs(e):
            last = None
            for k in range(NCH):
                last = e.matmul(B["PSBIG"][:, 0:8], B["ones_f32"][:],
                                B["gw_sb"][:, k * 8:(k + 1) * 8],
                                start=(k == 0), stop=(k == NCH - 1))
            return last

        S.comp("tensor", mm_gwcs, reads=("gw_sb", "ones_f32"), writes=("PSBIG_a",))
        S.comp("vector", lambda e: e.tensor_copy(out=B["gwcs"][:], in_=B["PSBIG"][:, 0:8]),
               reads=("PSBIG_a",), writes=("gwcs",))

        psum = B["PSBIG"]
        for th in range(2):
            t0 = th * B_TH
            for g in range(4):
                g0 = t0 + g * 256
                S.dma([(B["xg"][:, k * 256:(k + 1) * 256],
                        x1[128 * k:128 * (k + 1), g0:g0 + 256]) for k in range(NCH)],
                      writes=("xg",))
                S.comp("vector", lambda e: e.tensor_copy(
                    out=B["accbf"][:, 0:NCH * 256], in_=B["xg"][:, 0:NCH * 256]),
                    reads=("xg",), writes=("accbf",))
                emit_ln_stats(S, B, lambda k: B["accbf"][:, k * 256:(k + 1) * 256],
                              ("accbf",), 256)
                emit_stats_tm(S, B, 256, g * 2)
                emit_ln_apply_bf(S, B, lambda k: B["accbf"][:, k * 256:(k + 1) * 256],
                                 ("accbf",), 256,
                                 lambda k, g=g: B["y"][:, k * B_TH + g * 256:
                                                       k * B_TH + g * 256 + 256],
                                 "y")
                for tt in range(2):
                    gt = g * 2 + tt

                    def mm_gate(e, g=g, tt=tt):
                        last = None
                        for k in range(NCH):
                            last = e.matmul(
                                psum[:, 0:8],
                                B["xg"][:, k * 256 + tt * 128:k * 256 + tt * 128 + 128],
                                B["gw_sb"][:, k * 8:(k + 1) * 8],
                                start=(k == 0), stop=(k == NCH - 1))
                        return last

                    S.comp("tensor", mm_gate, reads=("xg", "gw_sb"), writes=("PSBIG_a",))
                    S.comp("vector", lambda e, gt=gt: e.tensor_scalar(
                        out=B["st_negmu_tm"][:, gt:gt + 1], in0=B["st_mu_tm"][:, gt:gt + 1],
                        scalar1=-1.0, scalar2=None, op0=ALU.mult),
                        reads=("st_mu_tm",), writes=("st_negmu_tm",))
                    S.comp("vector", lambda e, gt=gt: e.scalar_tensor_tensor(
                        out=B["sc"][:], in0=B["gwcs"][:],
                        scalar=B["st_negmu_tm"][:, gt:gt + 1], in1=psum[:, 0:8],
                        op0=ALU.mult, op1=ALU.add),
                        reads=("gwcs", "st_negmu_tm", "PSBIG_a"), writes=("sc",))
                    S.comp("vector", lambda e, gt=gt: e.tensor_scalar(
                        out=B["sc"][:], in0=B["sc"][:],
                        scalar1=B["st_r_tm"][:, gt:gt + 1], scalar2=None, op0=ALU.mult),
                        reads=("sc", "st_r_tm"), writes=("sc",))
                    S.comp("vector", lambda e: e.reduce_max(
                        out=B["m1"][:], in_=B["sc"][:], axis=AX.X, negate=True),
                        reads=("sc",), writes=("m1",))
                    S.comp("scalar", lambda e: e.activation(
                        out=B["sc2"][:], in_=B["sc"][:], func=AF.Exp,
                        bias=B["m1"][:], scale=1.0),
                        reads=("sc", "m1"), writes=("sc2",))
                    S.comp("vector", lambda e: e.reduce_sum(
                        out=B["m1"][:], in_=B["sc2"][:], axis=AX.X),
                        reads=("sc2",), writes=("m1",))
                    S.comp("vector", lambda e: e.reciprocal(
                        out=B["m1"][:], in_=B["m1"][:]),
                        reads=("m1",), writes=("m1",))
                    S.comp("vector", lambda e: e.tensor_scalar(
                        out=B["sc"][:], in0=B["sc2"][:], scalar1=B["m1"][:],
                        scalar2=None, op0=ALU.mult),
                        reads=("sc2", "m1"), writes=("sc",))
                    S.comp("vector", lambda e: e.tensor_tensor(
                        out=B["sc2"][:], in0=B["sc"][:], in1=B["gb_sb"][:], op=ALU.add),
                        reads=("sc", "gb_sb"), writes=("sc2",))
                    S.comp("vector", lambda e: e.reduce_max(
                        out=B["m1"][:], in_=B["sc2"][:], axis=AX.X),
                        reads=("sc2",), writes=("m1",))
                    S.comp("vector", lambda e: e.tensor_scalar(
                        out=B["rw"][:], in0=B["sc2"][:], scalar1=B["m1"][:],
                        scalar2=None, op0=ALU.is_ge),
                        reads=("sc2", "m1"), writes=("rw",))
                    S.comp("vector", lambda e: e.scalar_tensor_tensor(
                        out=B["rw"][:], in0=B["rw"][:], scalar=-1e30, in1=B["sc2"][:],
                        op0=ALU.mult, op1=ALU.add),
                        reads=("rw", "sc2"), writes=("rw",))
                    S.comp("vector", lambda e: e.reduce_max(
                        out=B["m1"][:], in_=B["rw"][:], axis=AX.X),
                        reads=("rw",), writes=("m1",))
                    S.comp("vector", lambda e: e.tensor_scalar(
                        out=B["rw"][:], in0=B["sc2"][:], scalar1=B["m1"][:],
                        scalar2=None, op0=ALU.is_ge),
                        reads=("sc2", "m1"), writes=("rw",))
                    S.comp("vector", lambda e: e.tensor_tensor(
                        out=B["rw"][:], in0=B["rw"][:], in1=B["sc"][:], op=ALU.mult),
                        reads=("rw", "sc"), writes=("rw",))
                    S.comp("tensor", lambda e: e.transpose(
                        B["PSTR"][0:8, 0:128], B["rw"][:], B["ident_f32"][:]),
                        reads=("rw", "ident_f32"), writes=("PSTR",))
                    S.comp("vector", lambda e, gt=gt: e.tensor_copy(
                        out=B["rwT"][0:8, gt * 128:gt * 128 + 128],
                        in_=B["PSTR"][0:8, 0:128]),
                        reads=("PSTR",), writes=("rwT",))
            def plan_wload(e_i, hh_i):
                p = (e_i * 2 + hh_i) % 2
                hc0, hnc = B_HH[hh_i]
                hw = hnc * 128
                w1k, w3k, w2k = f"w1h{p}", f"w3h{p}", f"w2h{p}"
                b1k, b3k = f"b1h{p}", f"b3h{p}"
                for k in range(NCH):
                    for p0 in range(0, hw, 512):
                        pw = min(512, hw - p0)
                        stg = "wstage0" if (k + p0) % 2 == 0 else "wstage1"
                        S.dma([(B[stg][:, 0:pw],
                                w1a[e_i * D + 128 * k:e_i * D + 128 * (k + 1),
                                    hc0 * 128 + p0:hc0 * 128 + p0 + pw])],
                              writes=(stg,))
                        S.comp("gpsimd", lambda e, k=k, p0=p0, pw=pw, stg=stg, w1k=w1k: e.tensor_copy(
                            out=B[w1k][:, k * 768 + p0:k * 768 + p0 + pw],
                            in_=B[stg][:, 0:pw]),
                            reads=(stg,), writes=(w1k,))
                        stg2 = "wstage0" if (k + p0) % 2 == 1 else "wstage1"
                        S.dma([(B[stg2][:, 0:pw],
                                w3a[e_i * D + 128 * k:e_i * D + 128 * (k + 1),
                                    hc0 * 128 + p0:hc0 * 128 + p0 + pw])],
                              writes=(stg2,))
                        S.comp("gpsimd", lambda e, k=k, p0=p0, pw=pw, stg2=stg2, w3k=w3k: e.tensor_copy(
                            out=B[w3k][:, k * 768 + p0:k * 768 + p0 + pw],
                            in_=B[stg2][:, 0:pw]),
                            reads=(stg2,), writes=(w3k,))
                for j in range(hnc):
                    for p0 in range(0, D, 512):
                        stg = "wstage0" if (j + p0 // 512) % 2 == 0 else "wstage1"
                        S.dma([(B[stg][:, 0:512],
                                w2a[e_i * H + (hc0 + j) * 128:e_i * H + (hc0 + j) * 128 + 128,
                                    p0:p0 + 512])],
                              writes=(stg,))
                        S.comp("gpsimd", lambda e, j=j, p0=p0, stg=stg, w2k=w2k: e.tensor_copy(
                            out=B[w2k][:, j * D + p0:j * D + p0 + 512],
                            in_=B[stg][:, 0:512]),
                            reads=(stg,), writes=(w2k,))
                S.dma([(B[b1k][:, 0:11], b1a[e_i * 128:(e_i + 1) * 128, :]),
                       (B[b3k][:, 0:11], b3a[e_i * 128:(e_i + 1) * 128, :])],
                      writes=(b1k, b3k))

            def plan_rwb(e_i):
                S.dma([(B["rwrow"][0:1, :], B["rwT"][e_i:e_i + 1, :])],
                      reads=("rwT",), writes=("rwrow",))
                for nt in range(2):
                    S.comp("tensor", lambda e, nt=nt: e.matmul(
                        psum[:, 0:512], B["ones_row"][0:1, 0:128],
                        B["rwrow"][0:1, nt * 512:nt * 512 + 512],
                        start=True, stop=True),
                        reads=("rwrow", "ones_row"), writes=("PSBIG_a",))
                    S.comp("vector", lambda e, nt=nt: e.tensor_copy(
                        out=B["rw_b"][:, nt * 512:nt * 512 + 512],
                        in_=psum[:, 0:512]),
                        reads=("PSBIG_a",), writes=("rw_b",))

            def plan_compute(e_i, hh_i):
                p = (e_i * 2 + hh_i) % 2
                hc0, hnc = B_HH[hh_i]
                w1k, w3k, w2k = f"w1h{p}", f"w3h{p}", f"w2h{p}"
                b1k, b3k = f"b1h{p}", f"b3h{p}"
                hk = f"hbuf{p}"
                for mc in range(hnc):
                    for nt in range(2):
                        par = (mc * 2 + nt) % 2
                        pb = psum if par == 0 else B["PS2"]
                        pka = "PSBIG_a" if par == 0 else "PS2_a"
                        pkb = "PSBIG_b" if par == 0 else "PS2_b"
                        g1k = "g1sb0"
                        g3k = "g3sb0"

                        def mm1(e, mc=mc, nt=nt, w1k=w1k, pb=pb):
                            last = None
                            for k in range(NCH):
                                last = e.matmul(
                                    pb[:, 0:512],
                                    B[w1k][:, k * 768 + 128 * mc:k * 768 + 128 * mc + 128],
                                    B["y"][:, k * B_TH + nt * 512:k * B_TH + nt * 512 + 512],
                                    start=(k == 0), stop=(k == NCH - 1))
                            return last

                        S.comp("tensor", mm1, reads=(w1k, "y"), writes=(pka,))

                        def mm3(e, mc=mc, nt=nt, w3k=w3k, pb=pb):
                            last = None
                            for k in range(NCH):
                                last = e.matmul(
                                    pb[:, 512:1024],
                                    B[w3k][:, k * 768 + 128 * mc:k * 768 + 128 * mc + 128],
                                    B["y"][:, k * B_TH + nt * 512:k * B_TH + nt * 512 + 512],
                                    start=(k == 0), stop=(k == NCH - 1))
                            return last

                        S.comp("tensor", mm3, reads=(w3k, "y"), writes=(pkb,))
                        S.comp("scalar", lambda e, mc=mc, b1k=b1k, hc0=hc0, pb=pb, g1k=g1k: e.add(
                            out=B[g1k][:], in_=pb[:, 0:512],
                            add=B[b1k][:, hc0 + mc:hc0 + mc + 1]),
                            reads=(pka, b1k), writes=(g1k,))
                        S.comp("scalar", lambda e, mc=mc, b3k=b3k, hc0=hc0, pb=pb, g3k=g3k: e.add(
                            out=B[g3k][:], in_=pb[:, 512:1024],
                            add=B[b3k][:, hc0 + mc:hc0 + mc + 1]),
                            reads=(pkb, b3k), writes=(g3k,))
                        S.comp("vector", lambda e, g1k=g1k, g3k=g3k: e.tensor_tensor(
                            out=B[g1k][:], in0=B[g1k][:], in1=B[g3k][:],
                            op=ALU.mult),
                            reads=(g1k, g3k), writes=(g1k,))
                        if e_i < 8:
                            S.comp("scalar", lambda e, g1k=g1k, g3k=g3k: e.activation(
                                out=B[g3k][:], in_=B[g1k][:], func=AF.Silu,
                                bias=0.0, scale=1.0),
                                reads=(g1k,), writes=(g3k,))
                            S.comp("vector", lambda e, mc=mc, nt=nt, g3k=g3k, hk=hk: e.tensor_tensor(
                                out=B[hk][:, mc * B_TH + nt * 512:mc * B_TH + nt * 512 + 512],
                                in0=B[g3k][:], in1=B["rw_b"][:, nt * 512:nt * 512 + 512],
                                op=ALU.mult),
                                reads=(g3k, "rw_b"), writes=(hk,))
                        else:
                            S.comp("scalar", lambda e, mc=mc, nt=nt, g1k=g1k, hk=hk: e.activation(
                                out=B[hk][:, mc * B_TH + nt * 512:mc * B_TH + nt * 512 + 512],
                                in_=B[g1k][:], func=AF.Silu, bias=0.0, scale=1.0),
                                reads=(g1k,), writes=(hk,))
                for nt in range(2):
                    for m in range(NCH):
                        par = (nt * NCH + m) % 2
                        pb = psum if par == 0 else B["PS2"]
                        half = 0 if True else 0
                        pk = "PSBIG_a" if par == 0 else "PS2_a"

                        def mm2b(e, nt=nt, m=m, hnc=hnc, w2k=w2k, pb=pb, hk=hk,
                                 add_bias=(e_i == 8 and hh_i == 0)):
                            last = None
                            for j in range(hnc):
                                is_last = (j == hnc - 1) and not add_bias
                                last = e.matmul(
                                    pb[:, 0:512],
                                    B[w2k][:, j * D + 128 * m:j * D + 128 * m + 128],
                                    B[hk][:, j * B_TH + nt * 512:j * B_TH + nt * 512 + 512],
                                    start=(j == 0), stop=is_last)
                            if add_bias:
                                e.matmul(pb[:, 0:512],
                                         B["b2bf"][0:8, 128 * m:128 * m + 128],
                                         B["rwT"][0:8, nt * 512:nt * 512 + 512],
                                         start=False, stop=False)
                                last = e.matmul(
                                    pb[:, 0:512],
                                    B["sb2bf"][0:1, 128 * m:128 * m + 128],
                                    B["ones_row"][0:1, 0:512],
                                    start=False, stop=True)
                            return last

                        S.comp("tensor", mm2b,
                               reads=(w2k, hk, "b2bf", "rwT", "sb2bf", "ones_row"),
                               writes=(pk,))
                        if e_i == 0 and hh_i == 0:
                            S.comp("vector", lambda e, nt=nt, m=m, pb=pb: e.tensor_copy(
                                out=B["acc"][:, m * B_TH + nt * 512:m * B_TH + nt * 512 + 512],
                                in_=pb[:, 0:512]),
                                reads=(pk,), writes=("acc",))
                        else:
                            S.comp("vector", lambda e, nt=nt, m=m, pb=pb: e.tensor_tensor(
                                out=B["acc"][:, m * B_TH + nt * 512:m * B_TH + nt * 512 + 512],
                                in0=B["acc"][:, m * B_TH + nt * 512:m * B_TH + nt * 512 + 512],
                                in1=pb[:, 0:512], op=ALU.add),
                                reads=("acc", pk), writes=("acc",))

            steps = [(e_i, hh_i) for e_i in range(9) for hh_i in range(2)]
            plan_wload(*steps[0])
            for si, (e_i, hh_i) in enumerate(steps):
                if hh_i == 0 and e_i < 8:
                    plan_rwb(e_i)
                if si + 1 < len(steps):
                    plan_wload(*steps[si + 1])
                plan_compute(e_i, hh_i)
            for g in range(4):
                g0 = t0 + g * 256
                for k in range(NCH):
                    S.comp("vector", lambda e, k=k, g=g: e.tensor_copy(
                        out=B["accbf"][:, k * 256:(k + 1) * 256],
                        in_=B["acc"][:, k * B_TH + g * 256:k * B_TH + g * 256 + 256]),
                        reads=("acc",), writes=("accbf",))
                emit_tail_group(S, B, lambda k: B["accbf"][:, k * 256:(k + 1) * 256],
                                ("accbf",), 256, x1, out_t, g0)

        S.barrier_wait("sync", ("out",))
        S.finalize()
    return nc


def run_B(x1_fm, gate_w, gate_b, w1, b1, w3, b3, w2, b2, sw1, sb1, sw3, sb3,
          sw2, sb2):
    nc = build_B()
    H = 1408
    w1a = np.ascontiguousarray(np.concatenate(
        [np.asarray(w1, np.float32).reshape(8 * 1024, H),
         np.asarray(sw1, np.float32)], axis=0))
    w3a = np.ascontiguousarray(np.concatenate(
        [np.asarray(w3, np.float32).reshape(8 * 1024, H),
         np.asarray(sw3, np.float32)], axis=0))
    w2a = np.ascontiguousarray(np.concatenate(
        [np.asarray(w2, np.float32).reshape(8 * H, 1024),
         np.asarray(sw2, np.float32)], axis=0))
    b1a = np.ascontiguousarray(np.concatenate(
        [np.stack([_bias_p(np.asarray(b1, np.float32)[e]) for e in range(8)]),
         _bias_p(np.asarray(sb1, np.float32))[None]], axis=0).reshape(9 * 128, 11))
    b3a = np.ascontiguousarray(np.concatenate(
        [np.stack([_bias_p(np.asarray(b3, np.float32)[e]) for e in range(8)]),
         _bias_p(np.asarray(sb3, np.float32))[None]], axis=0).reshape(9 * 128, 11))
    m_common = {
        "gw_t": _fmT(gate_w), "gb_b": np.ascontiguousarray(
            np.broadcast_to(np.asarray(gate_b, np.float32)[None, :], (128, 8))),
        "w1a": w1a, "w3a": w3a, "w2a": w2a, "b1a": b1a, "b3a": b3a,
        "b2d": np.ascontiguousarray(np.asarray(b2, np.float32)),
        "sb2f": _bias_f(sb2), "identf": np.eye(128, dtype=np.float32),
    }
    in_maps = []
    for c in range(8):
        b, q = c // 4, c % 4
        m = dict(m_common)
        m["x1"] = np.ascontiguousarray(x1_fm[b][:, 2048 * q:2048 * (q + 1)])
        in_maps.append(m)
    res = run_bass_kernel_spmd(nc, in_maps, core_ids=list(range(8)))
    out = np.empty((2, D, 8192), dtype=np.float32)
    for c in range(8):
        b, q = c // 4, c % 4
        out[b][:, 2048 * q:2048 * (q + 1)] = np.asarray(res.results[c]["out"])
    return out



def build_C1():
    nc = bass.Bass("TRN2", target_bir_lowering=False)
    es = ExitStack()
    S_LEN = 8192
    x2 = nc.dram_tensor("x2", [D, S_LEN], F32, kind="ExternalInput").ap()
    wq = nc.dram_tensor("wq", [D, 256], F32, kind="ExternalInput").ap()
    wk = nc.dram_tensor("wk", [D, 256], F32, kind="ExternalInput").ap()
    wv = nc.dram_tensor("wv", [D, 256], F32, kind="ExternalInput").ap()
    thr = nc.dram_tensor("thr", [1, S_LEN], F32, kind="ExternalInput").ap()
    out_t = nc.dram_tensor("out", [256, S_LEN], BF16, kind="ExternalOutput").ap()

    with es:
        B = Buffers(nc, es)
        S = Sched(nc, es)
        alloc_common(B)
        alloc_ln(B, 512)
        B.sb("qfm", (128, 2 * S_LEN), BF16)
        B.sb("kfm", (128, 2 * S_LEN), BF16)
        B.sb("vtm", (128, 64 * 256), BF16)
        B.sb("xg", (128, NCH * 512), F32)
        B.sb("y2g", (128, NCH * 512), BF16)
        B.sb("wq_sb", (128, NCH * 256), BF16)
        B.sb("wk_sb", (128, NCH * 256), BF16)
        B.sb("wv_sb", (128, NCH * 256), BF16)
        B.sb("attnT0", (128, 1024), BF16)
        B.sb("attnT1", (128, 1024), BF16)
        B.sb("attnT2", (128, 1024), BF16)
        B.sb("thr_pc", (1, 512), F32)
        B.sb("thrb", (128, 512), F32)
        B.sb("kcol", (128, 64), F32)
        B.sb("mask_sb", (128, 512), BF16)
        B.sb("ctxbuf0", (128, 512), BF16)
        B.sb("ctxbuf1", (128, 512), BF16)
        B.sb("lnd", (128, 512), F32)
        B.sb("rden", (128, 512), F32)
        B.sb("ones_f32", (128, 128), F32)
        B.sb("b_nt", (128, 16), F32)
        B.sb("bq4", (128, 4), F32)
        B.sb("bk4", (128, 4), F32)
        B.sb("negB4", (128, 4), F32)
        B.sb("c_sc", (128, 1), F32)
        B.ps("PS2", (128, 1024), F32)
        B.ps("PS3", (128, 1024), F32)
        B.ps("PCTX", (128, 512), F32)
        B.ps("PDEN", (128, 512), F32)
        psum = B["PSBIG"]

        emit_init_consts(S, B, [("c_sc", 0.125)])
        S.comp("vector", lambda e: e.memset(B["ones_f32"][:], 1.0), writes=("ones_f32",))
        S.comp("gpsimd", lambda e: e.iota(B["kcol"][:], pattern=[[128, 64]],
                                          base=0, channel_multiplier=1,
                                          allow_small_or_imprecise_dtypes=True),
               writes=("kcol",))
        emit_weight_load(S, B, wq, B["wq_sb"], NCH, 256, "wq_sb")
        emit_weight_load(S, B, wk, B["wk_sb"], NCH, 256, "wk_sb")
        emit_weight_load(S, B, wv, B["wv_sb"], NCH, 256, "wv_sb")

        for g in range(16):
            g0 = g * 512
            S.dma([(B["xg"][:, k * 512:(k + 1) * 512],
                    x2[128 * k:128 * (k + 1), g0:g0 + 512]) for k in range(NCH)],
                  writes=("xg",))
            S.comp("vector", lambda e: e.tensor_copy(
                out=B["y2g"][:, 0:NCH * 512], in_=B["xg"][:, 0:NCH * 512]),
                reads=("xg",), writes=("y2g",))
            emit_ln_stats(S, B, lambda k: B["y2g"][:, k * 512:(k + 1) * 512],
                          ("y2g",), 512)
            emit_ln_apply_bf(S, B, lambda k: B["y2g"][:, k * 512:(k + 1) * 512],
                             ("y2g",), 512,
                             lambda k: B["y2g"][:, k * 512:(k + 1) * 512],
                             "y2g")
            for which, wkey in (("qfm", "wq_sb"), ("kfm", "wk_sb")):
                for mh in range(2):
                    half = (mh % 2) * 512
                    pk = "PSBIG_a" if mh % 2 == 0 else "PSBIG_b"

                    def mmqk(e, mh=mh, half=half, wkey=wkey):
                        last = None
                        for k in range(NCH):
                            last = e.matmul(
                                psum[:, half:half + 512],
                                B[wkey][:, k * 256 + 128 * mh:k * 256 + 128 * mh + 128],
                                B["y2g"][:, k * 512:(k + 1) * 512],
                                start=(k == 0), stop=(k == NCH - 1))
                        return last

                    S.comp("tensor", mmqk, reads=(wkey, "y2g"), writes=(pk,))
                    S.comp("scalar", lambda e, which=which, mh=mh, half=half, g0=g0: e.copy(
                        out=B[which][:, mh * 8192 + g0:mh * 8192 + g0 + 512],
                        in_=psum[:, half:half + 512]),
                        reads=(pk,), writes=(which,))
            for tt in range(4):
                half = (tt % 2) * 512
                pk = "PSBIG_a" if tt % 2 == 0 else "PSBIG_b"

                def mmv(e, tt=tt, half=half):
                    last = None
                    for k in range(NCH):
                        last = e.matmul(
                            psum[:, half:half + 256],
                            B["y2g"][:, k * 512 + tt * 128:k * 512 + tt * 128 + 128],
                            B["wv_sb"][:, k * 256:(k + 1) * 256],
                            start=(k == 0), stop=(k == NCH - 1))
                    return last

                S.comp("tensor", mmv, reads=("wv_sb", "y2g"), writes=(pk,))
                S.comp("scalar", lambda e, g=g, tt=tt, half=half: e.copy(
                    out=B["vtm"][:, (g * 4 + tt) * 256:(g * 4 + tt) * 256 + 256],
                    in_=psum[:, half:half + 256]),
                    reads=(pk,), writes=("vtm",))

        for h in range(4):
            hp, hl = h // 2, h % 2
            p0 = 64 * hl
            for which, bcol in (("qfm", "bq4"), ("kfm", "bk4")):
                for half_tok in range(2):
                    c0 = hp * 8192 + half_tok * 4096
                    S.comp("gpsimd", lambda e, which=which, p0=p0, c0=c0: e.tensor_mul(
                        out=B["sqbuf"][0:64, 0:4096],
                        in0=B[which][p0:p0 + 64, c0:c0 + 4096],
                        in1=B[which][p0:p0 + 64, c0:c0 + 4096]),
                        reads=(which,), writes=("sqbuf",))
                    for nt in range(8):
                        halfp = (nt % 2) * 512
                        pk = "PSBIG_a" if nt % 2 == 0 else "PSBIG_b"
                        S.comp("tensor", lambda e, nt=nt, halfp=halfp: e.matmul(
                            psum[:, halfp:halfp + 512], B["ones128"][0:64, 0:128],
                            B["sqbuf"][0:64, nt * 512:nt * 512 + 512],
                            start=True, stop=True),
                            reads=("sqbuf", "ones128"), writes=(pk,))
                        S.comp("vector", lambda e, nt=nt, halfp=halfp, half_tok=half_tok: e.reduce_max(
                            out=B["b_nt"][:, half_tok * 8 + nt:half_tok * 8 + nt + 1],
                            in_=psum[:, halfp:halfp + 512], axis=AX.X),
                            reads=(pk,), writes=("b_nt",))
                S.comp("vector", lambda e, bcol=bcol, h=h: e.reduce_max(
                    out=B[bcol][:, h:h + 1], in_=B["b_nt"][:, 0:16], axis=AX.X),
                    reads=("b_nt",), writes=(bcol,))
            S.comp("vector", lambda e, h=h: e.tensor_tensor(
                out=B["bq4"][:, h:h + 1], in0=B["bq4"][:, h:h + 1],
                in1=B["bk4"][:, h:h + 1], op=ALU.mult),
                reads=("bq4", "bk4"), writes=("bq4",))
            S.comp("scalar", lambda e, h=h: e.activation(
                out=B["bk4"][:, h:h + 1], in_=B["bq4"][:, h:h + 1], func=AF.Ln,
                bias=0.0, scale=1.0),
                reads=("bq4",), writes=("bk4",))
            S.comp("scalar", lambda e, h=h: e.activation(
                out=B["bq4"][:, h:h + 1], in_=B["bk4"][:, h:h + 1], func=AF.Exp,
                bias=0.0, scale=B["c_ph"][:]),
                reads=("bk4", "c_ph"), writes=("bq4",))
            S.comp("vector", lambda e, h=h: e.tensor_scalar(
                out=B["negB4"][:, h:h + 1], in0=B["bq4"][:, h:h + 1],
                scalar1=-0.125, scalar2=None, op0=ALU.mult),
                reads=("bq4",), writes=("negB4",))

        for hp in range(2):
            for qb in range(16):
                nkt = 4 * qb + 4
                S.dma([(B["thr_pc"][0:1, :], thr[0:1, qb * 512:qb * 512 + 512])],
                      writes=("thr_pc",))
                S.comp("tensor", lambda e: e.matmul(
                    psum[:, 0:512], B["ones_f32"][0:1, 0:128], B["thr_pc"][0:1, :],
                    start=True, stop=True),
                    reads=("thr_pc", "ones_f32"), writes=("PSBIG_a",))
                S.comp("vector", lambda e: e.tensor_copy(
                    out=B["thrb"][:], in_=psum[:, 0:512]),
                    reads=("PSBIG_a",), writes=("thrb",))
                for hl in range(2):
                    h = 2 * hp + hl
                    p0 = 64 * hl
                    pend = []
                    PSETS = ((psum, ("PSBIG_a", "PSBIG_b")),
                             (B["PS2"], ("PS2_a", "PS2_b")),
                             (B["PS3"], ("PS3_a", "PS3_b")))
                    for pr in range(nkt // 2):
                        kts = [2 * pr, 2 * pr + 1]
                        pb, pk = PSETS[pr % 3]
                        ak = f"attnT{pr % 3}"
                        at = B[ak]

                        def sc_body(e, kts=kts, qb=qb, hp=hp, p0=p0, pb=pb):
                            last = None
                            for j, kt in enumerate(kts):
                                last = e.matmul(
                                    pb[:, j * 512:(j + 1) * 512],
                                    B["kfm"][p0:p0 + 64, hp * 8192 + kt * 128:hp * 8192 + kt * 128 + 128],
                                    B["qfm"][p0:p0 + 64, hp * 8192 + qb * 512:hp * 8192 + qb * 512 + 512],
                                    start=True, stop=True)
                            return last

                        S.comp("tensor", sc_body, reads=("kfm", "qfm"), writes=pk)
                        S.comp("scalar", lambda e, at=at, h=h, pb=pb: e.activation(
                            out=at[:, 0:1024], in_=pb[:, 0:1024], func=AF.Exp,
                            bias=B["negB4"][:, h:h + 1], scale=B["c_sc"][:]),
                            reads=pk + ("negB4", "c_sc"), writes=(ak,))
                        for j, kt in enumerate(kts):
                            if kt >= max(4 * qb, 1):
                                S.comp("vector", lambda e, kt=kt: e.tensor_scalar(
                                    out=B["mask_sb"][:], in0=B["thrb"][:],
                                    scalar1=B["kcol"][:, kt:kt + 1], scalar2=None,
                                    op0=ALU.is_ge),
                                    reads=("thrb", "kcol"), writes=("mask_sb",))
                                S.comp("vector", lambda e, at=at, j=j, ak=ak: e.tensor_tensor(
                                    out=at[:, j * 512:(j + 1) * 512],
                                    in0=at[:, j * 512:(j + 1) * 512], in1=B["mask_sb"][:],
                                    op=ALU.mult),
                                    reads=(ak, "mask_sb"), writes=(ak,))
                        pend.append((kts, ak))
                        if len(pend) > 2:
                            emit_c1_ctx(S, B, pend[0][0], pend[0][1], p0, h, nkt)
                            pend.pop(0)
                    for kk, aa in pend:
                        emit_c1_ctx(S, B, kk, aa, p0, h, nkt)
                cb = "ctxbuf0" if qb % 2 == 0 else "ctxbuf1"
                S.comp("scalar", lambda e: e.activation(
                    out=B["lnd"][:], in_=B["PDEN"][:, 0:512], func=AF.Ln,
                    bias=0.0, scale=1.0),
                    reads=("PDEN",), writes=("lnd",))
                S.comp("scalar", lambda e: e.activation(
                    out=B["rden"][:], in_=B["lnd"][:], func=AF.Exp,
                    bias=0.0, scale=B["c_neg1"][:]),
                    reads=("lnd", "c_neg1"), writes=("rden",))
                S.comp("vector", lambda e, cb=cb: e.tensor_tensor(
                    out=B[cb][:], in0=B["PCTX"][:, 0:512], in1=B["rden"][:],
                    op=ALU.mult),
                    reads=("PCTX", "rden"), writes=(cb,))
                S.dma([(out_t[hp * 128:(hp + 1) * 128, qb * 512:qb * 512 + 512],
                        B[cb][:])],
                      reads=(cb,), writes=("out",))

        S.barrier_wait("sync", ("out",))
        S.finalize()
    return nc


def c1_ctx_body(e, B, kts, ak, p0, h, nkt):
    at = B[ak]
    last = None
    for j, kt in enumerate(kts):
        e.matmul(B["PCTX"][p0:p0 + 64, 0:512],
                 B["vtm"][:, kt * 256 + 64 * h:kt * 256 + 64 * h + 64],
                 at[:, j * 512:(j + 1) * 512],
                 start=(kt == 0), stop=(kt == nkt - 1))
        last = e.matmul(B["PDEN"][p0:p0 + 64, 0:512],
                        B["ones128"][:, 0:64],
                        at[:, j * 512:(j + 1) * 512],
                        start=(kt == 0), stop=(kt == nkt - 1))
    return last


def emit_c1_ctx(S, B, kts, ak, p0, h, nkt):
    S.comp("tensor", lambda e: c1_ctx_body(e, B, kts, ak, p0, h, nkt),
           reads=(ak, "vtm", "ones128"), writes=("PCTX", "PDEN"))


def run_C1(x2_fm, w_q, w_k, w_v):
    nc = build_C1()
    wqT = _fmT(w_q)
    wkT = _fmT(w_k)
    wvT = _fmT(w_v)
    thr = np.maximum(np.arange(8192, dtype=np.float32), 195.0).reshape(1, 8192)
    in_maps = []
    for c in range(8):
        b, hq = c // 4, c % 4
        in_maps.append({
            "x2": np.ascontiguousarray(x2_fm[b]),
            "wq": np.ascontiguousarray(wqT[:, 256 * hq:256 * (hq + 1)]),
            "wk": np.ascontiguousarray(wkT[:, 256 * hq:256 * (hq + 1)]),
            "wv": np.ascontiguousarray(wvT[:, 256 * hq:256 * (hq + 1)]),
            "thr": thr,
        })
    res = run_bass_kernel_spmd(nc, in_maps, core_ids=list(range(8)))
    ctx = np.empty((2, D, 8192), dtype=BF)
    for c in range(8):
        b, hq = c // 4, c % 4
        ctx[b][256 * hq:256 * (hq + 1), :] = np.asarray(res.results[c]["out"])
    return ctx



def build_C2():
    nc = bass.Bass("TRN2", target_bir_lowering=False)
    es = ExitStack()
    ctx = nc.dram_tensor("ctx", [D, 2048], BF16, kind="ExternalInput").ap()
    x2 = nc.dram_tensor("x2", [D, 2048], F32, kind="ExternalInput").ap()
    wo = nc.dram_tensor("wo", [D, D], F32, kind="ExternalInput").ap()
    bo = nc.dram_tensor("bo", [128, 8], F32, kind="ExternalInput").ap()
    out_t = nc.dram_tensor("out", [D, 2048], F32, kind="ExternalOutput").ap()

    with es:
        B = Buffers(nc, es)
        S = Sched(nc, es)
        alloc_common(B)
        alloc_ln(B, 512)
        B.sb("ctx_sb", (128, NCH * 2048), BF16)
        B.sb("wo_sb", (128, NCH * D), BF16)
        B.sb("bo_sb", (128, 8), F32)
        B.sb("dstbuf", (128, NCH * 2048), BF16)
        B.sb("xg", (128, NCH * 512), F32)
        B.sb("res_t", (128, 512), F32)
        psum = B["PSBIG"]

        emit_init_consts(S, B)
        emit_weight_load(S, B, wo, B["wo_sb"], NCH, D, "wo_sb")
        S.dma([(B["bo_sb"][:], bo)], writes=("bo_sb",))
        S.dma([(B["ctx_sb"][:, k * 2048:(k + 1) * 2048],
                ctx[128 * k:128 * (k + 1), :]) for k in range(NCH)],
              writes=("ctx_sb",))
        for nt in range(4):
            for m in range(NCH):
                half = ((nt * NCH + m) % 2) * 512
                pk = "PSBIG_a" if (nt * NCH + m) % 2 == 0 else "PSBIG_b"

                def mmo(e, nt=nt, m=m, half=half):
                    last = None
                    for k in range(NCH):
                        last = e.matmul(
                            psum[:, half:half + 512],
                            B["wo_sb"][:, k * D + 128 * m:k * D + 128 * m + 128],
                            B["ctx_sb"][:, k * 2048 + nt * 512:k * 2048 + nt * 512 + 512],
                            start=(k == 0), stop=(k == NCH - 1))
                    return last

                S.comp("tensor", mmo, reads=("wo_sb", "ctx_sb"), writes=(pk,))
                S.comp("scalar", lambda e, nt=nt, m=m, half=half: e.add(
                    out=B["dstbuf"][:, m * 2048 + nt * 512:m * 2048 + nt * 512 + 512],
                    in_=psum[:, half:half + 512], add=B["bo_sb"][:, m:m + 1]),
                    reads=(pk, "bo_sb"), writes=("dstbuf",))
        for g in range(4):
            g0 = g * 512
            emit_tail_group(S, B, lambda k, g0=g0: B["dstbuf"][:, k * 2048 + g0:
                                                              k * 2048 + g0 + 512],
                            ("dstbuf",), 512, x2, out_t, g0)
        S.barrier_wait("sync", ("out",))
        S.finalize()
    return nc


def run_C2(ctx_fm_bf, x2_fm, w_o, b_o):
    nc = build_C2()
    woT = _fmT(w_o)
    bop = _bias_p(b_o)
    in_maps = []
    for c in range(8):
        b, q = c // 4, c % 4
        in_maps.append({
            "ctx": np.ascontiguousarray(ctx_fm_bf[b][:, 2048 * q:2048 * (q + 1)]),
            "x2": np.ascontiguousarray(x2_fm[b][:, 2048 * q:2048 * (q + 1)]),
            "wo": woT, "bo": bop,
        })
    res = run_bass_kernel_spmd(nc, in_maps, core_ids=list(range(8)))
    out = np.empty((2, D, 8192), dtype=np.float32)
    for c in range(8):
        b, q = c // 4, c % 4
        out[b][:, 2048 * q:2048 * (q + 1)] = np.asarray(res.results[c]["out"])
    return out



D_BLK = 6 * 128


def build_D():
    nc = bass.Bass("TRN2", target_bir_lowering=False)
    es = ExitStack()
    HID = 3072
    x3 = nc.dram_tensor("x3", [D, 2048], F32, kind="ExternalInput").ap()
    wg_t = nc.dram_tensor("wg_t", [D, HID], I32, kind="ExternalInput").ap()
    wu_t = nc.dram_tensor("wu_t", [D, HID], I32, kind="ExternalInput").ap()
    wd_t = nc.dram_tensor("wd_t", [HID, D], I32, kind="ExternalInput").ap()
    out_t = nc.dram_tensor("out", [D, 2048], F32, kind="ExternalOutput").ap()

    with es:
        B = Buffers(nc, es)
        S = Sched(nc, es)
        alloc_common(B)
        alloc_ln(B, 512)
        B.sb("y3", (128, NCH * 1024), BF16)
        B.sb("acc", (128, NCH * 1024), F32)
        B.sb("accbf", (128, NCH * 512), BF16)
        B.sb("xg", (128, NCH * 512), F32)
        B.sb("res_t", (128, 512), F32)
        B.sb("istage0", (128, 512), I32)
        B.sb("istage1", (128, 512), I32)
        for p in range(2):
            B.sb(f"wgb{p}", (128, NCH * 768), BF16)
            B.sb(f"wub{p}", (128, NCH * 768), BF16)
            B.sb(f"wdb{p}", (128, 6 * D), BF16)
        B.sb("hbuf", (128, 6 * 1024), BF16)
        B.sb("g1sb", (128, 512), BF16)
        B.sb("g3sb", (128, 512), BF16)
        B.sb("c127", (128, 1), F32)
        psum = B["PSBIG"]

        emit_init_consts(S, B, [("c127", 1.0 / 127.0)])

        for th in range(2):
            t0 = th * 1024
            for g in range(2):
                g0 = t0 + g * 512
                S.dma([(B["xg"][:, k * 512:(k + 1) * 512],
                        x3[128 * k:128 * (k + 1), g0:g0 + 512]) for k in range(NCH)],
                      writes=("xg",))
                S.comp("vector", lambda e: e.tensor_copy(
                    out=B["accbf"][:, 0:NCH * 512], in_=B["xg"][:, 0:NCH * 512]),
                    reads=("xg",), writes=("accbf",))
                emit_ln_stats(S, B, lambda k: B["accbf"][:, k * 512:(k + 1) * 512],
                              ("accbf",), 512)
                emit_ln_apply_bf(S, B, lambda k: B["accbf"][:, k * 512:(k + 1) * 512],
                                 ("accbf",), 512,
                                 lambda k, g=g: B["y3"][:, k * 1024 + g * 512:
                                                        k * 1024 + g * 512 + 512],
                                 "y3")
            for blk in range(4):
                p = blk % 2
                wgk, wuk, wdk = f"wgb{p}", f"wub{p}", f"wdb{p}"
                c0 = blk * 768
                for k in range(NCH):
                    for p0 in (0, 512):
                        pw = 512 if p0 == 0 else 256
                        stg = "istage0" if (k + p0 // 512) % 2 == 0 else "istage1"
                        S.dma([(B[stg][:, 0:pw],
                                wg_t[128 * k:128 * (k + 1), c0 + p0:c0 + p0 + pw])],
                              writes=(stg,))
                        S.comp("gpsimd", lambda e, k=k, p0=p0, pw=pw, stg=stg, wgk=wgk: e.tensor_copy(
                            out=B[wgk][:, k * 768 + p0:k * 768 + p0 + pw],
                            in_=B[stg][:, 0:pw]),
                            reads=(stg,), writes=(wgk,))
                        stg2 = "istage0" if (k + p0 // 512) % 2 == 1 else "istage1"
                        S.dma([(B[stg2][:, 0:pw],
                                wu_t[128 * k:128 * (k + 1), c0 + p0:c0 + p0 + pw])],
                              writes=(stg2,))
                        S.comp("gpsimd", lambda e, k=k, p0=p0, pw=pw, stg2=stg2, wuk=wuk: e.tensor_copy(
                            out=B[wuk][:, k * 768 + p0:k * 768 + p0 + pw],
                            in_=B[stg2][:, 0:pw]),
                            reads=(stg2,), writes=(wuk,))
                for j in range(6):
                    for p0 in (0, 512):
                        stg = "istage0" if (j + p0 // 512) % 2 == 0 else "istage1"
                        S.dma([(B[stg][:, 0:512],
                                wd_t[(blk * 6 + j) * 128:(blk * 6 + j) * 128 + 128,
                                     p0:p0 + 512])],
                              writes=(stg,))
                        S.comp("gpsimd", lambda e, j=j, p0=p0, stg=stg, wdk=wdk: e.tensor_copy(
                            out=B[wdk][:, j * D + p0:j * D + p0 + 512],
                            in_=B[stg][:, 0:512]),
                            reads=(stg,), writes=(wdk,))
                for mc in range(6):
                    for nt in range(2):
                        def mmg(e, mc=mc, nt=nt, wgk=wgk):
                            last = None
                            for k in range(NCH):
                                last = e.matmul(
                                    psum[:, 0:512],
                                    B[wgk][:, k * 768 + 128 * mc:k * 768 + 128 * mc + 128],
                                    B["y3"][:, k * 1024 + nt * 512:k * 1024 + nt * 512 + 512],
                                    start=(k == 0), stop=(k == NCH - 1))
                            return last

                        S.comp("tensor", mmg, reads=(wgk, "y3"), writes=("PSBIG_a",))
                        S.comp("scalar", lambda e: e.activation(
                            out=B["g1sb"][:], in_=psum[:, 0:512], func=AF.Silu,
                            bias=0.0, scale=B["c127"][:]),
                            reads=("PSBIG_a", "c127"), writes=("g1sb",))

                        def mmu(e, mc=mc, nt=nt, wuk=wuk):
                            last = None
                            for k in range(NCH):
                                last = e.matmul(
                                    psum[:, 512:1024],
                                    B[wuk][:, k * 768 + 128 * mc:k * 768 + 128 * mc + 128],
                                    B["y3"][:, k * 1024 + nt * 512:k * 1024 + nt * 512 + 512],
                                    start=(k == 0), stop=(k == NCH - 1))
                            return last

                        S.comp("tensor", mmu, reads=(wuk, "y3"), writes=("PSBIG_b",))
                        S.comp("scalar", lambda e: e.copy(
                            out=B["g3sb"][:], in_=psum[:, 512:1024]),
                            reads=("PSBIG_b",), writes=("g3sb",))
                        S.comp("vector", lambda e, mc=mc, nt=nt: e.tensor_tensor(
                            out=B["hbuf"][:, mc * 1024 + nt * 512:mc * 1024 + nt * 512 + 512],
                            in0=B["g1sb"][:], in1=B["g3sb"][:], op=ALU.mult),
                            reads=("g1sb", "g3sb"), writes=("hbuf",))
                for nt in range(2):
                    for m in range(NCH):
                        half = ((nt * NCH + m) % 2) * 512
                        pk = "PSBIG_a" if (nt * NCH + m) % 2 == 0 else "PSBIG_b"

                        def mmd(e, nt=nt, m=m, half=half, wdk=wdk):
                            last = None
                            for j in range(6):
                                last = e.matmul(
                                    psum[:, half:half + 512],
                                    B[wdk][:, j * D + 128 * m:j * D + 128 * m + 128],
                                    B["hbuf"][:, j * 1024 + nt * 512:j * 1024 + nt * 512 + 512],
                                    start=(j == 0), stop=(j == 5))
                            return last

                        S.comp("tensor", mmd, reads=(wdk, "hbuf"), writes=(pk,))
                        if blk == 0:
                            S.comp("vector", lambda e, nt=nt, m=m, half=half: e.tensor_scalar(
                                out=B["acc"][:, m * 1024 + nt * 512:m * 1024 + nt * 512 + 512],
                                in0=psum[:, half:half + 512],
                                scalar1=1.0 / (127.0 * 127.0), scalar2=None, op0=ALU.mult),
                                reads=(pk,), writes=("acc",))
                        else:
                            S.comp("vector", lambda e, nt=nt, m=m, half=half: e.scalar_tensor_tensor(
                                out=B["acc"][:, m * 1024 + nt * 512:m * 1024 + nt * 512 + 512],
                                in0=psum[:, half:half + 512], scalar=1.0 / (127.0 * 127.0),
                                in1=B["acc"][:, m * 1024 + nt * 512:m * 1024 + nt * 512 + 512],
                                op0=ALU.mult, op1=ALU.add),
                                reads=(pk, "acc"), writes=("acc",))
            for g in range(2):
                g0 = t0 + g * 512
                for k in range(NCH):
                    S.comp("vector", lambda e, k=k, g=g: e.tensor_copy(
                        out=B["accbf"][:, k * 512:(k + 1) * 512],
                        in_=B["acc"][:, k * 1024 + g * 512:k * 1024 + g * 512 + 512]),
                        reads=("acc",), writes=("accbf",))
                emit_tail_group(S, B, lambda k: B["accbf"][:, k * 512:(k + 1) * 512],
                                ("accbf",), 512, x3, out_t, g0)

        S.barrier_wait("sync", ("out",))
        S.finalize()
    return nc


def run_D(x3_fm, w_gate, w_up, w_down):
    nc = build_D()
    wg_t = np.ascontiguousarray(np.asarray(w_gate).T)
    wu_t = np.ascontiguousarray(np.asarray(w_up).T)
    wd_t = np.ascontiguousarray(np.asarray(w_down).T)
    in_maps = []
    for c in range(8):
        b, q = c // 4, c % 4
        in_maps.append({
            "x3": np.ascontiguousarray(x3_fm[b][:, 2048 * q:2048 * (q + 1)]),
            "wg_t": wg_t, "wu_t": wu_t, "wd_t": wd_t,
        })
    res = run_bass_kernel_spmd(nc, in_maps, core_ids=list(range(8)))
    out = np.empty((2, D, 8192), dtype=np.float32)
    for c in range(8):
        b, q = c // 4, c % 4
        out[b][:, 2048 * q:2048 * (q + 1)] = np.asarray(res.results[c]["out"])
    return out



def kernel(**inputs):
    g = lambda n: inputs[n]
    x = np.asarray(g("L0_attn_x"), np.float32)
    x_fm = np.ascontiguousarray(x.transpose(0, 2, 1))

    l0 = run_A(0, x_fm, None, g("L0_attn_Win0"), g("L0_attn_bin0"),
               g("L0_attn_Wout0"), g("L0_attn_bout0"))
    l1 = run_A(1, x_fm, l0, g("L0_attn_Win1"), g("L0_attn_bin1"),
               g("L0_attn_Wout1"), g("L0_attn_bout1"))
    x1 = run_A(2, x_fm, l1, g("L0_attn_Win2"), g("L0_attn_bin2"),
               g("L0_attn_Wout2"), g("L0_attn_bout2"))
    x2 = run_B(x1, g("L0_ffn_gate_w"), g("L0_ffn_gate_b"), g("L0_ffn_w1"),
               g("L0_ffn_b1"), g("L0_ffn_w3"), g("L0_ffn_b3"), g("L0_ffn_w2"),
               g("L0_ffn_b2"), g("L0_ffn_sw1"), g("L0_ffn_sb1"), g("L0_ffn_sw3"),
               g("L0_ffn_sb3"), g("L0_ffn_sw2"), g("L0_ffn_sb2"))
    ctx = run_C1(x2, g("L1_attn_w_q"), g("L1_attn_w_k"), g("L1_attn_w_v"))
    x3 = run_C2(ctx, x2, g("L1_attn_w_o"), g("L1_attn_b_o"))
    y = run_D(x3, g("L1_ffn_w_gate"), g("L1_ffn_w_up"), g("L1_ffn_w_down"))
    return np.ascontiguousarray(y.transpose(0, 2, 1))
